# version 06
# baseline (speedup 1.0000x reference)
import math
import numpy as np
from contextlib import ExitStack

import concourse.bass as bass
import concourse.mybir as mybir
from concourse.bass_utils import run_bass_kernel_spmd

F32 = mybir.dt.float32
BF16 = mybir.dt.bfloat16
I8 = mybir.dt.int8
I32 = mybir.dt.int32
AF = mybir.ActivationFunctionType
ALU = mybir.AluOpType
AX = mybir.AxisListType

PI = float(np.pi)
CW1 = 6.28125
CW2 = 2 * PI - CW1
E = 2048
H = 16
D = 128
NSEQ = 4096
INTER = 8192
TGABS = {0: [0, 512, 3072, 3584], 1: [1024, 1536, 2048, 2560]}
SLOT_LNT = [3, 2, 1, 0]
OWN_GROUPS = {0: [0, 1, 6, 7], 1: [2, 3, 4, 5]}
COL_EPS, COL_NHALF, COL_ONE, COL_NEG1, COL_LN1E4, COL_HALFC, COL_I2PI = range(7)


class Sem:
    def __init__(self, h):
        self.h = h
        self.n = 0

    def inc(self, ins, k=1):
        ins.then_inc(self.h, k)
        self.n += k
        return self.n


def pair_list(half):
    pairs = []
    for si, tg in enumerate(SLOT_LNT):
        a0 = TGABS[half][tg]
        mcN = (a0 + 512) // 128
        for mc in range(mcN):
            dv = mc - a0 // 128
            pairs.append(dict(si=si, tg=tg, mc=mc,
                              dv=dv if 0 <= dv <= 3 else None,
                              first=(mc == 0), last=(mc == mcN - 1)))
    return pairs


class KB:
    def __init__(self):
        nc = bass.Bass("TRN2", target_bir_lowering=False, debug=False)
        self.nc = nc
        self.TEN, self.VEC, self.SCA, self.GPS, self.SYN = \
            nc.tensor, nc.vector, nc.scalar, nc.gpsimd, nc.sync
        dt = nc.dram_tensor
        self.xb = dt("xb", [NSEQ, E], F32, kind="ExternalInput")
        self.x_own = dt("x_own", [2048, E], F32, kind="ExternalInput")
        self.pos_own = dt("pos_own", [1, 2048], F32, kind="ExternalInput")
        self.cfg = dt("cfg", [1, 4], I32, kind="ExternalInput")
        self.wq = dt("wq", [E, E], F32, kind="ExternalInput")
        self.wkv = dt("wkv", [128, 16 * 256], F32, kind="ExternalInput")
        self.wsm = dt("wsm", [128, 16 * 64], F32, kind="ExternalInput")
        self.wout = dt("wout", [E, E], F32, kind="ExternalInput")
        self.wgt = dt("wgt", [INTER, E], I8, kind="ExternalInput")
        self.wut = dt("wut", [INTER, E], I8, kind="ExternalInput")
        self.wdn = dt("wdn", [INTER, E], I8, kind="ExternalInput")
        self.sgv = dt("sgv", [128, 64], F32, kind="ExternalInput")
        self.suv = dt("suv", [128, 64], F32, kind="ExternalInput")
        self.sdv = dt("sdv", [1, E], F32, kind="ExternalInput")
        self.out = dt("out", [2048, E], F32, kind="ExternalOutput")
        self.q_scr = dt("q_scr", [128, H * 2048], BF16, kind="Internal")
        self.fp_scr = dt("fp_scr", [2048, E], F32, kind="Internal")
        self.kt_scr = dt("kt_scr", [128, 4096], BF16, kind="Internal")
        self.vp_scr = dt("vp_scr", [128, 4096], BF16, kind="Internal")
        self.kn_scr = dt("kn_scr", [128, 1024], F32, kind="Internal")

        self.es = ExitStack()
        self.sems = {k: Sem(self.es.enter_context(nc.semaphore(k))) for k in
                     ["sXL", "sWL", "sQS", "sQL", "sFS", "sFL", "sOUT", "sG2",
                      "sPE", "sSC", "sVE", "sVE2", "sGP", "sMS", "sSE", "sSEs"]}
        self.sSE = self.sems["sSE"]
        def mksems(prefix, k):
            return [Sem(self.es.enter_context(nc.semaphore(f"{prefix}{i}")))
                    for i in range(k)]
        self.sXLs = mksems("sXLs", 3)
        self.sWQ = mksems("sWQ", 2)
        self.sQSs = mksems("sQSs", 2)
        self.sWG = mksems("sWG", 2)
        self.sWD = mksems("sWD", 2)
        self.sXO = mksems("sXO", 2)
        self.sFB = mksems("sFB", 4)
        self.sQC = mksems("sQC", 2)
        self.sFP = mksems("sFP", 3)
        self.sOU = mksems("sOU", 2)
        self.sSEg = Sem(self.es.enter_context(nc.semaphore("sSEg")))
        for k, v in self.sems.items():
            setattr(self, k, v)

    def sb(self, st, name, shape, dtt):
        return st.enter_context(self.nc.sbuf_tensor(name, shape, dtt))

    def ps(self, st, name, shape, dtt):
        return st.enter_context(self.nc.psum_tensor(name, shape, dtt))

    def se(self, eng, ins):
        s = (self.sSEg if eng is self.GPS
             else self.sems["sSEs"] if eng is self.SCA else self.sSE)
        n = s.inc(ins)
        eng.wait_ge(s.h, n)

    def snap(self):
        return {k: s.n for k, s in self.sems.items()}

    def restore(self, st):
        for k, s in self.sems.items():
            s.n = st[k]

    def p0(self):
        nc, TEN, VEC, SCA, GPS, SYN = self.nc, self.TEN, self.VEC, self.SCA, self.GPS, self.SYN
        es = self.es
        self.c_ = self.sb(es, "c_", [128, 8], F32)
        self.onesb_c = self.sb(es, "onesb_c", [128, 1], BF16)
        self.onesb_r = self.sb(es, "onesb_r", [1, 128], BF16)
        self.onesf_r = self.sb(es, "onesf_r", [1, 128], F32)
        self.identb = self.sb(es, "identb", [128, 128], BF16)
        self.identf = self.sb(es, "identf", [128, 128], F32)
        self.cfgs = self.sb(es, "cfgs", [1, 4], I32)
        self.maxqrow = self.sb(es, "maxqrow", [1, 16], F32)
        self.mkrow = self.sb(es, "mkrow", [1, 32], F32)
        self.knm = self.sb(es, "knm", [1, 4], F32)
        self.b11 = self.sb(es, "b11", [1, 8], F32)
        c_ = self.c_

        ins = SYN.dma_start(out=self.cfgs[:, :], in_=self.cfg[:, :])
        self.sMS.inc(ins, 16)
        for col, val in [(COL_EPS, 1e-5), (COL_NHALF, -0.5), (COL_ONE, 1.0),
                         (COL_NEG1, -1.0), (COL_LN1E4, -math.log(1e4) / 64.0),
                         (COL_HALFC, 0.5), (COL_I2PI, 1.0 / (2 * PI))]:
            self.sGP.inc(GPS.memset(c_[:, col:col + 1], val))
        self.sGP.inc(GPS.memset(self.onesb_c[:, :], 1.0))
        self.sGP.inc(GPS.memset(self.onesb_r[:, :], 1.0))
        self.sGP.inc(GPS.memset(self.onesf_r[:, :], 1.0))
        self.sGP.inc(GPS.memset(self.maxqrow[:, :], 0.0))
        self.sGP.inc(GPS.memset(self.knm[:, :], 0.0))
        ins = GPS.memset(self.identf[:, :], 1.0)
        self.se(GPS, ins)
        ins = GPS.affine_select(self.identf[:, :], self.identf[:, :], [[1, 128]],
                                ALU.is_equal, 0.0, base=0, channel_multiplier=-1)
        self.sGP.inc(ins)
        ins = GPS.memset(self.identb[:, :], 1.0)
        self.se(GPS, ins)
        ins = GPS.affine_select(self.identb[:, :], self.identb[:, :], [[1, 128]],
                                ALU.is_equal, 0.0, base=0, channel_multiplier=-1)
        self.sGP.inc(ins)

        self.regs = {}
        for nm, e in [("t", TEN), ("v", VEC), ("s", SCA), ("g", GPS)]:
            e.wait_ge(self.sMS.h, 16)
            r = e.alloc_register("rhalf_" + nm)
            e.reg_load(r, self.cfgs[0:1, 0:1])
            self.regs[nm] = r

        self.esA = ExitStack()
        self.xnT = self.sb(self.esA, "xnT", [128, 16 * 2048], BF16)
        self.esT = ExitStack()
        self.tabk = self.sb(self.esT, "tabk", [128, 4096], F32)
        self.tabq = self.sb(self.esT, "tabq", [128, 2048], F32)
        self.esG = ExitStack()
        self.ktT = self.sb(self.esG, "ktT", [128, 4096], BF16)
        self.vpn = self.sb(self.esG, "vpn", [128, 4096], BF16)
        self.kvhn = self.sb(self.esG, "kvhn", [128, 1024], F32)

        with ExitStack() as esP:
            ang = self.sb(esP, "p0_ang", [64, 4096], F32)
            angq = self.sb(esP, "p0_angq", [64, 2048], F32)
            tb1 = self.sb(esP, "p0_tb1", [64, 4096], F32)
            tb2 = self.sb(esP, "p0_tb2", [64, 4096], F32)
            tbi = self.sb(esP, "p0_tbi", [64, 4096], I32)
            invj = self.sb(esP, "p0_inv", [64, 1], F32)
            jcol = self.sb(esP, "p0_jcol", [64, 1], F32)
            posq = self.sb(esP, "p0_posq", [1, 2048], F32)
            pp0 = self.ps(esP, "pp0", [64, 512], F32)

            SYN.wait_ge(self.sMS.h, self.sMS.n)
            ins = SYN.dma_start(out=posq[:, :], in_=self.pos_own[:, :])
            self.sMS.inc(ins, 16)
            TEN.wait_ge(self.sGP.h, self.sGP.n)
            VEC.wait_ge(self.sGP.h, self.sGP.n)
            self.sGP.inc(GPS.iota(jcol[:, :], [[1, 1]], base=0, channel_multiplier=1,
                                   allow_small_or_imprecise_dtypes=True))
            self.sGP.inc(GPS.iota(ang[:, :], [[1, 4096]], base=0, channel_multiplier=0,
                                  allow_small_or_imprecise_dtypes=True))
            self.sGP.inc(GPS.memset(self.ktT[:, :], 0.0))
            self.sGP.inc(GPS.memset(self.vpn[:, :], 0.0))
            self.sGP.inc(GPS.memset(self.kvhn[:, :], 0.0))

            SCA.wait_ge(self.sGP.h, self.sGP.n)
            TEN.wait_ge(self.sGP.h, self.sGP.n)
            VEC.wait_ge(self.sGP.h, self.sGP.n)
            ins = SCA.activation(invj[:, :], jcol[:, :], AF.Exp,
                                 scale=c_[0:64, COL_LN1E4:COL_LN1E4 + 1])
            self.sSC.inc(ins)
            VEC.wait_ge(self.sSC.h, self.sSC.n)
            ins = VEC.tensor_scalar_mul(ang[:, :], ang[:, :], invj[:, :])
            self.sVE.inc(ins)
            VEC.wait_ge(self.sVE.h, self.sVE.n)
            for i in range(4):
                TEN.wait_ge(self.sMS.h, self.sMS.n)
                if i > 0:
                    TEN.wait_ge(self.sVE.h, self.sVE.n)
                ins = TEN.matmul(pp0[:, :], self.onesf_r[0:1, 0:64],
                                 posq[0:1, 512 * i:512 * (i + 1)], start=True, stop=True)
                self.sPE.inc(ins)
                VEC.wait_ge(self.sPE.h, self.sPE.n)
                ins = VEC.tensor_copy(angq[:, 512 * i:512 * (i + 1)], pp0[:, :])
                self.sVE.inc(ins)
                VEC.wait_ge(self.sVE.h, self.sVE.n)
            ins = VEC.tensor_scalar_mul(angq[:, :], angq[:, :], invj[:, :])
            self.se(VEC, ins)

            def sin_into(dst, src, n, add_half_pi, post_scale=None):
                a = tb1[:, 0:n]
                if add_half_pi:
                    ins = VEC.tensor_scalar(a, src[:, 0:n], PI / 2, None, ALU.add)
                else:
                    ins = VEC.tensor_copy(a, src[:, 0:n])
                self.se(VEC, ins)
                t_ = tb2[:, 0:n]
                ins = VEC.tensor_scalar_mul(t_, a, c_[0:64, COL_I2PI:COL_I2PI + 1])
                self.se(VEC, ins)
                ins = VEC.tensor_copy(tbi[:, 0:n], t_)
                self.se(VEC, ins)
                ins = VEC.tensor_copy(t_, tbi[:, 0:n])
                self.se(VEC, ins)
                ins = VEC.scalar_tensor_tensor(a, t_, -CW1, a, ALU.mult, ALU.add)
                self.se(VEC, ins)
                ins = VEC.scalar_tensor_tensor(a, t_, -CW2, a, ALU.mult, ALU.add)
                self.se(VEC, ins)
                ins = VEC.tensor_scalar(t_, a, PI, None, ALU.is_gt)
                self.se(VEC, ins)
                ins = VEC.scalar_tensor_tensor(a, t_, -2 * PI, a, ALU.mult, ALU.add)
                self.se(VEC, ins)
                ins = VEC.tensor_scalar(t_, a, -PI, None, ALU.is_lt)
                self.se(VEC, ins)
                ins = VEC.scalar_tensor_tensor(a, t_, 2 * PI, a, ALU.mult, ALU.add)
                self.sVE.inc(ins)
                SCA.wait_ge(self.sVE.h, self.sVE.n)
                ins = SCA.activation(dst, a, AF.Sin)
                self.sSC.inc(ins)
                VEC.wait_ge(self.sSC.h, self.sSC.n)
                if post_scale is not None:
                    ins = VEC.tensor_scalar(dst, dst, post_scale, None, ALU.mult)
                    self.sVE.inc(ins)

            s = 1.0 / math.sqrt(D)
            sin_into(self.tabk[64:128, 0:4096], ang, 4096, False, s)
            sin_into(self.tabk[0:64, 0:4096], ang, 4096, True, s)
            sin_into(self.tabq[64:128, 0:2048], angq, 2048, False)
            sin_into(self.tabq[0:64, 0:2048], angq, 2048, True)
            ins = VEC.tensor_copy(tb1[0:1, 0:1], tb1[0:1, 0:1])
            self.sVE.inc(ins)
            for eng in (TEN, SCA, GPS, SYN, VEC):
                eng.wait_ge(self.sVE.h, self.sVE.n)
                eng.wait_ge(self.sGP.h, self.sGP.n)
                eng.wait_ge(self.sSC.h, self.sSC.n)

    def p1(self):
        nc, TEN, VEC, SCA, GPS, SYN = self.nc, self.TEN, self.VEC, self.SCA, self.GPS, self.SYN
        c_ = self.c_
        esP1o = ExitStack()
        kvhgT = self.sb(esP1o, "p1_kvhg", [64, 4096], F32)
        with ExitStack() as esP:
            xt = [self.sb(esP, f"p1_xt{i}", [128, E], F32) for i in range(3)]
            xnb = [self.sb(esP, f"p1_xnb{i}", [128, E], BF16) for i in range(2)]
            sqd = [self.sb(esP, "p1_sqd0", [128, E], F32)] * 2
            xnt_tmp = self.sb(esP, "p1_xnt_tmp", [128, 16 * 512], BF16)
            wkv_sb = self.sb(esP, "p1_wkv", [128, 16 * 256], BF16)
            wsm_sb = self.sb(esP, "p1_wsm", [128, 16 * 64], BF16)
            tmpv = self.sb(esP, "p1_tmpv", [128, 512], BF16)
            rp1 = self.sb(esP, "p1_rp1", [64, 512], F32)
            rp2 = self.sb(esP, "p1_rp2", [64, 512], F32)
            rp3 = self.sb(esP, "p1_rp3", [64, 512], F32)
            rp4 = self.sb(esP, "p1_rp4", [64, 512], F32)
            st1 = [self.sb(esP, f"p1_s1{i}", [128, 1], F32) for i in range(2)]
            st2 = [self.sb(esP, f"p1_s2{i}", [128, 1], F32) for i in range(2)]
            stm = [self.sb(esP, f"p1_sm{i}", [128, 1], F32) for i in range(2)]
            str_ = [self.sb(esP, f"p1_sr{i}", [128, 1], F32) for i in range(2)]
            pxp = [self.ps(esP, f"p1_pxp{i}", [128, 512], BF16) for i in range(2)]
            pkp = self.ps(esP, "p1_pkp", [128, 512], F32)
            pvp = self.ps(esP, "p1_pvp", [128, 512], F32)
            pkg = self.ps(esP, "p1_pkg", [64, 512], F32)
            pvt = self.ps(esP, "p1_pvt", [128, 512], BF16)

            ins = GPS.dma_start(out=wkv_sb[:, :], in_=self.wkv[:, :])
            self.sWL.inc(ins, 16)
            GPS.wait_ge(self.sWL.h, self.sWL.n)
            ins = GPS.dma_start(out=wsm_sb[:, :], in_=self.wsm[:, :])
            wl_mark = self.sWL.inc(ins, 16)
            TEN.wait_ge(self.sWL.h, wl_mark)

            m_xn = {}
            m_sq = {}
            m_ev = {}
            m_tb = {}
            m_var = {}
            m_kp = {}
            m_vp = {}
            m_kg = {}
            m_vt = {}
            m_grp = {}
            GB = 0
            for g in range(8):
                for tt in range(4):
                    i = 4 * g + tt
                    if i >= 3:
                        SYN.wait_ge(self.sVE.h, m_xn[i - 3])
                        SYN.wait_ge(self.sSC.h, m_sq[i - 3])
                    ins = SYN.dma_start(
                        out=xt[i % 3][:, :],
                        in_=self.xb[512 * g + 128 * tt: 512 * g + 128 * (tt + 1), :])
                    xl = self.sXLs[i % 3].inc(ins, 16)
                    SCA.wait_ge(self.sXLs[i % 3].h, xl)
                    if i >= 1:
                        SCA.wait_ge(self.sVE2.h, m_var[i - 1])
                    ins = SCA.activation(sqd[i % 2][:, :], xt[i % 3][:, :], AF.Square)
                    m_sq[i] = self.sSC.inc(ins)
                    VEC.wait_ge(self.sXLs[i % 3].h, xl)
                    ins = VEC.reduce_sum(st1[i % 2][:, :], xt[i % 3][:, :], AX.X)
                    self.se(VEC, ins)
                    VEC.wait_ge(self.sSC.h, m_sq[i])
                    ins = VEC.reduce_sum(st2[i % 2][:, :], sqd[i % 2][:, :], AX.X)
                    self.se(VEC, ins)
                    ins = VEC.tensor_scalar(stm[i % 2][:, :], st1[i % 2][:, :], 1.0 / E,
                                             None, ALU.mult)
                    self.se(VEC, ins)
                    ins = VEC.tensor_scalar(st2[i % 2][:, :], st2[i % 2][:, :], 1.0 / E,
                                            None, ALU.mult)
                    self.se(VEC, ins)
                    ins = VEC.tensor_tensor(st1[i % 2][:, :], stm[i % 2][:, :],
                                            stm[i % 2][:, :], ALU.mult)
                    self.se(VEC, ins)
                    ins = VEC.tensor_tensor(st2[i % 2][:, :], st2[i % 2][:, :],
                                            st1[i % 2][:, :], ALU.subtract)
                    m_var[i] = self.sVE2.inc(ins)
                    SCA.wait_ge(self.sVE2.h, m_var[i])
                    ins = SCA.activation(st1[i % 2][:, :], st2[i % 2][:, :], AF.Ln,
                                         bias=c_[:, COL_EPS:COL_EPS + 1])
                    self.se(SCA, ins)
                    ins = SCA.activation(str_[i % 2][:, :], st1[i % 2][:, :], AF.Exp,
                                         scale=c_[:, COL_NHALF:COL_NHALF + 1])
                    rstd_m = self.sSC.inc(ins)
                    VEC.wait_ge(self.sSC.h, rstd_m)
                    ins = VEC.tensor_scalar(xnb[i % 2][:, :], xt[i % 3][:, :],
                                            stm[i % 2][:, :], str_[i % 2][:, :],
                                            ALU.subtract, ALU.mult)
                    m_xn[i] = self.sVE.inc(ins)
                    TEN.wait_ge(self.sVE.h, m_xn[i])
                    for bch in range(4):
                        if GB >= 2:
                            TEN.wait_ge(self.sVE2.h, m_ev[GB - 2])
                        last = None
                        for k in range(4):
                            ec = 4 * bch + k
                            last = TEN.transpose(
                                pxp[GB % 2][:, 128 * k:128 * (k + 1)],
                                xnb[i % 2][:, 128 * ec:128 * (ec + 1)],
                                self.identb[:, :])
                        m_tb[GB] = self.sPE.inc(last)
                        VEC.wait_ge(self.sPE.h, m_tb[GB])
                        if g > 0:
                            VEC.wait_ge(self.sPE.h, m_kg[g - 1])
                        dst = xnt_tmp[:, :].rearrange("p (c r) -> p c r", c=16)[
                            :, 4 * bch:4 * bch + 4, 128 * tt:128 * (tt + 1)]
                        src = pxp[GB % 2][:, :].rearrange("p (c r) -> p c r", c=4)
                        ins = VEC.tensor_copy(dst, src)
                        m_ev[GB] = self.sVE2.inc(ins)
                        GB += 1
                TEN.wait_ge(self.sVE2.h, m_ev[GB - 1])
                if g > 0:
                    TEN.wait_ge(self.sG2.h, m_grp[g - 1])
                last = None
                for cc in range(16):
                    last = TEN.matmul(pkp[:, :], wkv_sb[:, 256 * cc:256 * cc + 128],
                                      xnt_tmp[:, 512 * cc:512 * (cc + 1)],
                                      start=(cc == 0), stop=(cc == 15))
                m_kp[g] = self.sPE.inc(last)
                for cc in range(16):
                    last = TEN.matmul(pvp[:, :], wkv_sb[:, 256 * cc + 128:256 * (cc + 1)],
                                      xnt_tmp[:, 512 * cc:512 * (cc + 1)],
                                      start=(cc == 0), stop=(cc == 15))
                m_vp[g] = self.sPE.inc(last)
                for cc in range(16):
                    last = TEN.matmul(pkg[:, :], wsm_sb[:, 64 * cc:64 * (cc + 1)],
                                      xnt_tmp[:, 512 * cc:512 * (cc + 1)],
                                      start=(cc == 0), stop=(cc == 15))
                m_kg[g] = self.sPE.inc(last)
                if g > 0:
                    TEN.wait_ge(self.sG2.h, m_grp[g - 1])
                    last = None
                    for k in range(4):
                        last = TEN.transpose(pvt[:, 128 * k:128 * (k + 1)],
                                             tmpv[:, 128 * k:128 * (k + 1)],
                                             self.identb[:, :])
                    m_vt[g - 1] = self.sPE.inc(last)
                cS = self.tabk[0:64, 512 * g:512 * (g + 1)]
                sS = self.tabk[64:128, 512 * g:512 * (g + 1)]
                VEC.wait_ge(self.sPE.h, m_kp[g])
                self.sSE.inc(VEC.tensor_tensor(rp1[:, :], pkp[0:64, :], cS, ALU.mult))
                self.sSE.inc(VEC.tensor_tensor(rp2[:, :], pkp[64:128, :], sS, ALU.mult))
                self.sSE.inc(VEC.tensor_tensor(rp3[:, :], pkp[64:128, :], cS, ALU.mult))
                self.sSE.inc(VEC.tensor_tensor(rp4[:, :], pkp[0:64, :], sS, ALU.mult))
                VEC.wait_ge(self.sSE.h, self.sSE.n)
                self.sSE.inc(VEC.tensor_tensor(self.ktT[0:64, 512 * g:512 * (g + 1)],
                                               rp1[:, :], rp2[:, :], ALU.subtract))
                self.sSE.inc(VEC.tensor_tensor(self.ktT[64:128, 512 * g:512 * (g + 1)],
                                               rp3[:, :], rp4[:, :], ALU.add))
                VEC.wait_ge(self.sSE.h, self.sSE.n)
                VEC.wait_ge(self.sPE.h, m_vp[g])
                if g > 0:
                    VEC.wait_ge(self.sPE.h, m_vt[g - 1])
                self.sG2.inc(VEC.tensor_copy(tmpv[:, :], pvp[:, :]))
                VEC.wait_ge(self.sPE.h, m_kg[g])
                self.sG2.inc(VEC.tensor_copy(kvhgT[:, 512 * g:512 * (g + 1)], pkg[:, :]))
                if g > 0:
                    self.sG2.inc(VEC.tensor_copy(self.vpn[:, 512 * (g - 1):512 * g],
                                                 pvt[:, :]))
                stH = self.snap()
                endsH = []
                for half in (0, 1):
                    self.restore(stH)
                    ctx = (VEC.If_eq(self.regs["v"], 0) if half == 0 else VEC.Else())
                    with ctx:
                        if g in OWN_GROUPS[half]:
                            loc = OWN_GROUPS[half].index(g) * 512
                            dst = self.xnT[:, :].rearrange("p (c r) -> p c r", c=16)[
                                :, :, loc:loc + 512]
                            src = xnt_tmp[:, :].rearrange("p (c r) -> p c r", c=16)
                            ins = VEC.tensor_copy(dst, src)
                        else:
                            ins = VEC.tensor_copy(rp1[0:1, 0:1], rp1[0:1, 0:1])
                        m_grp[g] = self.sG2.inc(ins)
                    endsH.append(self.snap())
                assert endsH[0] == endsH[1]
            TEN.wait_ge(self.sG2.h, m_grp[7])
            last = None
            for k in range(4):
                last = TEN.transpose(pvt[:, 128 * k:128 * (k + 1)],
                                     tmpv[:, 128 * k:128 * (k + 1)], self.identb[:, :])
            m_vt[7] = self.sPE.inc(last)
            VEC.wait_ge(self.sPE.h, m_vt[7])
            ins = VEC.tensor_copy(self.vpn[:, 512 * 7:512 * 8], pvt[:, :])
            self.sG2.inc(ins)

            for eng in (self.TEN, self.SCA, self.GPS, self.SYN):
                eng.wait_ge(self.sG2.h, self.sG2.n)
        with ExitStack() as esP:
            bufA = self.sb(esP, "p1_bA", [32, 4096], F32)
            bufB = self.sb(esP, "p1_bB", [32, 4096], F32)
            bufC = self.sb(esP, "p1_bC", [32, 4096], F32)
            self.S2 = self.sb(esP, "p1_S2", [32, 4096], F32)
            mkcol = self.sb(esP, "p1_mkc", [32, 1], F32)
            sqb = self.sb(esP, "p1_sqb", [128, 512], BF16)
            pkn = self.ps(esP, "p1_pkn", [1, 512], F32)
            pmt = self.ps(esP, "p1_pmt", [128, 128], F32)

            gT = kvhgT[32:64, :]
            kvT = kvhgT[0:32, :]
            SCA.wait_ge(self.sG2.h, self.sG2.n)
            ins = SCA.activation(bufA[:, :], gT, AF.Abs)
            self.se(SCA, ins)
            ins = SCA.activation(bufB[:, :], bufA[:, :], AF.Exp,
                                 scale=c_[0:32, COL_NEG1:COL_NEG1 + 1])
            self.se(SCA, ins)
            ins = SCA.activation(bufA[:, :], bufB[:, :], AF.Ln,
                                 bias=c_[0:32, COL_ONE:COL_ONE + 1])
            ls_m = self.sSC.inc(ins)
            VEC.wait_ge(self.sSC.h, ls_m)
            ins = VEC.tensor_scalar_min(bufC[:, :], gT, 0.0)
            self.se(VEC, ins)
            ins = VEC.tensor_tensor(bufB[:, :], bufC[:, :], bufA[:, :], ALU.subtract)
            self.se(VEC, ins)
            ins = VEC.tensor_tensor_scan(bufC[:, :], bufB[:, :], bufB[:, :], 0.0,
                                         ALU.add, ALU.bypass)
            self.se(VEC, ins)
            ins = VEC.tensor_tensor(bufA[:, :], bufC[:, :], bufB[:, :], ALU.subtract)
            lc_m = self.sVE.inc(ins)
            SCA.wait_ge(self.sVE.h, lc_m)
            ins = SCA.activation(bufC[:, :], bufA[:, :], AF.Exp)
            e_m = self.sSC.inc(ins)
            VEC.wait_ge(self.sSC.h, e_m)
            ins = VEC.tensor_tensor(bufB[:, :], kvT, bufC[:, :], ALU.mult)
            self.se(VEC, ins)
            ins = VEC.tensor_tensor_scan(self.S2[:, :], bufB[:, :], bufB[:, :], 0.0,
                                         ALU.add, ALU.bypass)
            self.se(VEC, ins)
            ins = VEC.reduce_max(mkcol[:, :], self.S2[:, :], AX.X,
                                 apply_absolute_value=True)
            s2_m = self.sVE.inc(ins)

            TEN.wait_ge(self.sVE.h, s2_m)
            for b4 in range(8):
                if b4 >= 1:
                    TEN.wait_ge(self.sVE2.h, self.sVE2.n)
                last = None
                for k in range(4):
                    mc = 4 * b4 + k
                    last = TEN.transpose(pmt[:, 32 * k:32 * (k + 1)],
                                         self.S2[:, 128 * mc:128 * (mc + 1)],
                                         self.identf[0:32, 0:32])
                pe_m = self.sPE.inc(last)
                VEC.wait_ge(self.sPE.h, pe_m)
                ins = VEC.tensor_copy(self.kvhn[:, 128 * b4:128 * (b4 + 1)], pmt[:, :])
                self.sVE2.inc(ins)
            TEN.wait_ge(self.sVE2.h, self.sVE2.n)
            last = TEN.transpose(pmt[0:1, 0:32],
                                 mkcol[:, :], self.identf[0:32, 0:32])
            pe_m = self.sPE.inc(last)
            VEC.wait_ge(self.sPE.h, pe_m)
            ins = VEC.tensor_copy(self.mkrow[:, :], pmt[0:1, 0:32])
            self.sVE2.inc(ins)
            for ch in range(8):
                SCA.wait_ge(self.sVE2.h, self.sVE2.n)
                SCA.wait_ge(self.sG2.h, self.sG2.n)
                ins = SCA.activation(sqb[:, :], self.ktT[:, 512 * ch:512 * (ch + 1)],
                                     AF.Square)
                sc_m = self.sSC.inc(ins)
                TEN.wait_ge(self.sSC.h, sc_m)
                if ch >= 1:
                    TEN.wait_ge(self.sVE2.h, self.sVE2.n)
                ins = TEN.matmul(pkn[:, :], self.onesb_c[:, :], sqb[:, :],
                                 start=True, stop=True)
                pe_m = self.sPE.inc(ins)
                VEC.wait_ge(self.sPE.h, pe_m)
                ins = VEC.reduce_max(self.knm[0:1, 1:2], pkn[:, :], AX.X)
                self.se(VEC, ins)
                ins = VEC.tensor_tensor(self.knm[0:1, 0:1], self.knm[0:1, 0:1],
                                        self.knm[0:1, 1:2], ALU.max)
                n = self.sVE2.inc(ins)
                VEC.wait_ge(self.sVE2.h, n)
            for eng in (TEN, SCA, GPS, SYN):
                eng.wait_ge(self.sVE2.h, self.sVE2.n)
                eng.wait_ge(self.sVE.h, self.sVE.n)
        esP1o.close()

    def p2(self):
        nc, TEN, VEC, SCA, GPS, SYN = self.nc, self.TEN, self.VEC, self.SCA, self.GPS, self.SYN
        with ExitStack() as esP:
            wqb = [self.sb(esP, f"p2_wq{i}", [128, E], BF16) for i in range(2)]
            qro = [self.sb(esP, f"p2_qr{i}", [128, 512], BF16) for i in range(2)]
            sqb = [self.sb(esP, f"p2_sq{i}", [128, 512], BF16) for i in range(2)]
            rp1 = self.sb(esP, "p2_rp1", [64, 512], F32)
            rp2 = self.sb(esP, "p2_rp2", [64, 512], F32)
            rp3 = self.sb(esP, "p2_rp3", [64, 512], F32)
            rp4 = self.sb(esP, "p2_rp4", [64, 512], F32)
            mx1 = self.sb(esP, "p2_mx1", [1, 4], F32)
            pq = [self.ps(esP, f"p2_pq{i}", [128, 512], F32) for i in range(2)]
            pq2 = self.ps(esP, "p2_pq2", [1, 512], F32)

            m_wl = {}
            m_mm = {}
            m_rope = {}
            m_sq = {}
            m_q2 = {}
            m_qs = {}
            m_mx = {}
            for och in range(16):
                if och >= 2:
                    GPS.wait_ge(self.sPE.h, m_mm[(och - 2, 3)])
                ins = GPS.dma_start(out=wqb[och % 2][:, :],
                                    in_=self.wq[128 * och:128 * (och + 1), :])
                m_wl[och] = self.sWQ[och % 2].inc(ins, 16)
                TEN.wait_ge(self.sWQ[och % 2].h, m_wl[och])
                for tg in range(4):
                    u = 4 * och + tg
                    if u >= 2:
                        TEN.wait_ge(self.sVE.h, m_rope[u - 2])
                    last = None
                    for cc in range(16):
                        last = TEN.matmul(pq[u % 2][:, :],
                                          wqb[och % 2][:, 128 * cc:128 * (cc + 1)],
                                          self.xnT[:, 2048 * cc + 512 * tg:
                                                   2048 * cc + 512 * (tg + 1)],
                                          start=(cc == 0), stop=(cc == 15))
                    m_mm[(och, tg)] = self.sPE.inc(last)
                    VEC.wait_ge(self.sPE.h, m_mm[(och, tg)])
                    if u >= 2:
                        VEC.wait_ge(self.sQSs[u % 2].h, m_qs[u - 2])
                        VEC.wait_ge(self.sSC.h, m_sq[u - 2])
                    cS = self.tabq[0:64, 512 * tg:512 * (tg + 1)]
                    sS = self.tabq[64:128, 512 * tg:512 * (tg + 1)]
                    self.sSE.inc(VEC.tensor_tensor(rp1[:, :], pq[u % 2][0:64, :], cS, ALU.mult))
                    self.sSE.inc(VEC.tensor_tensor(rp2[:, :], pq[u % 2][64:128, :], sS, ALU.mult))
                    self.sSE.inc(VEC.tensor_tensor(rp3[:, :], pq[u % 2][64:128, :], cS, ALU.mult))
                    self.sSE.inc(VEC.tensor_tensor(rp4[:, :], pq[u % 2][0:64, :], sS, ALU.mult))
                    VEC.wait_ge(self.sSE.h, self.sSE.n)
                    ins = VEC.tensor_tensor(qro[u % 2][0:64, :], rp1[:, :], rp2[:, :],
                                            ALU.subtract)
                    self.sVE.inc(ins)
                    ins = VEC.tensor_tensor(qro[u % 2][64:128, :], rp3[:, :], rp4[:, :],
                                            ALU.add)
                    m_rope[u] = self.sVE.inc(ins)
                    GPS.wait_ge(self.sVE.h, m_rope[u])
                    ins = GPS.dma_start(
                        out=self.q_scr[:, 2048 * och + 512 * tg:
                                       2048 * och + 512 * (tg + 1)],
                        in_=qro[u % 2][:, :])
                    m_qs[u] = self.sQSs[u % 2].inc(ins, 16)
                    SCA.wait_ge(self.sVE.h, m_rope[u])
                    if u >= 1:
                        SCA.wait_ge(self.sPE.h, m_q2.get(u - 1, 0))
                    ins = SCA.activation(sqb[u % 2][:, :], qro[u % 2][:, :], AF.Square)
                    m_sq[u] = self.sSC.inc(ins)
                    TEN.wait_ge(self.sSC.h, m_sq[u])
                    if u >= 1:
                        TEN.wait_ge(self.sVE2.h, m_mx[u - 1])
                    ins = TEN.matmul(pq2[:, :], self.onesb_c[:, :], sqb[u % 2][:, :],
                                     start=True, stop=True)
                    m_q2[u] = self.sPE.inc(ins)
                    VEC.wait_ge(self.sPE.h, m_q2[u])
                    ins = VEC.reduce_max(mx1[0:1, 0:1], pq2[:, :], AX.X)
                    self.se(VEC, ins)
                    ins = VEC.tensor_tensor(self.maxqrow[0:1, och:och + 1],
                                            self.maxqrow[0:1, och:och + 1],
                                            mx1[0:1, 0:1], ALU.max)
                    m_mx[u] = self.sVE2.inc(ins)
                    VEC.wait_ge(self.sVE2.h, m_mx[u])
            for eng in (TEN, SCA, GPS, SYN, VEC):
                eng.wait_ge(self.sVE2.h, self.sVE2.n)
                eng.wait_ge(self.sQSs[0].h, self.sQSs[0].n)
                eng.wait_ge(self.sQSs[1].h, self.sQSs[1].n)

    def p25(self):
        nc, TEN, VEC, SCA, GPS, SYN = self.nc, self.TEN, self.VEC, self.SCA, self.GPS, self.SYN
        ins = SYN.dma_start(out=self.kt_scr[:, :], in_=self.ktT[:, :])
        self.sG2.inc(ins, 16)
        SYN.wait_ge(self.sG2.h, self.sG2.n)
        ins = SYN.dma_start(out=self.vp_scr[:, :], in_=self.vpn[:, :])
        self.sG2.inc(ins, 16)
        SYN.wait_ge(self.sG2.h, self.sG2.n)
        ins = SYN.dma_start(out=self.kn_scr[:, :], in_=self.kvhn[:, :])
        self.sG2.inc(ins, 16)
        SYN.wait_ge(self.sG2.h, self.sG2.n)
        for eng in (TEN, VEC, SCA, GPS):
            eng.wait_ge(self.sG2.h, self.sG2.n)
        self.esG.close()
        self.esT.close()

        with ExitStack() as esP:
            fuse = self.sb(esP, "p25_fuse", [128, 64 * 512], BF16)
            wgb = [self.sb(esP, f"p25_wg{i}", [128, E], BF16) for i in range(2)]
            wub = [self.sb(esP, f"p25_wu{i}", [128, E], BF16) for i in range(2)]
            wdb = [self.sb(esP, f"p25_wd{i}", [128, 2 * E], BF16) for i in range(2)]
            sgb = [self.sb(esP, f"p25_sg{i}", [128, 512], F32) for i in range(2)]
            usb = [self.sb(esP, f"p25_us{i}", [128, 512], F32) for i in range(2)]
            ssg = self.sb(esP, "p25_ssg", [128, 64], F32)
            ssu = self.sb(esP, "p25_ssu", [128, 64], F32)
            sdb = self.sb(esP, "p25_sdb", [128, E], F32)
            sdr = self.sb(esP, "p25_sdr", [1, E], F32)
            xsl = [self.sb(esP, f"p25_xs{i}", [128, 512], F32) for i in range(3)]
            ffb = [self.sb(esP, f"p25_ff{i}", [128, 512], F32) for i in range(3)]
            tmpe = [self.sb(esP, f"p25_te{i}", [128, 512], F32) for i in range(2)]

            ins = GPS.dma_start(out=ssg[:, :], in_=self.sgv[:, :])
            self.sWL.inc(ins, 16)
            GPS.wait_ge(self.sWL.h, self.sWL.n)
            ins = GPS.dma_start(out=ssu[:, :], in_=self.suv[:, :])
            self.sWL.inc(ins, 16)
            GPS.wait_ge(self.sWL.h, self.sWL.n)
            ins = GPS.dma_start(out=sdr[:, :], in_=self.sdv[:, :])
            wl0 = self.sWL.inc(ins, 16)

            with ExitStack() as esQ:
                psd = self.ps(esQ, "p25_psd", [128, 512], F32)
                TEN.wait_ge(self.sWL.h, wl0)
                for i in range(4):
                    if i > 0:
                        TEN.wait_ge(self.sVE.h, self.sVE.n)
                    ins = TEN.matmul(psd[:, :], self.onesf_r[0:1, 0:128],
                                     sdr[0:1, 512 * i:512 * (i + 1)],
                                     start=True, stop=True)
                    pe_m = self.sPE.inc(ins)
                    VEC.wait_ge(self.sPE.h, pe_m)
                    ins = VEC.tensor_copy(sdb[:, 512 * i:512 * (i + 1)], psd[:, :])
                    self.sVE.inc(ins)
                TEN.wait_ge(self.sVE.h, self.sVE.n)
                self.sdb_done = self.sVE.n
                SCA.wait_ge(self.sWL.h, wl0)

            self.ffb_ring = 0
            self.xsl_ring = 0
            self.xsl_free = {}
            self.xsl_done = {}
            self.sXO2 = self.sXO + [Sem(self.es.enter_context(self.nc.semaphore("sXO2c")))]
            m_w = {}
            m_gu = {}
            m_sc = {}
            m_fu = {}
            m_dn = {}
            dn_done = {}
            for tb in range(4):
                with ExitStack() as esQ:
                    pg = [self.ps(esQ, f"p25_pg{tb}_{i}", [128, 512], F32) for i in range(2)]
                    pu = [self.ps(esQ, f"p25_pu{tb}_{i}", [128, 512], F32) for i in range(2)]
                    for ic in range(64):
                        if ic == 0 and tb > 0:
                            TEN.wait_ge(self.sVE2.h, self.sVE2.n)
                        if ic >= 2:
                            GPS.wait_ge(self.sPE.h, m_gu[(tb, ic - 2)])
                        sw = self.sWG[ic % 2]
                        ins = GPS.dma_start(out=wgb[ic % 2][:, :],
                                            in_=self.wgt[128 * ic:128 * (ic + 1), :])
                        sw.inc(ins, 16)
                        ins = GPS.dma_start(out=wub[ic % 2][:, :],
                                            in_=self.wut[128 * ic:128 * (ic + 1), :])
                        m_w[(tb, ic)] = sw.inc(ins, 16)
                        TEN.wait_ge(sw.h, m_w[(tb, ic)])
                        if ic >= 2:
                            TEN.wait_ge(self.sSC.h, m_sc[(tb, ic - 2)])
                        last = None
                        for cc in range(16):
                            last = TEN.matmul(pg[ic % 2][:, :],
                                              wgb[ic % 2][:, 128 * cc:128 * (cc + 1)],
                                              self.xnT[:, 2048 * cc + 512 * tb:
                                                       2048 * cc + 512 * (tb + 1)],
                                              start=(cc == 0), stop=(cc == 15))
                        for cc in range(16):
                            last = TEN.matmul(pu[ic % 2][:, :],
                                              wub[ic % 2][:, 128 * cc:128 * (cc + 1)],
                                              self.xnT[:, 2048 * cc + 512 * tb:
                                                       2048 * cc + 512 * (tb + 1)],
                                              start=(cc == 0), stop=(cc == 15))
                        m_gu[(tb, ic)] = self.sPE.inc(last)
                        SCA.wait_ge(self.sPE.h, m_gu[(tb, ic)])
                        if ic >= 2:
                            SCA.wait_ge(self.sVE.h, m_fu[(tb, ic - 2)])
                        self.sSC.inc(SCA.activation(sgb[ic % 2][:, :], pg[ic % 2][:, :],
                                                    AF.Silu, scale=ssg[:, ic:ic + 1]))
                        ins = SCA.activation(usb[ic % 2][:, :], pu[ic % 2][:, :],
                                             AF.Copy, scale=ssu[:, ic:ic + 1])
                        m_sc[(tb, ic)] = self.sSC.inc(ins)
                        VEC.wait_ge(self.sSC.h, m_sc[(tb, ic)])
                        if ic == 0:
                            VEC.wait_ge(self.sVE.h, self.sdb_done)
                        if tb > 0 and ic == 0:
                            VEC.wait_ge(self.sPE.h, dn_done[tb - 1])
                        ins = VEC.tensor_tensor(fuse[:, 512 * ic:512 * (ic + 1)],
                                                sgb[ic % 2][:, :], usb[ic % 2][:, :],
                                                ALU.mult)
                        m_fu[(tb, ic)] = self.sVE.inc(ins)
                with ExitStack() as esQ:
                    pd = [self.ps(esQ, f"p25_pd{tb}_{j}", [128, 512], F32) for j in range(8)]
                    for p2 in range(2):
                        for iq in range(32):
                            par = iq % 2
                            if iq >= 2:
                                GPS.wait_ge(self.sPE.h, m_dn[(tb, p2, iq - 2)])
                            else:
                                prev = ((tb, p2 - 1, 30 + iq) if p2 == 1 else
                                        (tb - 1, 1, 30 + iq) if tb > 0 else None)
                                if prev is not None and prev in m_dn:
                                    GPS.wait_ge(self.sPE.h, m_dn[prev])
                            src_ap = bass.AP(self.wdn, 256 * iq * E,
                                             [[E, 128], [128 * E, 2], [1, E]])
                            ins = GPS.dma_start(out=wdb[par][:, :], in_=src_ap)
                            wd_m = self.sWD[par].inc(ins, 16)
                            TEN.wait_ge(self.sWD[par].h, wd_m)
                            if iq == 0:
                                TEN.wait_ge(self.sVE.h, m_fu[(tb, 63)])
                                if (tb, p2) != (0, 0):
                                    TEN.wait_ge(self.sVE2.h, self.sVE2.n)
                            last = None
                            for sub in range(2):
                                ic = 2 * iq + sub
                                for j in range(2):
                                    ts_ = 2 * p2 + j
                                    for oc in range(4):
                                        last = TEN.matmul(
                                            pd[4 * j + oc][:, :],
                                            fuse[:, 512 * ic + 128 * ts_:
                                                 512 * ic + 128 * (ts_ + 1)],
                                            wdb[par][:, 2048 * sub + 512 * oc:
                                                     2048 * sub + 512 * (oc + 1)],
                                            start=(ic == 0), stop=(ic == 63))
                            m_dn[(tb, p2, iq)] = self.sPE.inc(last)
                        pe_last = self.sPE.n
                        VEC.wait_ge(self.sPE.h, pe_last)
                        for k8 in range(8):
                            j, oc = divmod(k8, 4)
                            ri = self.xsl_ring
                            xslot = ri % 3
                            self.xsl_ring += 1
                            trow = 512 * tb + 128 * (2 * p2 + j)
                            if ri >= 3:
                                SYN.wait_ge(self.sVE2.h, self.xsl_done[ri - 3])
                            ins = SYN.dma_start(
                                out=xsl[xslot][:, :],
                                in_=self.x_own[trow:trow + 128,
                                               512 * oc:512 * (oc + 1)])
                            m_x = self.sXO2[xslot].inc(ins, 16)
                            slot = self.ffb_ring % 3
                            self.ffb_ring += 1
                            if self.sFB[slot].n > 0:
                                VEC.wait_ge(self.sFB[slot].h, self.sFB[slot].n)
                            VEC.wait_ge(self.sXO2[xslot].h, m_x)
                            ins = VEC.tensor_tensor(tmpe[oc % 2][:, :],
                                                    pd[4 * j + oc][:, :],
                                                    sdb[:, 512 * oc:512 * (oc + 1)],
                                                    ALU.mult)
                            self.se(VEC, ins)
                            ins = VEC.tensor_tensor(
                                ffb[slot][:, :], tmpe[oc % 2][:, :],
                                xsl[xslot][:, :], ALU.add)
                            ve_m = self.sVE2.inc(ins)
                            self.xsl_done[ri] = ve_m
                            SYN.wait_ge(self.sVE2.h, ve_m)
                            ins = SYN.dma_start(
                                out=self.fp_scr[trow:trow + 128,
                                                512 * oc:512 * (oc + 1)],
                                in_=ffb[slot][:, :])
                            self.sFB[slot].inc(ins, 16)
                        dn_done[tb] = pe_last
            if DBG_EXTRA:
                self.dbg_dump([("sdb", sdb[:, :], [128, E], 0),
                               ("ssg", ssg[:, :], [128, 64], 0)])
            for eng in (TEN, VEC, SCA, GPS, SYN):
                eng.wait_ge(self.sVE2.h, self.sVE2.n)
        self.esA.close()

    def p3(self):
        nc, TEN, VEC, SCA, GPS, SYN = self.nc, self.TEN, self.VEC, self.SCA, self.GPS, self.SYN
        c_ = self.c_
        esB = ExitStack()
        self.esB = esB
        self.dmask = self.sb(esB, "b_dmask", [128, 4 * 512], BF16)
        self.OT = self.sb(esB, "b_OT", [128, H * 2048], BF16)
        self.wout_sb = self.sb(esB, "b_wout", [128, 16 * 2048], BF16)
        esB2 = ExitStack()
        self.esB2 = esB2
        ktT = self.sb(esB2, "b_ktT", [128, 4096], BF16)
        vpn = self.sb(esB2, "b_vpn", [128, 4096], BF16)
        kvhn = self.sb(esB2, "b_kvhn", [128, 1024], F32)
        qc = [self.sb(esB2, f"b_qc{i}", [128, 2048], BF16) for i in range(2)]
        ktp = [self.sb(esB2, f"b_ktp{i}", [128, 4096], BF16) for i in range(2)]
        vpp = [self.sb(esB2, "b_vpp0", [128, 4096], BF16)] * 2
        PT = [self.sb(esB2, f"b_pt{i}", [128, 512], BF16) for i in range(3)]
        khro = self.sb(esB2, "b_khro", [1, 4096], BF16)
        rbs = self.sb(esB2, "b_rbs", [128, 512], BF16)
        negB = [self.sb(esB2, f"b_nb{i}", [128, 1], F32) for i in range(2)]
        lr = self.sb(esB2, "b_lr", [1, 512], F32)
        rr = self.sb(esB2, "b_rr", [1, 512], F32)

        for v_ in range(4):
            ins = GPS.memset(self.dmask[:, 512 * v_:512 * (v_ + 1)], 1.0)
            self.se(GPS, ins)
            ins = GPS.affine_select(self.dmask[:, 512 * v_:512 * (v_ + 1)],
                                    self.dmask[:, 512 * v_:512 * (v_ + 1)],
                                    [[1, 512]], ALU.is_ge, 0.0,
                                    base=-128 * v_, channel_multiplier=-1)
            self.sGP.inc(ins)
        gp_mask_done = self.sGP.n
        for eng in (TEN, VEC, SCA):
            eng.wait_ge(self.sGP.h, gp_mask_done)
        ins = SYN.dma_start(out=ktT[:, :], in_=self.kt_scr[:, :])
        self.sG2.inc(ins, 16)
        SYN.wait_ge(self.sG2.h, self.sG2.n)
        ins = SYN.dma_start(out=vpn[:, :], in_=self.vp_scr[:, :])
        self.sG2.inc(ins, 16)
        SYN.wait_ge(self.sG2.h, self.sG2.n)
        ins = SYN.dma_start(out=kvhn[:, :], in_=self.kn_scr[:, :])
        g2m = self.sG2.inc(ins, 16)
        for b in range(16):
            if b > 0:
                GPS.wait_ge(self.sWL.h, self.sWL.n)
            ins = GPS.dma_start(out=self.wout_sb[:, 2048 * b:2048 * (b + 1)],
                                in_=self.wout[128 * b:128 * (b + 1), :])
            self.sWL.inc(ins, 16)
        self.wout_done = self.sWL.n
        GPS.wait_ge(self.sQSs[0].h, self.sQSs[0].n)
        GPS.wait_ge(self.sQSs[1].h, self.sQSs[1].n)
        ins = GPS.dma_start(out=qc[0][:, :], in_=self.q_scr[:, 0:2048])
        self.sQC[0].inc(ins, 16)
        m_ql = {0: (0, self.sQC[0].n)}

        with ExitStack() as esQ:
            pS = [self.ps(esQ, f"b_pS{i}", [128, 512], F32) for i in range(3)]
            pO = [self.ps(esQ, f"b_pO{i}", [128, 512], F32) for i in range(2)]
            pL = [self.ps(esQ, f"b_pL{i}", [1, 512], F32) for i in range(2)]
            pM0 = self.ps(esQ, "b_pM0", [128, 512], F32)
            pM = [pM0, pM0]

            for eng in (TEN, VEC, SCA):
                eng.wait_ge(self.sG2.h, g2m)

            head_end = {}
            for h in range(H):
                if h + 1 < H:
                    if h >= 1:
                        GPS.wait_ge(self.sPE.h, head_end[h - 1]["lastS"])
                    ins = GPS.dma_start(out=qc[(h + 1) % 2][:, :],
                                        in_=self.q_scr[:, 2048 * (h + 1):2048 * (h + 2)])
                    sl = (h + 1) % 2
                    m_ql[h + 1] = (sl, self.sQC[sl].inc(ins, 16))

                he2 = head_end.get(h - 2)
                kh_ev = []
                for b4 in range(8):
                    if b4 >= 1:
                        TEN.wait_ge(self.sVE.h, kh_ev[b4 - 1])
                    elif h > 0:
                        TEN.wait_ge(self.sVE2.h, head_end[h - 1]["sve2"])
                    last = None
                    for k in range(4):
                        mc = 4 * b4 + k
                        last = TEN.transpose(pM0[0:1, 128 * k:128 * (k + 1)],
                                             kvhn[:, 32 * mc + h:32 * mc + h + 1],
                                             self.identf[:, :])
                    pe_m = self.sPE.inc(last)
                    VEC.wait_ge(self.sPE.h, pe_m)
                    ins = VEC.tensor_copy(khro[0:1, 512 * b4:512 * (b4 + 1)],
                                          pM0[0:1, 0:512])
                    kh_ev.append(self.sVE.inc(ins))
                ins = VEC.tensor_tensor(self.b11[0:1, 0:1], self.mkrow[0:1, h:h + 1],
                                         self.mkrow[0:1, h:h + 1], ALU.mult)
                self.se(VEC, ins)
                ins = VEC.tensor_tensor(self.b11[0:1, 1:2], self.b11[0:1, 0:1],
                                        self.knm[0:1, 0:1], ALU.mult)
                self.se(VEC, ins)
                ins = VEC.tensor_tensor(self.b11[0:1, 2:3], self.b11[0:1, 1:2],
                                        self.maxqrow[0:1, h:h + 1], ALU.mult)
                ve_m = self.sVE.inc(ins)
                SCA.wait_ge(self.sVE.h, ve_m)
                ins = SCA.activation(self.b11[0:1, 3:4], self.b11[0:1, 2:3], AF.Ln)
                self.se(SCA, ins)
                ins = SCA.activation(self.b11[0:1, 4:5], self.b11[0:1, 3:4], AF.Exp,
                                     scale=c_[0:1, COL_HALFC:COL_HALFC + 1])
                sc_m = self.sSC.inc(ins)
                VEC.wait_ge(self.sSC.h, sc_m)
                ins = VEC.tensor_scalar(self.b11[0:1, 5:6], self.b11[0:1, 4:5], -1.0,
                                        None, ALU.mult)
                ve_m = self.sVE.inc(ins)
                TEN.wait_ge(self.sVE.h, ve_m)
                TEN.wait_ge(self.sVE.h, kh_ev[7])
                ins = TEN.matmul(pM0[:, 0:1], self.onesf_r[0:1, :],
                                 self.b11[0:1, 5:6], start=True, stop=True)
                pe_m = self.sPE.inc(ins)
                VEC.wait_ge(self.sPE.h, pe_m)
                ins = VEC.tensor_copy(negB[h % 2][:, :], pM0[:, 0:1])
                nb_ev = self.sVE.inc(ins)
                ktp_ev = []
                for c8 in range(8):
                    if c8 == 0:
                        TEN.wait_ge(self.sVE.h, nb_ev)
                    else:
                        TEN.wait_ge(self.sVE.h, ktp_ev[c8 - 1])
                    ins = TEN.matmul(pM0[:, :], self.onesb_r[0:1, :],
                                     khro[0:1, 512 * c8:512 * (c8 + 1)],
                                     start=True, stop=True)
                    pe_m = self.sPE.inc(ins)
                    VEC.wait_ge(self.sPE.h, pe_m)
                    if c8 == 0 and he2 is not None:
                        VEC.wait_ge(self.sPE.h, he2["lastS"])
                    ins = VEC.tensor_tensor(ktp[h % 2][:, 512 * c8:512 * (c8 + 1)],
                                            ktT[:, 512 * c8:512 * (c8 + 1)],
                                            pM0[:, :], ALU.mult)
                    ktp_ev.append(self.sVE.inc(ins))
                he1 = head_end.get(h - 1)
                for mc in range(32):
                    if mc == 0 and he1 is not None:
                        VEC.wait_ge(self.sPE.h, he1["lastAV"])
                    ins = VEC.tensor_scalar_mul(
                        vpp[h % 2][:, 128 * mc:128 * (mc + 1)],
                        vpn[:, 128 * mc:128 * (mc + 1)],
                        kvhn[:, 32 * mc + 16 + h:32 * mc + 16 + h + 1])
                    self.sVE.inc(ins)
                hprep = self.sVE.n

                TEN.wait_ge(self.sQC[m_ql[h][0]].h, m_ql[h][1])
                st0 = self.snap()
                bmarks = {}
                ends = []
                for half in (0, 1):
                    self.restore(st0)
                    pairs = pair_list(half)
                    NP = len(pairs)
                    assert NP == 72, NP
                    slot_last = {}
                    for pj, pr in enumerate(pairs):
                        if pr["last"]:
                            slot_last[pr["si"]] = pj

                    def exp_mark(pj, pairs=pairs, st0=st0):
                        return st0["sSC"] + pj + 1 + pairs[pj]["si"]

                    def rinv_mark(si, slot_last=slot_last, st0=st0):
                        return st0["sSC"] + slot_last[si] + 2 + si

                    def lv_mark(si, st0=st0):
                        return st0["sVE2"] + 2 * si + 1

                    def oev_mark(si, st0=st0):
                        return st0["sVE2"] + 2 * si + 2

                    gp_mark = {}
                    d = 0
                    for pj, pr in enumerate(pairs):
                        if pr["dv"] is not None:
                            d += 1
                            gp_mark[pj] = st0["sGP"] + d
                    mS = {}
                    mAV = {}
                    mBC = {}

                    def emit_lav(pj, pairs=pairs, gp_mark=gp_mark, mS=mS, mAV=mAV,
                                 mBC=mBC, exp_mark=exp_mark, rinv_mark=rinv_mark,
                                 oev_mark=oev_mark, ktp_ev=ktp_ev, h=h):
                        pr = pairs[pj]
                        si, mc = pr["si"], pr["mc"]
                        if pr["dv"] is not None:
                            TEN.wait_ge(self.sGP.h, gp_mark[pj])
                        else:
                            TEN.wait_ge(self.sSC.h, exp_mark(pj))
                        if pr["first"] and si >= 2:
                            TEN.wait_ge(self.sVE2.h, oev_mark(si - 2))
                        TEN.matmul(pL[si % 2][:, :], self.onesb_c[:, :],
                                   PT[pj % 3][:, :], start=pr["first"], stop=pr["last"])
                        ins = TEN.matmul(pO[si % 2][:, :],
                                         vpp[h % 2][:, 128 * mc:128 * (mc + 1)],
                                         PT[pj % 3][:, :],
                                         start=pr["first"], stop=pr["last"])
                        mAV[pj] = self.sPE.inc(ins)
                        if pr["last"]:
                            TEN.wait_ge(self.sSC.h, rinv_mark(si))
                            if si == 0:
                                TEN.wait_ge(self.sVE.h, ktp_ev[7])
                            else:
                                TEN.wait_ge(self.sVE2.h, oev_mark(si - 1))
                            ins = TEN.matmul(pM0[:, :], self.onesf_r[0:1, :],
                                             rr[0:1, :], start=True, stop=True)
                            mBC[si] = self.sPE.inc(ins)

                    with (TEN.If_eq(self.regs["t"], 0) if half == 0 else TEN.Else()):
                        TEN.wait_ge(self.sVE.h, hprep)
                        TEN.wait_ge(self.sSC.h, st0["sSC"])
                        TEN.wait_ge(self.sVE2.h, st0["sVE2"])
                        TEN.wait_ge(self.sGP.h, st0["sGP"])
                        for pj, pr in enumerate(pairs):
                            if pj >= 3:
                                TEN.wait_ge(self.sSC.h, exp_mark(pj - 3))
                            ins = TEN.matmul(pS[pj % 3][:, :],
                                             ktp[h % 2][:, 128 * pr["mc"]:
                                                        128 * (pr["mc"] + 1)],
                                             qc[h % 2][:, 512 * pr["tg"]:
                                                       512 * (pr["tg"] + 1)],
                                             start=True, stop=True)
                            mS[pj] = self.sPE.inc(ins)
                            if pj >= 2:
                                emit_lav(pj - 2)
                        emit_lav(NP - 2)
                        emit_lav(NP - 1)
                    bmarks[half] = dict(mS=mS, mAV=mAV, mBC=mBC, exp_mark=exp_mark,
                                        rinv_mark=rinv_mark, lv_mark=lv_mark,
                                        oev_mark=oev_mark, gp_mark=gp_mark,
                                        slot_last=slot_last, pairs=pairs)
                    ends.append(self.snap())
                assert ends[0] == ends[1], (h, "TEN", ends)

                st1 = self.snap()
                e2 = []
                for half in (0, 1):
                    self.restore(st1)
                    bm = bmarks[half]
                    with (SCA.If_eq(self.regs["s"], 0) if half == 0 else SCA.Else()):
                        SCA.wait_ge(self.sVE.h, nb_ev)
                        for pj, pr in enumerate(bm["pairs"]):
                            SCA.wait_ge(self.sPE.h, bm["mS"][pj])
                            if pj >= 3:
                                SCA.wait_ge(self.sPE.h, bm["mAV"][pj - 3])
                            ins = SCA.activation(PT[pj % 3][:, :], pS[pj % 3][:, :],
                                                 AF.Exp, bias=negB[h % 2][:, :])
                            self.sSC.inc(ins)
                            if pr["last"]:
                                SCA.wait_ge(self.sVE2.h, bm["lv_mark"](pr["si"]))
                                if pr["si"] >= 1:
                                    SCA.wait_ge(self.sPE.h, bm["mBC"][pr["si"] - 1])
                                SCA.activation(rr[:, :], lr[:, :], AF.Ln)
                                ins = SCA.activation(rr[:, :], rr[:, :], AF.Exp,
                                                     scale=c_[0:1, COL_NEG1:COL_NEG1 + 1])
                                self.sSC.inc(ins)
                    e2.append(self.snap())
                assert e2[0] == e2[1], (h, "SCA")

                st2 = self.snap()
                e3 = []
                for half in (0, 1):
                    self.restore(st2)
                    bm = bmarks[half]
                    with (GPS.If_eq(self.regs["g"], 0) if half == 0 else GPS.Else()):
                        for pj, pr in enumerate(bm["pairs"]):
                            if pr["dv"] is None:
                                continue
                            GPS.wait_ge(self.sSC.h, bm["exp_mark"](pj))
                            ins = GPS.affine_select(
                                PT[pj % 3][:, :], PT[pj % 3][:, :], [[1, 512]],
                                ALU.is_ge, 0.0, base=-128 * pr["dv"],
                                channel_multiplier=-1)
                            self.sGP.inc(ins)
                    e3.append(self.snap())
                assert e3[0] == e3[1], (h, "GPS")

                st3 = self.snap()
                e4 = []
                for half in (0, 1):
                    self.restore(st3)
                    bm = bmarks[half]
                    with (VEC.If_eq(self.regs["v"], 0) if half == 0 else VEC.Else()):
                        for pj, pr in enumerate(bm["pairs"]):
                            if not pr["last"]:
                                continue
                            si = pr["si"]
                            tg = SLOT_LNT[si]
                            VEC.wait_ge(self.sPE.h, bm["mAV"][pj])
                            if si >= 1:
                                VEC.wait_ge(self.sSC.h, bm["rinv_mark"](si - 1))
                            ins = VEC.tensor_scalar(lr[:, :], pL[si % 2][:, :],
                                                    1e-30, None, ALU.add)
                            self.sVE2.inc(ins)
                            VEC.wait_ge(self.sPE.h, bm["mBC"][si])
                            ins = VEC.tensor_copy(rbs[:, :], pM0[:, :])
                            self.se(VEC, ins)
                            ins = VEC.tensor_tensor(
                                self.OT[:, 2048 * h + 512 * tg:
                                        2048 * h + 512 * (tg + 1)],
                                pO[si % 2][:, :], rbs[:, :], ALU.mult)
                            self.sVE2.inc(ins)
                    e4.append(self.snap())
                assert e4[0] == e4[1], (h, "VEC")

                head_end[h] = dict(
                    lastS=max(bmarks[0]["mS"][71], bmarks[1]["mS"][71]),
                    lastAV=max(bmarks[0]["mAV"][71], bmarks[1]["mAV"][71]),
                    sve2=self.sVE2.n)

            for eng in (TEN, VEC, SCA, GPS, SYN):
                eng.wait_ge(self.sVE2.h, self.sVE2.n)
                eng.wait_ge(self.sSC.h, self.sSC.n)
        self.esB2.close()

    def p4(self):
        nc, TEN, VEC, SCA, GPS, SYN = self.nc, self.TEN, self.VEC, self.SCA, self.GPS, self.SYN
        with ExitStack() as esP:
            fpt = [self.sb(esP, f"p4_fp{i}", [128, E], F32) for i in range(3)]
            orow = [self.sb(esP, f"p4_or{i}", [128, E], F32) for i in range(2)]
            py = [self.ps(esP, f"p4_py{j}", [128, 512], F32) for j in range(8)]

            TEN.wait_ge(self.sWL.h, self.wout_done)
            for s_ in self.sFB:
                SYN.wait_ge(s_.h, s_.n)
            m_fl = {}
            m_y = {}
            m_ev = {}
            for t in range(16):
                if t >= 3:
                    SYN.wait_ge(self.sVE2.h, m_ev[t - 3])
                ins = SYN.dma_start(out=fpt[t % 3][:, :],
                                    in_=self.fp_scr[128 * t:128 * (t + 1), :])
                m_fl[t] = self.sFP[t % 3].inc(ins, 16)
                par = t % 2
                if t >= 2:
                    TEN.wait_ge(self.sVE2.h, m_ev[t - 2])
                last = None
                for hh in range(16):
                    for oc in range(4):
                        last = TEN.matmul(
                            py[4 * par + oc][:, :],
                            self.OT[:, 2048 * hh + 128 * t:2048 * hh + 128 * (t + 1)],
                            self.wout_sb[:, 2048 * hh + 512 * oc:
                                         2048 * hh + 512 * (oc + 1)],
                            start=(hh == 0), stop=(hh == 15))
                m_y[t] = self.sPE.inc(last)
                VEC.wait_ge(self.sPE.h, m_y[t])
                VEC.wait_ge(self.sFP[t % 3].h, m_fl[t])
                if t >= 2:
                    VEC.wait_ge(self.sOU[t % 2].h, self.sOU[t % 2].n)
                for oc in range(4):
                    ins = VEC.tensor_tensor(orow[par][:, 512 * oc:512 * (oc + 1)],
                                            py[4 * par + oc][:, :],
                                            fpt[t % 3][:, 512 * oc:512 * (oc + 1)],
                                            ALU.add)
                m_ev[t] = self.sVE2.inc(ins)
                SYN.wait_ge(self.sVE2.h, m_ev[t])
                ins = SYN.dma_start(out=self.out[128 * t:128 * (t + 1), :],
                                    in_=orow[par][:, :])
                self.sOU[par].inc(ins, 16)
            SYN.wait_ge(self.sOU[0].h, self.sOU[0].n)
            SYN.wait_ge(self.sOU[1].h, self.sOU[1].n)
        self.esB.close()

    def dbg_dump(self, tens):
        nc, SYN, GPS = self.nc, self.SYN, self.GPS
        with ExitStack() as esD:
            for i, (name, t, shape, via_sbuf) in enumerate(tens):
                o = nc.dram_tensor("dbg_" + name, shape, t.dtype if hasattr(t, "dtype")
                                   else F32, kind="ExternalOutput")
                if via_sbuf:
                    tmp = self.sb(esD, f"dbg_t{i}", [128, shape[1]], t.dtype)
                    for rr_ in range(0, shape[0], 128):
                        ins = SYN.dma_start(out=tmp[:, :], in_=t[rr_:rr_ + 128, :])
                        n = self.sOUT.inc(ins, 16)
                        SYN.wait_ge(self.sOUT.h, n)
                        ins = SYN.dma_start(out=o[rr_:rr_ + 128, :], in_=tmp[:, :])
                        n = self.sOUT.inc(ins, 16)
                        SYN.wait_ge(self.sOUT.h, n)
                else:
                    ins = SYN.dma_start(out=o[:, :], in_=t)
                    n = self.sOUT.inc(ins, 16)
                    SYN.wait_ge(self.sOUT.h, n)
            SYN.wait_ge(self.sOUT.h, self.sOUT.n)

    def build(self, stop=4):
        self.p0()
        self.p1()
        if stop == 1:
            self.dbg_dump([
                ("ktT", self.ktT[:, :], [128, 4096], 0),
                ("vpn", self.vpn[:, :], [128, 4096], 0),
                ("kvhn", self.kvhn[:, :], [128, 1024], 0),
                ("mkrow", self.mkrow[:, :], [1, 32], 0),
                ("knm", self.knm[:, :], [1, 4], 0),
                ("xnT", self.xnT[:, :], [128, 16 * 2048], 0),
            ])
            self.esG.close(); self.esT.close(); self.esA.close(); self.es.close()
            return self.nc
        self.p2()
        if stop == 2:
            self.dbg_dump([
                ("qscr", self.q_scr[:, :], [128, H * 2048], 0),
                ("maxq", self.maxqrow[:, :], [1, 16], 0),
            ])
            self.esG.close(); self.esT.close(); self.esA.close(); self.es.close()
            return self.nc
        self.p25()
        if stop == 25:
            self.dbg_dump([("fps", self.fp_scr, [2048, E], 1)])
            self.es.close()
            return self.nc
        self.p3()
        if stop == 3:
            self.dbg_dump([("OT", self.OT[:, :], [128, H * 2048], 0)])
            self.esB.close()
            self.es.close()
            return self.nc
        if DBG_EXTRA:
            for s_ in self.sFB:
                self.SYN.wait_ge(s_.h, s_.n)
            self.dbg_dump([("fps", self.fp_scr, [2048, E], 1)])
        self.p4()
        self.es.close()
        return self.nc


DBG_EXTRA = False
_NC_CACHE = {}


def get_nc(stop=4):
    if stop not in _NC_CACHE:
        _NC_CACHE[stop] = KB().build(stop=stop)
    return _NC_CACHE[stop]


def kernel(L0_attn_x, L0_attn_Wq, L0_attn_Wkv, L0_attn_Wkvh, L0_attn_Wg,
           L0_attn_Wout, L0_ffn_w_gate, L0_ffn_w_up, L0_ffn_w_down,
           L0_ffn_s_gate, L0_ffn_s_up, L0_ffn_s_down):
    x = np.ascontiguousarray(np.asarray(L0_attn_x, dtype=np.float32))
    WqT = np.asarray(L0_attn_Wq, np.float32).T
    wq_h = np.ascontiguousarray(
        WqT.reshape(16, 128, 16, 128).transpose(2, 1, 0, 3).reshape(E, E))
    wkv_h = np.ascontiguousarray(
        np.asarray(L0_attn_Wkv, np.float32).T.reshape(16, 128, 256)
        .transpose(1, 0, 2).reshape(128, 16 * 256))
    wsm_full = np.concatenate([np.asarray(L0_attn_Wkvh, np.float32).T,
                               np.asarray(L0_attn_Wg, np.float32).T], axis=1)
    wsm_h = np.ascontiguousarray(
        wsm_full.reshape(16, 128, 64).transpose(1, 0, 2).reshape(128, 16 * 64))
    wout_h = np.ascontiguousarray(np.asarray(L0_attn_Wout, np.float32).T)
    wgt_h = np.ascontiguousarray(
        np.asarray(L0_ffn_w_gate, np.int8).reshape(16, 128, 64, 128)
        .transpose(2, 1, 0, 3).reshape(INTER, E))
    wut_h = np.ascontiguousarray(
        np.asarray(L0_ffn_w_up, np.int8).reshape(16, 128, 64, 128)
        .transpose(2, 1, 0, 3).reshape(INTER, E))
    wdn_h = np.ascontiguousarray(np.asarray(L0_ffn_w_down, np.int8))
    sg_h = np.ascontiguousarray(np.asarray(L0_ffn_s_gate, np.float32).reshape(64, 128).T)
    su_h = np.ascontiguousarray(np.asarray(L0_ffn_s_up, np.float32).reshape(64, 128).T)
    sd_h = np.ascontiguousarray(np.asarray(L0_ffn_s_down, np.float32).reshape(1, E))

    nc = get_nc()
    in_maps = []
    rowmaps = []
    for c in range(8):
        b, half = divmod(c, 2)
        rows = (np.r_[0:1024, 3072:4096] if half == 0 else np.r_[1024:3072])
        rowmaps.append(rows)
        in_maps.append({
            "xb": np.ascontiguousarray(x[b]),
            "x_own": np.ascontiguousarray(x[b][rows]),
            "pos_own": np.ascontiguousarray(rows.astype(np.float32).reshape(1, 2048)),
            "cfg": np.array([[half, 0, 0, 0]], np.int32),
            "wq": wq_h, "wkv": wkv_h, "wsm": wsm_h, "wout": wout_h,
            "wgt": wgt_h, "wut": wut_h, "wdn": wdn_h,
            "sgv": sg_h, "suv": su_h, "sdv": sd_h,
        })
    res = run_bass_kernel_spmd(nc, in_maps, core_ids=list(range(8)))
    outp = np.empty((4, NSEQ, E), np.float32)
    for c in range(8):
        b = c // 2
        outp[b][rowmaps[c]] = res.results[c]["out"]
    return outp


# version 09
# speedup vs baseline: 1.0551x; 1.0551x over previous
import math
import numpy as np
from contextlib import ExitStack

import concourse.bass as bass
import concourse.mybir as mybir
from concourse.bass_utils import run_bass_kernel_spmd

F32 = mybir.dt.float32
BF16 = mybir.dt.bfloat16
I8 = mybir.dt.int8
I32 = mybir.dt.int32
AF = mybir.ActivationFunctionType
ALU = mybir.AluOpType
AX = mybir.AxisListType

PI = float(np.pi)
CW1 = 6.28125
CW2 = 2 * PI - CW1
E = 2048
H = 16
D = 128
NSEQ = 4096
INTER = 8192
TGABS = {0: [0, 512, 3072, 3584], 1: [1024, 1536, 2048, 2560]}
SLOT_LNT = [3, 2, 1, 0]
OWN_GROUPS = {0: [0, 1, 6, 7], 1: [2, 3, 4, 5]}
COL_EPS, COL_NHALF, COL_ONE, COL_NEG1, COL_LN1E4, COL_HALFC, COL_I2PI = range(7)


class Sem:
    def __init__(self, h):
        self.h = h
        self.n = 0

    def inc(self, ins, k=1):
        ins.then_inc(self.h, k)
        self.n += k
        return self.n


def pair_list(half):
    pairs = []
    for si, tg in enumerate(SLOT_LNT):
        a0 = TGABS[half][tg]
        mcN = (a0 + 512) // 128
        for mc in range(mcN):
            dv = mc - a0 // 128
            pairs.append(dict(si=si, tg=tg, mc=mc,
                              dv=dv if 0 <= dv <= 3 else None,
                              first=(mc == 0), last=(mc == mcN - 1)))
    return pairs


class KB:
    def __init__(self):
        nc = bass.Bass("TRN2", target_bir_lowering=False, debug=False)
        self.nc = nc
        self.TEN, self.VEC, self.SCA, self.GPS, self.SYN = \
            nc.tensor, nc.vector, nc.scalar, nc.gpsimd, nc.sync
        dt = nc.dram_tensor
        self.xb = dt("xb", [NSEQ, E], F32, kind="ExternalInput")
        self.x_own = dt("x_own", [2048, E], F32, kind="ExternalInput")
        self.pos_own = dt("pos_own", [1, 2048], F32, kind="ExternalInput")
        self.cfg = dt("cfg", [1, 4], I32, kind="ExternalInput")
        self.wq = dt("wq", [E, E], F32, kind="ExternalInput")
        self.wkv = dt("wkv", [128, 16 * 256], F32, kind="ExternalInput")
        self.wsm = dt("wsm", [128, 16 * 64], F32, kind="ExternalInput")
        self.wout = dt("wout", [E, E], F32, kind="ExternalInput")
        self.wgt = dt("wgt", [INTER, E], I8, kind="ExternalInput")
        self.wut = dt("wut", [INTER, E], I8, kind="ExternalInput")
        self.wdn = dt("wdn", [INTER, E], I8, kind="ExternalInput")
        self.sgv = dt("sgv", [128, 64], F32, kind="ExternalInput")
        self.suv = dt("suv", [128, 64], F32, kind="ExternalInput")
        self.sdv = dt("sdv", [1, E], F32, kind="ExternalInput")
        self.out = dt("out", [2048, E], F32, kind="ExternalOutput")
        self.q_scr = dt("q_scr", [128, H * 2048], BF16, kind="Internal")
        self.fp_scr = dt("fp_scr", [2048, E], F32, kind="Internal")
        self.kt_scr = dt("kt_scr", [128, 4096], BF16, kind="Internal")
        self.vp_scr = dt("vp_scr", [128, 4096], BF16, kind="Internal")
        self.kn_scr = dt("kn_scr", [128, 1024], F32, kind="Internal")

        self.es = ExitStack()
        self.sems = {k: Sem(self.es.enter_context(nc.semaphore(k))) for k in
                     ["sXL", "sWL", "sQS", "sQL", "sFS", "sFL", "sOUT", "sG2",
                      "sPE", "sSC", "sVE", "sVE2", "sGP", "sMS", "sSE", "sSEs"]}
        self.sSE = self.sems["sSE"]
        def mksems(prefix, k):
            return [Sem(self.es.enter_context(nc.semaphore(f"{prefix}{i}")))
                    for i in range(k)]
        self.sXLs = mksems("sXLs", 3)
        self.sWQ = mksems("sWQ", 2)
        self.sQSs = mksems("sQSs", 2)
        self.sWG = mksems("sWG", 2)
        self.sWD = mksems("sWD", 2)
        self.sXO = mksems("sXO", 2)
        self.sFB = mksems("sFB", 4)
        self.sQC = mksems("sQC", 2)
        self.sFP = mksems("sFP", 3)
        self.sOU = mksems("sOU", 2)
        self.sSEg = Sem(self.es.enter_context(nc.semaphore("sSEg")))
        for k, v in self.sems.items():
            setattr(self, k, v)

    def sb(self, st, name, shape, dtt):
        return st.enter_context(self.nc.sbuf_tensor(name, shape, dtt))

    def ps(self, st, name, shape, dtt):
        return st.enter_context(self.nc.psum_tensor(name, shape, dtt))

    def se(self, eng, ins):
        s = (self.sSEg if eng is self.GPS
             else self.sems["sSEs"] if eng is self.SCA else self.sSE)
        n = s.inc(ins)
        eng.wait_ge(s.h, n)

    def snap(self):
        return {k: s.n for k, s in self.sems.items()}

    def restore(self, st):
        for k, s in self.sems.items():
            s.n = st[k]

    def p0(self):
        nc, TEN, VEC, SCA, GPS, SYN = self.nc, self.TEN, self.VEC, self.SCA, self.GPS, self.SYN
        es = self.es
        self.c_ = self.sb(es, "c_", [128, 8], F32)
        self.onesb_c = self.sb(es, "onesb_c", [128, 1], BF16)
        self.onesb_r = self.sb(es, "onesb_r", [1, 128], BF16)
        self.onesf_r = self.sb(es, "onesf_r", [1, 128], F32)
        self.identb = self.sb(es, "identb", [128, 128], BF16)
        self.identf = self.sb(es, "identf", [128, 128], F32)
        self.cfgs = self.sb(es, "cfgs", [1, 4], I32)
        self.maxqrow = self.sb(es, "maxqrow", [1, 16], F32)
        self.mkrow = self.sb(es, "mkrow", [1, 32], F32)
        self.knm = self.sb(es, "knm", [1, 4], F32)
        self.b11 = self.sb(es, "b11", [1, 8], F32)
        c_ = self.c_

        ins = SYN.dma_start(out=self.cfgs[:, :], in_=self.cfg[:, :])
        self.sMS.inc(ins, 16)
        for col, val in [(COL_EPS, 1e-5), (COL_NHALF, -0.5), (COL_ONE, 1.0),
                         (COL_NEG1, -1.0), (COL_LN1E4, -math.log(1e4) / 64.0),
                         (COL_HALFC, 0.5), (COL_I2PI, 1.0 / (2 * PI))]:
            self.sGP.inc(GPS.memset(c_[:, col:col + 1], val))
        self.sGP.inc(GPS.memset(self.onesb_c[:, :], 1.0))
        self.sGP.inc(GPS.memset(self.onesb_r[:, :], 1.0))
        self.sGP.inc(GPS.memset(self.onesf_r[:, :], 1.0))
        self.sGP.inc(GPS.memset(self.maxqrow[:, :], 0.0))
        self.sGP.inc(GPS.memset(self.knm[:, :], 0.0))
        ins = GPS.memset(self.identf[:, :], 1.0)
        self.se(GPS, ins)
        ins = GPS.affine_select(self.identf[:, :], self.identf[:, :], [[1, 128]],
                                ALU.is_equal, 0.0, base=0, channel_multiplier=-1)
        self.sGP.inc(ins)
        ins = GPS.memset(self.identb[:, :], 1.0)
        self.se(GPS, ins)
        ins = GPS.affine_select(self.identb[:, :], self.identb[:, :], [[1, 128]],
                                ALU.is_equal, 0.0, base=0, channel_multiplier=-1)
        self.sGP.inc(ins)

        self.regs = {}
        for nm, e in [("t", TEN), ("v", VEC), ("s", SCA), ("g", GPS)]:
            e.wait_ge(self.sMS.h, 16)
            r = e.alloc_register("rhalf_" + nm)
            e.reg_load(r, self.cfgs[0:1, 0:1])
            self.regs[nm] = r

        self.esA = ExitStack()
        self.xnT = self.sb(self.esA, "xnT", [128, 16 * 2048], BF16)
        self.esTq = ExitStack()
        self.tabq = self.sb(self.esTq, "tabq", [128, 2048], F32)
        self.esTk = ExitStack()
        self.tabk = self.sb(self.esTk, "tabk", [128, 4096], F32)
        self.esG = ExitStack()
        self.ktT = self.sb(self.esG, "ktT", [128, 4096], BF16)
        self.vpn = self.sb(self.esG, "vpn", [128, 4096], BF16)
        self.kvhn = self.sb(self.esG, "kvhn", [128, 1024], F32)

        with ExitStack() as esP:
            ang = self.sb(esP, "p0_ang", [64, 4096], F32)
            angq = self.sb(esP, "p0_angq", [64, 2048], F32)
            tb1 = self.sb(esP, "p0_tb1", [64, 4096], F32)
            tb2 = self.sb(esP, "p0_tb2", [64, 4096], F32)
            tbi = self.sb(esP, "p0_tbi", [64, 4096], I32)
            invj = self.sb(esP, "p0_inv", [64, 1], F32)
            jcol = self.sb(esP, "p0_jcol", [64, 1], F32)
            posq = self.sb(esP, "p0_posq", [1, 2048], F32)
            pp0 = self.ps(esP, "pp0", [64, 512], F32)

            SYN.wait_ge(self.sMS.h, self.sMS.n)
            ins = SYN.dma_start(out=posq[:, :], in_=self.pos_own[:, :])
            self.sMS.inc(ins, 16)
            TEN.wait_ge(self.sGP.h, self.sGP.n)
            VEC.wait_ge(self.sGP.h, self.sGP.n)
            self.sGP.inc(GPS.iota(jcol[:, :], [[1, 1]], base=0, channel_multiplier=1,
                                   allow_small_or_imprecise_dtypes=True))
            self.sGP.inc(GPS.iota(ang[:, :], [[1, 4096]], base=0, channel_multiplier=0,
                                  allow_small_or_imprecise_dtypes=True))
            self.sGP.inc(GPS.memset(self.ktT[:, :], 0.0))
            self.sGP.inc(GPS.memset(self.vpn[:, :], 0.0))
            self.sGP.inc(GPS.memset(self.kvhn[:, :], 0.0))

            SCA.wait_ge(self.sGP.h, self.sGP.n)
            TEN.wait_ge(self.sGP.h, self.sGP.n)
            VEC.wait_ge(self.sGP.h, self.sGP.n)
            ins = SCA.activation(invj[:, :], jcol[:, :], AF.Exp,
                                 scale=c_[0:64, COL_LN1E4:COL_LN1E4 + 1])
            self.sSC.inc(ins)
            VEC.wait_ge(self.sSC.h, self.sSC.n)
            ins = VEC.tensor_scalar_mul(ang[:, :], ang[:, :], invj[:, :])
            self.sVE.inc(ins)
            VEC.wait_ge(self.sVE.h, self.sVE.n)
            for i in range(4):
                TEN.wait_ge(self.sMS.h, self.sMS.n)
                if i > 0:
                    TEN.wait_ge(self.sVE.h, self.sVE.n)
                ins = TEN.matmul(pp0[:, :], self.onesf_r[0:1, 0:64],
                                 posq[0:1, 512 * i:512 * (i + 1)], start=True, stop=True)
                self.sPE.inc(ins)
                VEC.wait_ge(self.sPE.h, self.sPE.n)
                ins = VEC.tensor_copy(angq[:, 512 * i:512 * (i + 1)], pp0[:, :])
                self.sVE.inc(ins)
                VEC.wait_ge(self.sVE.h, self.sVE.n)
            ins = VEC.tensor_scalar_mul(angq[:, :], angq[:, :], invj[:, :])
            self.se(VEC, ins)

            def sin_into(dst, src, n, add_half_pi, post_scale=None):
                a = tb1[:, 0:n]
                if add_half_pi:
                    ins = VEC.tensor_scalar(a, src[:, 0:n], PI / 2, None, ALU.add)
                else:
                    ins = VEC.tensor_copy(a, src[:, 0:n])
                self.se(VEC, ins)
                t_ = tb2[:, 0:n]
                ins = VEC.tensor_scalar_mul(t_, a, c_[0:64, COL_I2PI:COL_I2PI + 1])
                self.se(VEC, ins)
                ins = VEC.tensor_copy(tbi[:, 0:n], t_)
                self.se(VEC, ins)
                ins = VEC.tensor_copy(t_, tbi[:, 0:n])
                self.se(VEC, ins)
                ins = VEC.scalar_tensor_tensor(a, t_, -CW1, a, ALU.mult, ALU.add)
                self.se(VEC, ins)
                ins = VEC.scalar_tensor_tensor(a, t_, -CW2, a, ALU.mult, ALU.add)
                self.se(VEC, ins)
                ins = VEC.tensor_scalar(t_, a, PI, None, ALU.is_gt)
                self.se(VEC, ins)
                ins = VEC.scalar_tensor_tensor(a, t_, -2 * PI, a, ALU.mult, ALU.add)
                self.se(VEC, ins)
                ins = VEC.tensor_scalar(t_, a, -PI, None, ALU.is_lt)
                self.se(VEC, ins)
                ins = VEC.scalar_tensor_tensor(a, t_, 2 * PI, a, ALU.mult, ALU.add)
                self.sVE.inc(ins)
                SCA.wait_ge(self.sVE.h, self.sVE.n)
                ins = SCA.activation(dst, a, AF.Sin)
                self.sSC.inc(ins)
                VEC.wait_ge(self.sSC.h, self.sSC.n)
                if post_scale is not None:
                    ins = VEC.tensor_scalar(dst, dst, post_scale, None, ALU.mult)
                    self.sVE.inc(ins)

            s = 1.0 / math.sqrt(D)
            sin_into(self.tabk[64:128, 0:4096], ang, 4096, False, s)
            sin_into(self.tabk[0:64, 0:4096], ang, 4096, True, s)
            sin_into(self.tabq[64:128, 0:2048], angq, 2048, False)
            sin_into(self.tabq[0:64, 0:2048], angq, 2048, True)
            ins = VEC.tensor_copy(tb1[0:1, 0:1], tb1[0:1, 0:1])
            self.sVE.inc(ins)
            for eng in (TEN, SCA, GPS, SYN, VEC):
                eng.wait_ge(self.sVE.h, self.sVE.n)
                eng.wait_ge(self.sGP.h, self.sGP.n)
                eng.wait_ge(self.sSC.h, self.sSC.n)

    def p1(self):
        nc, TEN, VEC, SCA, GPS, SYN = self.nc, self.TEN, self.VEC, self.SCA, self.GPS, self.SYN
        c_ = self.c_
        esP1o = ExitStack()
        kvhgT = self.sb(esP1o, "p1_kvhg", [64, 4096], F32)
        with ExitStack() as esP:
            xt = [self.sb(esP, f"p1_xt{i}", [128, E], F32) for i in range(3)]
            xnb = [self.sb(esP, f"p1_xnb{i}", [128, E], BF16) for i in range(2)]
            sqd = [self.sb(esP, "p1_sqd0", [128, E], F32)] * 2
            xnt_tmp = self.sb(esP, "p1_xnt_tmp", [128, 16 * 512], BF16)
            wkv_sb = self.sb(esP, "p1_wkv", [128, 16 * 256], BF16)
            wsm_sb = self.sb(esP, "p1_wsm", [128, 16 * 64], BF16)
            tmpv = self.sb(esP, "p1_tmpv", [128, 512], BF16)
            rp1 = self.sb(esP, "p1_rp1", [64, 512], F32)
            rp2 = self.sb(esP, "p1_rp2", [64, 512], F32)
            rp3 = self.sb(esP, "p1_rp3", [64, 512], F32)
            rp4 = self.sb(esP, "p1_rp4", [64, 512], F32)
            st1 = [self.sb(esP, f"p1_s1{i}", [128, 1], F32) for i in range(2)]
            st2 = [self.sb(esP, f"p1_s2{i}", [128, 1], F32) for i in range(2)]
            stm = [self.sb(esP, f"p1_sm{i}", [128, 1], F32) for i in range(2)]
            str_ = [self.sb(esP, f"p1_sr{i}", [128, 1], F32) for i in range(2)]
            pxp = [self.ps(esP, f"p1_pxp{i}", [128, 512], BF16) for i in range(2)]
            pkp = self.ps(esP, "p1_pkp", [128, 512], F32)
            pvp = self.ps(esP, "p1_pvp", [128, 512], F32)
            pkg = self.ps(esP, "p1_pkg", [64, 512], F32)
            pvt = self.ps(esP, "p1_pvt", [128, 512], BF16)

            ins = GPS.dma_start(out=wkv_sb[:, :], in_=self.wkv[:, :])
            self.sWL.inc(ins, 16)
            GPS.wait_ge(self.sWL.h, self.sWL.n)
            ins = GPS.dma_start(out=wsm_sb[:, :], in_=self.wsm[:, :])
            wl_mark = self.sWL.inc(ins, 16)
            TEN.wait_ge(self.sWL.h, wl_mark)

            m_xn = {}
            m_sq = {}
            m_ev = {}
            m_tb = {}
            m_var = {}
            m_kp = {}
            m_vp = {}
            m_kg = {}
            m_vt = {}
            m_grp = {}
            GB = 0
            for g in range(8):
                for tt in range(4):
                    i = 4 * g + tt
                    if i >= 3:
                        SYN.wait_ge(self.sVE.h, m_xn[i - 3])
                        SYN.wait_ge(self.sSC.h, m_sq[i - 3])
                    ins = SYN.dma_start(
                        out=xt[i % 3][:, :],
                        in_=self.xb[512 * g + 128 * tt: 512 * g + 128 * (tt + 1), :])
                    xl = self.sXLs[i % 3].inc(ins, 16)
                    SCA.wait_ge(self.sXLs[i % 3].h, xl)
                    if i >= 1:
                        SCA.wait_ge(self.sVE2.h, m_var[i - 1])
                    ins = SCA.activation(sqd[i % 2][:, :], xt[i % 3][:, :], AF.Square)
                    m_sq[i] = self.sSC.inc(ins)
                    VEC.wait_ge(self.sXLs[i % 3].h, xl)
                    ins = VEC.reduce_sum(st1[i % 2][:, :], xt[i % 3][:, :], AX.X)
                    self.se(VEC, ins)
                    VEC.wait_ge(self.sSC.h, m_sq[i])
                    ins = VEC.reduce_sum(st2[i % 2][:, :], sqd[i % 2][:, :], AX.X)
                    self.se(VEC, ins)
                    ins = VEC.tensor_scalar(stm[i % 2][:, :], st1[i % 2][:, :], 1.0 / E,
                                             None, ALU.mult)
                    self.se(VEC, ins)
                    ins = VEC.tensor_scalar(st2[i % 2][:, :], st2[i % 2][:, :], 1.0 / E,
                                            None, ALU.mult)
                    self.se(VEC, ins)
                    ins = VEC.tensor_tensor(st1[i % 2][:, :], stm[i % 2][:, :],
                                            stm[i % 2][:, :], ALU.mult)
                    self.se(VEC, ins)
                    ins = VEC.tensor_tensor(st2[i % 2][:, :], st2[i % 2][:, :],
                                            st1[i % 2][:, :], ALU.subtract)
                    m_var[i] = self.sVE2.inc(ins)
                    SCA.wait_ge(self.sVE2.h, m_var[i])
                    ins = SCA.activation(st1[i % 2][:, :], st2[i % 2][:, :], AF.Ln,
                                         bias=c_[:, COL_EPS:COL_EPS + 1])
                    self.se(SCA, ins)
                    ins = SCA.activation(str_[i % 2][:, :], st1[i % 2][:, :], AF.Exp,
                                         scale=c_[:, COL_NHALF:COL_NHALF + 1])
                    rstd_m = self.sSC.inc(ins)
                    VEC.wait_ge(self.sSC.h, rstd_m)
                    ins = VEC.tensor_scalar(xnb[i % 2][:, :], xt[i % 3][:, :],
                                            stm[i % 2][:, :], str_[i % 2][:, :],
                                            ALU.subtract, ALU.mult)
                    m_xn[i] = self.sVE.inc(ins)
                    TEN.wait_ge(self.sVE.h, m_xn[i])
                    for bch in range(4):
                        if GB >= 2:
                            TEN.wait_ge(self.sVE2.h, m_ev[GB - 2])
                        last = None
                        for k in range(4):
                            ec = 4 * bch + k
                            last = TEN.transpose(
                                pxp[GB % 2][:, 128 * k:128 * (k + 1)],
                                xnb[i % 2][:, 128 * ec:128 * (ec + 1)],
                                self.identb[:, :])
                        m_tb[GB] = self.sPE.inc(last)
                        VEC.wait_ge(self.sPE.h, m_tb[GB])
                        if g > 0:
                            VEC.wait_ge(self.sPE.h, m_kg[g - 1])
                        dst = xnt_tmp[:, :].rearrange("p (c r) -> p c r", c=16)[
                            :, 4 * bch:4 * bch + 4, 128 * tt:128 * (tt + 1)]
                        src = pxp[GB % 2][:, :].rearrange("p (c r) -> p c r", c=4)
                        ins = VEC.tensor_copy(dst, src)
                        m_ev[GB] = self.sVE2.inc(ins)
                        GB += 1
                TEN.wait_ge(self.sVE2.h, m_ev[GB - 1])
                if g > 0:
                    TEN.wait_ge(self.sG2.h, m_grp[g - 1])
                last = None
                for cc in range(16):
                    last = TEN.matmul(pkp[:, :], wkv_sb[:, 256 * cc:256 * cc + 128],
                                      xnt_tmp[:, 512 * cc:512 * (cc + 1)],
                                      start=(cc == 0), stop=(cc == 15))
                m_kp[g] = self.sPE.inc(last)
                for cc in range(16):
                    last = TEN.matmul(pvp[:, :], wkv_sb[:, 256 * cc + 128:256 * (cc + 1)],
                                      xnt_tmp[:, 512 * cc:512 * (cc + 1)],
                                      start=(cc == 0), stop=(cc == 15))
                m_vp[g] = self.sPE.inc(last)
                for cc in range(16):
                    last = TEN.matmul(pkg[:, :], wsm_sb[:, 64 * cc:64 * (cc + 1)],
                                      xnt_tmp[:, 512 * cc:512 * (cc + 1)],
                                      start=(cc == 0), stop=(cc == 15))
                m_kg[g] = self.sPE.inc(last)
                if g > 0:
                    TEN.wait_ge(self.sG2.h, m_grp[g - 1])
                    last = None
                    for k in range(4):
                        last = TEN.transpose(pvt[:, 128 * k:128 * (k + 1)],
                                             tmpv[:, 128 * k:128 * (k + 1)],
                                             self.identb[:, :])
                    m_vt[g - 1] = self.sPE.inc(last)
                cS = self.tabk[0:64, 512 * g:512 * (g + 1)]
                sS = self.tabk[64:128, 512 * g:512 * (g + 1)]
                VEC.wait_ge(self.sPE.h, m_kp[g])
                self.sSE.inc(VEC.tensor_tensor(rp1[:, :], pkp[0:64, :], cS, ALU.mult))
                self.sSE.inc(VEC.tensor_tensor(rp2[:, :], pkp[64:128, :], sS, ALU.mult))
                self.sSE.inc(VEC.tensor_tensor(rp3[:, :], pkp[64:128, :], cS, ALU.mult))
                self.sSE.inc(VEC.tensor_tensor(rp4[:, :], pkp[0:64, :], sS, ALU.mult))
                VEC.wait_ge(self.sSE.h, self.sSE.n)
                self.sSE.inc(VEC.tensor_tensor(self.ktT[0:64, 512 * g:512 * (g + 1)],
                                               rp1[:, :], rp2[:, :], ALU.subtract))
                self.sSE.inc(VEC.tensor_tensor(self.ktT[64:128, 512 * g:512 * (g + 1)],
                                               rp3[:, :], rp4[:, :], ALU.add))
                VEC.wait_ge(self.sSE.h, self.sSE.n)
                VEC.wait_ge(self.sPE.h, m_vp[g])
                if g > 0:
                    VEC.wait_ge(self.sPE.h, m_vt[g - 1])
                self.sG2.inc(VEC.tensor_copy(tmpv[:, :], pvp[:, :]))
                VEC.wait_ge(self.sPE.h, m_kg[g])
                self.sG2.inc(VEC.tensor_copy(kvhgT[:, 512 * g:512 * (g + 1)], pkg[:, :]))
                if g > 0:
                    self.sG2.inc(VEC.tensor_copy(self.vpn[:, 512 * (g - 1):512 * g],
                                                 pvt[:, :]))
                stH = self.snap()
                endsH = []
                for half in (0, 1):
                    self.restore(stH)
                    ctx = (VEC.If_eq(self.regs["v"], 0) if half == 0 else VEC.Else())
                    with ctx:
                        if g in OWN_GROUPS[half]:
                            loc = OWN_GROUPS[half].index(g) * 512
                            dst = self.xnT[:, :].rearrange("p (c r) -> p c r", c=16)[
                                :, :, loc:loc + 512]
                            src = xnt_tmp[:, :].rearrange("p (c r) -> p c r", c=16)
                            ins = VEC.tensor_copy(dst, src)
                        else:
                            ins = VEC.tensor_copy(rp1[0:1, 0:1], rp1[0:1, 0:1])
                        m_grp[g] = self.sG2.inc(ins)
                    endsH.append(self.snap())
                assert endsH[0] == endsH[1]
            TEN.wait_ge(self.sG2.h, m_grp[7])
            last = None
            for k in range(4):
                last = TEN.transpose(pvt[:, 128 * k:128 * (k + 1)],
                                     tmpv[:, 128 * k:128 * (k + 1)], self.identb[:, :])
            m_vt[7] = self.sPE.inc(last)
            VEC.wait_ge(self.sPE.h, m_vt[7])
            ins = VEC.tensor_copy(self.vpn[:, 512 * 7:512 * 8], pvt[:, :])
            self.sG2.inc(ins)

            for eng in (self.TEN, self.SCA, self.GPS, self.SYN):
                eng.wait_ge(self.sG2.h, self.sG2.n)
        with ExitStack() as esP:
            bufA = self.sb(esP, "p1_bA", [32, 4096], F32)
            bufB = self.sb(esP, "p1_bB", [32, 4096], F32)
            bufC = self.sb(esP, "p1_bC", [32, 4096], F32)
            self.S2 = self.sb(esP, "p1_S2", [32, 4096], F32)
            mkcol = self.sb(esP, "p1_mkc", [32, 1], F32)
            sqb = self.sb(esP, "p1_sqb", [128, 512], BF16)
            pkn = self.ps(esP, "p1_pkn", [1, 512], F32)
            pmt = self.ps(esP, "p1_pmt", [128, 128], F32)

            gT = kvhgT[32:64, :]
            kvT = kvhgT[0:32, :]
            SCA.wait_ge(self.sG2.h, self.sG2.n)
            ins = SCA.activation(bufA[:, :], gT, AF.Abs)
            self.se(SCA, ins)
            ins = SCA.activation(bufB[:, :], bufA[:, :], AF.Exp,
                                 scale=c_[0:32, COL_NEG1:COL_NEG1 + 1])
            self.se(SCA, ins)
            ins = SCA.activation(bufA[:, :], bufB[:, :], AF.Ln,
                                 bias=c_[0:32, COL_ONE:COL_ONE + 1])
            ls_m = self.sSC.inc(ins)
            VEC.wait_ge(self.sSC.h, ls_m)
            ins = VEC.tensor_scalar_min(bufC[:, :], gT, 0.0)
            self.se(VEC, ins)
            ins = VEC.tensor_tensor(bufB[:, :], bufC[:, :], bufA[:, :], ALU.subtract)
            self.se(VEC, ins)
            ins = VEC.tensor_tensor_scan(bufC[:, :], bufB[:, :], bufB[:, :], 0.0,
                                         ALU.add, ALU.bypass)
            self.se(VEC, ins)
            ins = VEC.tensor_tensor(bufA[:, :], bufC[:, :], bufB[:, :], ALU.subtract)
            lc_m = self.sVE.inc(ins)
            SCA.wait_ge(self.sVE.h, lc_m)
            ins = SCA.activation(bufC[:, :], bufA[:, :], AF.Exp)
            e_m = self.sSC.inc(ins)
            VEC.wait_ge(self.sSC.h, e_m)
            ins = VEC.tensor_tensor(bufB[:, :], kvT, bufC[:, :], ALU.mult)
            self.se(VEC, ins)
            ins = VEC.tensor_tensor_scan(self.S2[:, :], bufB[:, :], bufB[:, :], 0.0,
                                         ALU.add, ALU.bypass)
            self.se(VEC, ins)
            ins = VEC.reduce_max(mkcol[:, :], self.S2[:, :], AX.X,
                                 apply_absolute_value=True)
            s2_m = self.sVE.inc(ins)

            TEN.wait_ge(self.sVE.h, s2_m)
            for b4 in range(8):
                if b4 >= 1:
                    TEN.wait_ge(self.sVE2.h, self.sVE2.n)
                last = None
                for k in range(4):
                    mc = 4 * b4 + k
                    last = TEN.transpose(pmt[:, 32 * k:32 * (k + 1)],
                                         self.S2[:, 128 * mc:128 * (mc + 1)],
                                         self.identf[0:32, 0:32])
                pe_m = self.sPE.inc(last)
                VEC.wait_ge(self.sPE.h, pe_m)
                ins = VEC.tensor_copy(self.kvhn[:, 128 * b4:128 * (b4 + 1)], pmt[:, :])
                self.sVE2.inc(ins)
            TEN.wait_ge(self.sVE2.h, self.sVE2.n)
            last = TEN.transpose(pmt[0:1, 0:32],
                                 mkcol[:, :], self.identf[0:32, 0:32])
            pe_m = self.sPE.inc(last)
            VEC.wait_ge(self.sPE.h, pe_m)
            ins = VEC.tensor_copy(self.mkrow[:, :], pmt[0:1, 0:32])
            self.sVE2.inc(ins)
            for ch in range(8):
                SCA.wait_ge(self.sVE2.h, self.sVE2.n)
                SCA.wait_ge(self.sG2.h, self.sG2.n)
                ins = SCA.activation(sqb[:, :], self.ktT[:, 512 * ch:512 * (ch + 1)],
                                     AF.Square)
                sc_m = self.sSC.inc(ins)
                TEN.wait_ge(self.sSC.h, sc_m)
                if ch >= 1:
                    TEN.wait_ge(self.sVE2.h, self.sVE2.n)
                ins = TEN.matmul(pkn[:, :], self.onesb_c[:, :], sqb[:, :],
                                 start=True, stop=True)
                pe_m = self.sPE.inc(ins)
                VEC.wait_ge(self.sPE.h, pe_m)
                ins = VEC.reduce_max(self.knm[0:1, 1:2], pkn[:, :], AX.X)
                self.se(VEC, ins)
                ins = VEC.tensor_tensor(self.knm[0:1, 0:1], self.knm[0:1, 0:1],
                                        self.knm[0:1, 1:2], ALU.max)
                n = self.sVE2.inc(ins)
                VEC.wait_ge(self.sVE2.h, n)
            for eng in (TEN, SCA, GPS, SYN):
                eng.wait_ge(self.sVE2.h, self.sVE2.n)
                eng.wait_ge(self.sVE.h, self.sVE.n)
        esP1o.close()

    def p2(self):
        nc, TEN, VEC, SCA, GPS, SYN = self.nc, self.TEN, self.VEC, self.SCA, self.GPS, self.SYN
        with ExitStack() as esP:
            wqb = [self.sb(esP, f"p2_wq{i}", [128, E], BF16) for i in range(2)]
            qro = [self.sb(esP, f"p2_qr{i}", [128, 512], BF16) for i in range(2)]
            sqb = [self.sb(esP, f"p2_sq{i}", [128, 512], BF16) for i in range(2)]
            rp1 = self.sb(esP, "p2_rp1", [64, 512], F32)
            rp2 = self.sb(esP, "p2_rp2", [64, 512], F32)
            rp3 = self.sb(esP, "p2_rp3", [64, 512], F32)
            rp4 = self.sb(esP, "p2_rp4", [64, 512], F32)
            mx1 = self.sb(esP, "p2_mx1", [1, 4], F32)
            pq = [self.ps(esP, f"p2_pq{i}", [128, 512], F32) for i in range(2)]
            pq2 = self.ps(esP, "p2_pq2", [1, 512], F32)

            m_wl = {}
            m_mm = {}
            m_rope = {}
            m_sq = {}
            m_q2 = {}
            m_qs = {}
            m_mx = {}
            GPS.wait_ge(self.sPE.h, self.f_sdrfree)
            for och in range(16):
                if och >= 2:
                    GPS.wait_ge(self.sPE.h, m_mm[(och - 2, 3)])
                ins = GPS.dma_start(out=wqb[och % 2][:, :],
                                    in_=self.wq[128 * och:128 * (och + 1), :])
                m_wl[och] = self.sWQ[och % 2].inc(ins, 16)
                TEN.wait_ge(self.sWQ[och % 2].h, m_wl[och])
                for tg in range(4):
                    u = 4 * och + tg
                    if u >= 2:
                        TEN.wait_ge(self.sVE.h, m_rope[u - 2])
                    last = None
                    for cc in range(16):
                        last = TEN.matmul(pq[u % 2][:, :],
                                          wqb[och % 2][:, 128 * cc:128 * (cc + 1)],
                                          self.xnT[:, 2048 * cc + 512 * tg:
                                                   2048 * cc + 512 * (tg + 1)],
                                          start=(cc == 0), stop=(cc == 15))
                    m_mm[(och, tg)] = self.sPE.inc(last)
                    VEC.wait_ge(self.sPE.h, m_mm[(och, tg)])
                    if u >= 2:
                        VEC.wait_ge(self.sQSs[u % 2].h, m_qs[u - 2])
                        VEC.wait_ge(self.sSC.h, m_sq[u - 2])
                    cS = self.tabq[0:64, 512 * tg:512 * (tg + 1)]
                    sS = self.tabq[64:128, 512 * tg:512 * (tg + 1)]
                    self.sSE.inc(VEC.tensor_tensor(rp1[:, :], pq[u % 2][0:64, :], cS, ALU.mult))
                    self.sSE.inc(VEC.tensor_tensor(rp2[:, :], pq[u % 2][64:128, :], sS, ALU.mult))
                    self.sSE.inc(VEC.tensor_tensor(rp3[:, :], pq[u % 2][64:128, :], cS, ALU.mult))
                    self.sSE.inc(VEC.tensor_tensor(rp4[:, :], pq[u % 2][0:64, :], sS, ALU.mult))
                    VEC.wait_ge(self.sSE.h, self.sSE.n)
                    ins = VEC.tensor_tensor(qro[u % 2][0:64, :], rp1[:, :], rp2[:, :],
                                            ALU.subtract)
                    self.sVE.inc(ins)
                    ins = VEC.tensor_tensor(qro[u % 2][64:128, :], rp3[:, :], rp4[:, :],
                                            ALU.add)
                    m_rope[u] = self.sVE.inc(ins)
                    GPS.wait_ge(self.sVE.h, m_rope[u])
                    ins = GPS.dma_start(
                        out=self.q_scr[:, 2048 * och + 512 * tg:
                                       2048 * och + 512 * (tg + 1)],
                        in_=qro[u % 2][:, :])
                    m_qs[u] = self.sQSs[u % 2].inc(ins, 16)
                    SCA.wait_ge(self.sVE.h, m_rope[u])
                    if u >= 1:
                        SCA.wait_ge(self.sPE.h, m_q2.get(u - 1, 0))
                    ins = SCA.activation(sqb[u % 2][:, :], qro[u % 2][:, :], AF.Square)
                    m_sq[u] = self.sSC.inc(ins)
                    TEN.wait_ge(self.sSC.h, m_sq[u])
                    if u >= 1:
                        TEN.wait_ge(self.sVE2.h, m_mx[u - 1])
                    ins = TEN.matmul(pq2[:, :], self.onesb_c[:, :], sqb[u % 2][:, :],
                                     start=True, stop=True)
                    m_q2[u] = self.sPE.inc(ins)
                    VEC.wait_ge(self.sPE.h, m_q2[u])
                    ins = VEC.reduce_max(mx1[0:1, 0:1], pq2[:, :], AX.X)
                    self.se(VEC, ins)
                    ins = VEC.tensor_tensor(self.maxqrow[0:1, och:och + 1],
                                            self.maxqrow[0:1, och:och + 1],
                                            mx1[0:1, 0:1], ALU.max)
                    m_mx[u] = self.sVE2.inc(ins)
                    VEC.wait_ge(self.sVE2.h, m_mx[u])
                for j in range(4):
                    self._gu_unit(0, 4 * och + j, self.f_pg0, self.f_pu0)
            for eng in (TEN, SCA, GPS, SYN, VEC):
                eng.wait_ge(self.sVE2.h, self.sVE2.n)
                eng.wait_ge(self.sQSs[0].h, self.sQSs[0].n)
                eng.wait_ge(self.sQSs[1].h, self.sQSs[1].n)

    def p25a(self):
        nc, TEN, VEC, SCA, GPS, SYN = self.nc, self.TEN, self.VEC, self.SCA, self.GPS, self.SYN
        ins = SYN.dma_start(out=self.kt_scr[:, :], in_=self.ktT[:, :])
        self.sG2.inc(ins, 16)
        SYN.wait_ge(self.sG2.h, self.sG2.n)
        ins = SYN.dma_start(out=self.vp_scr[:, :], in_=self.vpn[:, :])
        self.sG2.inc(ins, 16)
        SYN.wait_ge(self.sG2.h, self.sG2.n)
        ins = SYN.dma_start(out=self.kn_scr[:, :], in_=self.kvhn[:, :])
        self.sG2.inc(ins, 16)
        SYN.wait_ge(self.sG2.h, self.sG2.n)
        for eng in (TEN, VEC, SCA, GPS):
            eng.wait_ge(self.sG2.h, self.sG2.n)
        self.esG.close()
        self.esTk.close()

        esF = ExitStack()
        self.esF = esF
        self.f_fuse = self.sb(esF, "p25_fuse", [128, 64 * 512], BF16)
        self.f_wgb = [self.sb(esF, f"p25_wg{i}", [128, E], BF16) for i in range(2)]
        self.f_wub = [self.sb(esF, f"p25_wu{i}", [128, E], BF16) for i in range(2)]
        self.f_sgb = [self.sb(esF, f"p25_sg{i}", [128, 512], F32) for i in range(2)]
        self.f_usb = [self.sb(esF, f"p25_us{i}", [128, 512], F32) for i in range(2)]
        self.f_ssg = self.sb(esF, "p25_ssg", [128, 64], F32)
        self.f_ssu = self.sb(esF, "p25_ssu", [128, 64], F32)
        self.f_sdb = self.sb(esF, "p25_sdb", [128, E], F32)

        ins = GPS.dma_start(out=self.f_ssg[:, :], in_=self.sgv[:, :])
        self.sWL.inc(ins, 16)
        GPS.wait_ge(self.sWL.h, self.sWL.n)
        ins = GPS.dma_start(out=self.f_ssu[:, :], in_=self.suv[:, :])
        self.sWL.inc(ins, 16)
        GPS.wait_ge(self.sWL.h, self.sWL.n)

        with ExitStack() as esQ:
            sdr = self.sb(esQ, "p25_sdr", [1, E], F32)
            psd = self.ps(esQ, "p25_psd", [128, 512], F32)
            ins = GPS.dma_start(out=sdr[:, :], in_=self.sdv[:, :])
            wl0 = self.sWL.inc(ins, 16)
            TEN.wait_ge(self.sWL.h, wl0)
            for i in range(4):
                if i > 0:
                    TEN.wait_ge(self.sVE.h, self.sVE.n)
                ins = TEN.matmul(psd[:, :], self.onesf_r[0:1, 0:128],
                                 sdr[0:1, 512 * i:512 * (i + 1)],
                                 start=True, stop=True)
                pe_m = self.sPE.inc(ins)
                VEC.wait_ge(self.sPE.h, pe_m)
                ins = VEC.tensor_copy(self.f_sdb[:, 512 * i:512 * (i + 1)], psd[:, :])
                self.sVE.inc(ins)
            TEN.wait_ge(self.sVE.h, self.sVE.n)
            self.sdb_done = self.sVE.n
            self.f_sdrfree = self.sPE.n
            SCA.wait_ge(self.sWL.h, wl0)

        self.ffb_ring = 0
        self.xsl_ring = 0
        self.xsl_done = {}
        self.sXO2 = self.sXO + [Sem(self.es.enter_context(self.nc.semaphore("sXO2c")))]
        self.fm_w = {}
        self.fm_gu = {}
        self.fm_sc = {}
        self.fm_fu = {}
        self.fm_dn = {}
        self.f_dn_done = {}
        self.f_t0marks = (self.sVE.n, self.sVE2.n, self.sSC.n)
        self.esPG = ExitStack()
        self.f_pg0 = [self.ps(self.esPG, f"p25_pg0_{i}", [128, 512], F32)
                      for i in range(2)]
        self.f_pu0 = [self.ps(self.esPG, f"p25_pu0_{i}", [128, 512], F32)
                      for i in range(2)]

    def _gu_unit(self, tb, ic, pg, pu):
        TEN, VEC, SCA, GPS = self.TEN, self.VEC, self.SCA, self.GPS
        wgb, wub, sgb, usb = self.f_wgb, self.f_wub, self.f_sgb, self.f_usb
        if ic == 0:
            if tb > 0:
                TEN.wait_ge(self.sVE2.h, self.sVE2.n)
            else:
                TEN.wait_ge(self.sVE.h, self.f_t0marks[0])
                TEN.wait_ge(self.sVE2.h, self.f_t0marks[1])
                TEN.wait_ge(self.sSC.h, self.f_t0marks[2])
        if ic >= 2:
            GPS.wait_ge(self.sPE.h, self.fm_gu[(tb, ic - 2)])
        sw = self.sWG[ic % 2]
        ins = GPS.dma_start(out=wgb[ic % 2][:, :],
                            in_=self.wgt[128 * ic:128 * (ic + 1), :])
        sw.inc(ins, 16)
        ins = GPS.dma_start(out=wub[ic % 2][:, :],
                            in_=self.wut[128 * ic:128 * (ic + 1), :])
        self.fm_w[(tb, ic)] = sw.inc(ins, 16)
        TEN.wait_ge(sw.h, self.fm_w[(tb, ic)])
        if ic >= 2:
            TEN.wait_ge(self.sSC.h, self.fm_sc[(tb, ic - 2)])
        last = None
        for cc in range(16):
            last = TEN.matmul(pg[ic % 2][:, :],
                              wgb[ic % 2][:, 128 * cc:128 * (cc + 1)],
                              self.xnT[:, 2048 * cc + 512 * tb:
                                       2048 * cc + 512 * (tb + 1)],
                              start=(cc == 0), stop=(cc == 15))
        for cc in range(16):
            last = TEN.matmul(pu[ic % 2][:, :],
                              wub[ic % 2][:, 128 * cc:128 * (cc + 1)],
                              self.xnT[:, 2048 * cc + 512 * tb:
                                       2048 * cc + 512 * (tb + 1)],
                              start=(cc == 0), stop=(cc == 15))
        self.fm_gu[(tb, ic)] = self.sPE.inc(last)
        SCA.wait_ge(self.sPE.h, self.fm_gu[(tb, ic)])
        if ic >= 2:
            SCA.wait_ge(self.sVE.h, self.fm_fu[(tb, ic - 2)])
        self.sSC.inc(SCA.activation(sgb[ic % 2][:, :], pg[ic % 2][:, :],
                                    AF.Silu, scale=self.f_ssg[:, ic:ic + 1]))
        ins = SCA.activation(usb[ic % 2][:, :], pu[ic % 2][:, :],
                             AF.Copy, scale=self.f_ssu[:, ic:ic + 1])
        self.fm_sc[(tb, ic)] = self.sSC.inc(ins)
        VEC.wait_ge(self.sSC.h, self.fm_sc[(tb, ic)])
        if ic == 0:
            VEC.wait_ge(self.sVE.h, self.sdb_done)
        if tb > 0 and ic == 0:
            VEC.wait_ge(self.sPE.h, self.f_dn_done[tb - 1])
        ins = VEC.tensor_tensor(self.f_fuse[:, 512 * ic:512 * (ic + 1)],
                                sgb[ic % 2][:, :], usb[ic % 2][:, :],
                                ALU.mult)
        self.fm_fu[(tb, ic)] = self.sVE.inc(ins)

    def _down(self, tb):
        TEN, VEC, GPS, SYN = self.TEN, self.VEC, self.GPS, self.SYN
        wdb, fuse, sdb = self.f_wdb, self.f_fuse, self.f_sdb
        xsl, ffb, tmpe = self.f_xsl, self.f_ffb, self.f_tmpe
        with ExitStack() as esQ:
            pd = [self.ps(esQ, f"p25_pd{tb}_{j}", [128, 512], F32) for j in range(8)]
            for p2 in range(2):
                for q4 in range(16):
                    par = q4 % 2
                    seq = (tb * 2 + p2) * 16 + q4
                    if seq >= 2:
                        GPS.wait_ge(self.sPE.h, self.fm_dn[seq - 2])
                    src_ap = bass.AP(self.wdn,
                                     (512 * q4) * E + 1024 * p2,
                                     [[E, 128], [128 * E, 4], [1, 1024]])
                    ins = GPS.dma_start(out=wdb[par][:, :], in_=src_ap)
                    wd_m = self.sWD[par].inc(ins, 16)
                    TEN.wait_ge(self.sWD[par].h, wd_m)
                    if q4 == 0:
                        TEN.wait_ge(self.sVE.h, self.fm_fu[(tb, 63)])
                        if (tb, p2) != (0, 0):
                            TEN.wait_ge(self.sVE2.h, self.sVE2.n)
                    last = None
                    for sub in range(4):
                        kc = 4 * q4 + sub
                        for tt in range(4):
                            for og2 in range(2):
                                last = TEN.matmul(
                                    pd[2 * tt + og2][:, :],
                                    fuse[:, 512 * kc + 128 * tt:
                                         512 * kc + 128 * (tt + 1)],
                                    wdb[par][:, 1024 * sub + 512 * og2:
                                             1024 * sub + 512 * (og2 + 1)],
                                    start=(kc == 0), stop=(kc == 63))
                    self.fm_dn[seq] = self.sPE.inc(last)
                pe_last = self.sPE.n
                VEC.wait_ge(self.sPE.h, pe_last)
                for k8 in range(8):
                    tt, og2 = divmod(k8, 2)
                    ri = self.xsl_ring
                    xslot = ri % 3
                    self.xsl_ring += 1
                    trow = 512 * tb + 128 * tt
                    tcol = 1024 * p2 + 512 * og2
                    if ri >= 3:
                        SYN.wait_ge(self.sVE2.h, self.xsl_done[ri - 3])
                    ins = SYN.dma_start(
                        out=xsl[xslot][:, :],
                        in_=self.x_own[trow:trow + 128,
                                       tcol:tcol + 512])
                    m_x = self.sXO2[xslot].inc(ins, 16)
                    slot = self.ffb_ring % 3
                    self.ffb_ring += 1
                    if self.sFB[slot].n > 0:
                        VEC.wait_ge(self.sFB[slot].h, self.sFB[slot].n)
                    VEC.wait_ge(self.sXO2[xslot].h, m_x)
                    ins = VEC.tensor_tensor(tmpe[k8 % 2][:, :],
                                            pd[2 * tt + og2][:, :],
                                            sdb[:, tcol:tcol + 512],
                                            ALU.mult)
                    self.se(VEC, ins)
                    ins = VEC.tensor_tensor(
                        ffb[slot][:, :], tmpe[k8 % 2][:, :],
                        xsl[xslot][:, :], ALU.add)
                    ve_m = self.sVE2.inc(ins)
                    self.xsl_done[ri] = ve_m
                    SYN.wait_ge(self.sVE2.h, ve_m)
                    ins = SYN.dma_start(
                        out=self.fp_scr[trow:trow + 128,
                                        tcol:tcol + 512],
                        in_=ffb[slot][:, :])
                    self.sFB[slot].inc(ins, 16)
                self.f_dn_done[tb] = pe_last

    def p25b(self):
        nc, TEN, VEC, SCA, GPS, SYN = self.nc, self.TEN, self.VEC, self.SCA, self.GPS, self.SYN
        with ExitStack() as esW:
            self.f_wdb = [self.sb(esW, f"p25_wd{i}", [128, 2 * E], BF16)
                          for i in range(2)]
            self.f_xsl = [self.sb(esW, f"p25_xs{i}", [128, 512], F32) for i in range(3)]
            self.f_ffb = [self.sb(esW, f"p25_ff{i}", [128, 512], F32) for i in range(3)]
            self.f_tmpe = [self.sb(esW, f"p25_te{i}", [128, 512], F32) for i in range(2)]
            self.esPG.close()
            self._down(0)
            for tb in range(1, 4):
                with ExitStack() as esQ:
                    pg = [self.ps(esQ, f"p25_pg{tb}_{i}", [128, 512], F32)
                          for i in range(2)]
                    pu = [self.ps(esQ, f"p25_pu{tb}_{i}", [128, 512], F32)
                          for i in range(2)]
                    for ic in range(64):
                        self._gu_unit(tb, ic, pg, pu)
                self._down(tb)
            if DBG_EXTRA:
                self.dbg_dump([("sdb", self.f_sdb[:, :], [128, E], 0),
                               ("ssg", self.f_ssg[:, :], [128, 64], 0)])
            for eng in (TEN, VEC, SCA, GPS, SYN):
                eng.wait_ge(self.sVE2.h, self.sVE2.n)
        self.esF.close()
        self.esTq.close()
        self.esA.close()

    def p3(self):
        nc, TEN, VEC, SCA, GPS, SYN = self.nc, self.TEN, self.VEC, self.SCA, self.GPS, self.SYN
        c_ = self.c_
        esB = ExitStack()
        self.esB = esB
        self.dmask = self.sb(esB, "b_dmask", [128, 4 * 512], BF16)
        self.OT = self.sb(esB, "b_OT", [128, H * 2048], BF16)
        self.wout_sb = self.sb(esB, "b_wout", [128, 16 * 2048], BF16)
        esB2 = ExitStack()
        self.esB2 = esB2
        ktT = self.sb(esB2, "b_ktT", [128, 4096], BF16)
        vpn = self.sb(esB2, "b_vpn", [128, 4096], BF16)
        kvhn = self.sb(esB2, "b_kvhn", [128, 1024], F32)
        qc = [self.sb(esB2, f"b_qc{i}", [128, 2048], BF16) for i in range(2)]
        ktp = [self.sb(esB2, f"b_ktp{i}", [128, 4096], BF16) for i in range(2)]
        vpp = [self.sb(esB2, "b_vpp0", [128, 4096], BF16)] * 2
        PT = [self.sb(esB2, f"b_pt{i}", [128, 512], BF16) for i in range(3)]
        khro = self.sb(esB2, "b_khro", [1, 4096], BF16)
        rbs = self.sb(esB2, "b_rbs", [128, 512], BF16)
        negB = [self.sb(esB2, f"b_nb{i}", [128, 1], F32) for i in range(2)]
        lr = self.sb(esB2, "b_lr", [1, 512], F32)
        rr = self.sb(esB2, "b_rr", [1, 512], F32)

        for v_ in range(4):
            ins = GPS.memset(self.dmask[:, 512 * v_:512 * (v_ + 1)], 1.0)
            self.se(GPS, ins)
            ins = GPS.affine_select(self.dmask[:, 512 * v_:512 * (v_ + 1)],
                                    self.dmask[:, 512 * v_:512 * (v_ + 1)],
                                    [[1, 512]], ALU.is_ge, 0.0,
                                    base=-128 * v_, channel_multiplier=-1)
            self.sGP.inc(ins)
        gp_mask_done = self.sGP.n
        for eng in (TEN, VEC, SCA):
            eng.wait_ge(self.sGP.h, gp_mask_done)
        ins = SYN.dma_start(out=ktT[:, :], in_=self.kt_scr[:, :])
        self.sG2.inc(ins, 16)
        SYN.wait_ge(self.sG2.h, self.sG2.n)
        ins = SYN.dma_start(out=vpn[:, :], in_=self.vp_scr[:, :])
        self.sG2.inc(ins, 16)
        SYN.wait_ge(self.sG2.h, self.sG2.n)
        ins = SYN.dma_start(out=kvhn[:, :], in_=self.kn_scr[:, :])
        g2m = self.sG2.inc(ins, 16)
        for b in range(16):
            if b > 0:
                GPS.wait_ge(self.sWL.h, self.sWL.n)
            ins = GPS.dma_start(out=self.wout_sb[:, 2048 * b:2048 * (b + 1)],
                                in_=self.wout[128 * b:128 * (b + 1), :])
            self.sWL.inc(ins, 16)
        self.wout_done = self.sWL.n
        GPS.wait_ge(self.sQSs[0].h, self.sQSs[0].n)
        GPS.wait_ge(self.sQSs[1].h, self.sQSs[1].n)
        ins = GPS.dma_start(out=qc[0][:, :], in_=self.q_scr[:, 0:2048])
        self.sQC[0].inc(ins, 16)
        m_ql = {0: (0, self.sQC[0].n)}

        with ExitStack() as esQ:
            pS = [self.ps(esQ, f"b_pS{i}", [128, 512], F32) for i in range(3)]
            pO = [self.ps(esQ, f"b_pO{i}", [128, 512], F32) for i in range(2)]
            pL = [self.ps(esQ, f"b_pL{i}", [1, 512], F32) for i in range(2)]
            pM0 = self.ps(esQ, "b_pM0", [128, 512], F32)
            pM = [pM0, pM0]

            for eng in (TEN, VEC, SCA):
                eng.wait_ge(self.sG2.h, g2m)

            head_end = {}
            for h in range(H):
                if h + 1 < H:
                    if h >= 1:
                        GPS.wait_ge(self.sPE.h, head_end[h - 1]["lastS"])
                    ins = GPS.dma_start(out=qc[(h + 1) % 2][:, :],
                                        in_=self.q_scr[:, 2048 * (h + 1):2048 * (h + 2)])
                    sl = (h + 1) % 2
                    m_ql[h + 1] = (sl, self.sQC[sl].inc(ins, 16))

                he2 = head_end.get(h - 2)
                kh_ev = []
                for b4 in range(8):
                    if b4 >= 1:
                        TEN.wait_ge(self.sVE.h, kh_ev[b4 - 1])
                    elif h > 0:
                        TEN.wait_ge(self.sVE2.h, head_end[h - 1]["sve2"])
                    last = None
                    for k in range(4):
                        mc = 4 * b4 + k
                        last = TEN.transpose(pM0[0:1, 128 * k:128 * (k + 1)],
                                             kvhn[:, 32 * mc + h:32 * mc + h + 1],
                                             self.identf[:, :])
                    pe_m = self.sPE.inc(last)
                    VEC.wait_ge(self.sPE.h, pe_m)
                    ins = VEC.tensor_copy(khro[0:1, 512 * b4:512 * (b4 + 1)],
                                          pM0[0:1, 0:512])
                    kh_ev.append(self.sVE.inc(ins))
                ins = VEC.tensor_tensor(self.b11[0:1, 0:1], self.mkrow[0:1, h:h + 1],
                                         self.mkrow[0:1, h:h + 1], ALU.mult)
                self.se(VEC, ins)
                ins = VEC.tensor_tensor(self.b11[0:1, 1:2], self.b11[0:1, 0:1],
                                        self.knm[0:1, 0:1], ALU.mult)
                self.se(VEC, ins)
                ins = VEC.tensor_tensor(self.b11[0:1, 2:3], self.b11[0:1, 1:2],
                                        self.maxqrow[0:1, h:h + 1], ALU.mult)
                ve_m = self.sVE.inc(ins)
                SCA.wait_ge(self.sVE.h, ve_m)
                ins = SCA.activation(self.b11[0:1, 3:4], self.b11[0:1, 2:3], AF.Ln)
                self.se(SCA, ins)
                ins = SCA.activation(self.b11[0:1, 4:5], self.b11[0:1, 3:4], AF.Exp,
                                     scale=c_[0:1, COL_HALFC:COL_HALFC + 1])
                sc_m = self.sSC.inc(ins)
                VEC.wait_ge(self.sSC.h, sc_m)
                ins = VEC.tensor_scalar(self.b11[0:1, 5:6], self.b11[0:1, 4:5], -1.0,
                                        None, ALU.mult)
                ve_m = self.sVE.inc(ins)
                TEN.wait_ge(self.sVE.h, ve_m)
                TEN.wait_ge(self.sVE.h, kh_ev[7])
                ins = TEN.matmul(pM0[:, 0:1], self.onesf_r[0:1, :],
                                 self.b11[0:1, 5:6], start=True, stop=True)
                pe_m = self.sPE.inc(ins)
                VEC.wait_ge(self.sPE.h, pe_m)
                ins = VEC.tensor_copy(negB[h % 2][:, :], pM0[:, 0:1])
                nb_ev = self.sVE.inc(ins)
                ktp_ev = []
                for c8 in range(8):
                    if c8 == 0:
                        TEN.wait_ge(self.sVE.h, nb_ev)
                    else:
                        TEN.wait_ge(self.sVE.h, ktp_ev[c8 - 1])
                    ins = TEN.matmul(pM0[:, :], self.onesb_r[0:1, :],
                                     khro[0:1, 512 * c8:512 * (c8 + 1)],
                                     start=True, stop=True)
                    pe_m = self.sPE.inc(ins)
                    VEC.wait_ge(self.sPE.h, pe_m)
                    if c8 == 0 and he2 is not None:
                        VEC.wait_ge(self.sPE.h, he2["lastS"])
                    ins = VEC.tensor_tensor(ktp[h % 2][:, 512 * c8:512 * (c8 + 1)],
                                            ktT[:, 512 * c8:512 * (c8 + 1)],
                                            pM0[:, :], ALU.mult)
                    ktp_ev.append(self.sVE.inc(ins))
                he1 = head_end.get(h - 1)
                for mc in range(32):
                    if mc == 0 and he1 is not None:
                        VEC.wait_ge(self.sPE.h, he1["lastAV"])
                    ins = VEC.tensor_scalar_mul(
                        vpp[h % 2][:, 128 * mc:128 * (mc + 1)],
                        vpn[:, 128 * mc:128 * (mc + 1)],
                        kvhn[:, 32 * mc + 16 + h:32 * mc + 16 + h + 1])
                    self.sVE.inc(ins)
                hprep = self.sVE.n

                TEN.wait_ge(self.sQC[m_ql[h][0]].h, m_ql[h][1])
                st0 = self.snap()
                bmarks = {}
                ends = []
                for half in (0, 1):
                    self.restore(st0)
                    pairs = pair_list(half)
                    NP = len(pairs)
                    assert NP == 72, NP
                    slot_last = {}
                    for pj, pr in enumerate(pairs):
                        if pr["last"]:
                            slot_last[pr["si"]] = pj

                    def exp_mark(pj, pairs=pairs, st0=st0):
                        return st0["sSC"] + pj + 1 + pairs[pj]["si"]

                    def rinv_mark(si, slot_last=slot_last, st0=st0):
                        return st0["sSC"] + slot_last[si] + 2 + si

                    def lv_mark(si, st0=st0):
                        return st0["sVE2"] + 2 * si + 1

                    def oev_mark(si, st0=st0):
                        return st0["sVE2"] + 2 * si + 2

                    gp_mark = {}
                    d = 0
                    for pj, pr in enumerate(pairs):
                        if pr["dv"] is not None:
                            d += 1
                            gp_mark[pj] = st0["sGP"] + d
                    mS = {}
                    mAV = {}
                    mBC = {}

                    def emit_lav(pj, pairs=pairs, gp_mark=gp_mark, mS=mS, mAV=mAV,
                                 mBC=mBC, exp_mark=exp_mark, rinv_mark=rinv_mark,
                                 oev_mark=oev_mark, ktp_ev=ktp_ev, h=h):
                        pr = pairs[pj]
                        si, mc = pr["si"], pr["mc"]
                        if pr["dv"] is not None:
                            TEN.wait_ge(self.sGP.h, gp_mark[pj])
                        else:
                            TEN.wait_ge(self.sSC.h, exp_mark(pj))
                        if pr["first"] and si >= 2:
                            TEN.wait_ge(self.sVE2.h, oev_mark(si - 2))
                        TEN.matmul(pL[si % 2][:, :], self.onesb_c[:, :],
                                   PT[pj % 3][:, :], start=pr["first"], stop=pr["last"])
                        ins = TEN.matmul(pO[si % 2][:, :],
                                         vpp[h % 2][:, 128 * mc:128 * (mc + 1)],
                                         PT[pj % 3][:, :],
                                         start=pr["first"], stop=pr["last"])
                        mAV[pj] = self.sPE.inc(ins)
                        if pr["last"]:
                            TEN.wait_ge(self.sSC.h, rinv_mark(si))
                            if si == 0:
                                TEN.wait_ge(self.sVE.h, ktp_ev[7])
                            else:
                                TEN.wait_ge(self.sVE2.h, oev_mark(si - 1))
                            ins = TEN.matmul(pM0[:, :], self.onesf_r[0:1, :],
                                             rr[0:1, :], start=True, stop=True)
                            mBC[si] = self.sPE.inc(ins)

                    with (TEN.If_eq(self.regs["t"], 0) if half == 0 else TEN.Else()):
                        TEN.wait_ge(self.sVE.h, hprep)
                        TEN.wait_ge(self.sSC.h, st0["sSC"])
                        TEN.wait_ge(self.sVE2.h, st0["sVE2"])
                        TEN.wait_ge(self.sGP.h, st0["sGP"])
                        for pj, pr in enumerate(pairs):
                            if pj >= 3:
                                TEN.wait_ge(self.sSC.h, exp_mark(pj - 3))
                            ins = TEN.matmul(pS[pj % 3][:, :],
                                             ktp[h % 2][:, 128 * pr["mc"]:
                                                        128 * (pr["mc"] + 1)],
                                             qc[h % 2][:, 512 * pr["tg"]:
                                                       512 * (pr["tg"] + 1)],
                                             start=True, stop=True)
                            mS[pj] = self.sPE.inc(ins)
                            if pj >= 2:
                                emit_lav(pj - 2)
                        emit_lav(NP - 2)
                        emit_lav(NP - 1)
                    bmarks[half] = dict(mS=mS, mAV=mAV, mBC=mBC, exp_mark=exp_mark,
                                        rinv_mark=rinv_mark, lv_mark=lv_mark,
                                        oev_mark=oev_mark, gp_mark=gp_mark,
                                        slot_last=slot_last, pairs=pairs)
                    ends.append(self.snap())
                assert ends[0] == ends[1], (h, "TEN", ends)

                st1 = self.snap()
                e2 = []
                for half in (0, 1):
                    self.restore(st1)
                    bm = bmarks[half]
                    with (SCA.If_eq(self.regs["s"], 0) if half == 0 else SCA.Else()):
                        SCA.wait_ge(self.sVE.h, nb_ev)
                        for pj, pr in enumerate(bm["pairs"]):
                            SCA.wait_ge(self.sPE.h, bm["mS"][pj])
                            if pj >= 3:
                                SCA.wait_ge(self.sPE.h, bm["mAV"][pj - 3])
                            ins = SCA.activation(PT[pj % 3][:, :], pS[pj % 3][:, :],
                                                 AF.Exp, bias=negB[h % 2][:, :])
                            self.sSC.inc(ins)
                            if pr["last"]:
                                SCA.wait_ge(self.sVE2.h, bm["lv_mark"](pr["si"]))
                                if pr["si"] >= 1:
                                    SCA.wait_ge(self.sPE.h, bm["mBC"][pr["si"] - 1])
                                SCA.activation(rr[:, :], lr[:, :], AF.Ln)
                                ins = SCA.activation(rr[:, :], rr[:, :], AF.Exp,
                                                     scale=c_[0:1, COL_NEG1:COL_NEG1 + 1])
                                self.sSC.inc(ins)
                    e2.append(self.snap())
                assert e2[0] == e2[1], (h, "SCA")

                st2 = self.snap()
                e3 = []
                for half in (0, 1):
                    self.restore(st2)
                    bm = bmarks[half]
                    with (GPS.If_eq(self.regs["g"], 0) if half == 0 else GPS.Else()):
                        for pj, pr in enumerate(bm["pairs"]):
                            if pr["dv"] is None:
                                continue
                            GPS.wait_ge(self.sSC.h, bm["exp_mark"](pj))
                            ins = GPS.affine_select(
                                PT[pj % 3][:, :], PT[pj % 3][:, :], [[1, 512]],
                                ALU.is_ge, 0.0, base=-128 * pr["dv"],
                                channel_multiplier=-1)
                            self.sGP.inc(ins)
                    e3.append(self.snap())
                assert e3[0] == e3[1], (h, "GPS")

                st3 = self.snap()
                e4 = []
                for half in (0, 1):
                    self.restore(st3)
                    bm = bmarks[half]
                    with (VEC.If_eq(self.regs["v"], 0) if half == 0 else VEC.Else()):
                        for pj, pr in enumerate(bm["pairs"]):
                            if not pr["last"]:
                                continue
                            si = pr["si"]
                            tg = SLOT_LNT[si]
                            VEC.wait_ge(self.sPE.h, bm["mAV"][pj])
                            if si >= 1:
                                VEC.wait_ge(self.sSC.h, bm["rinv_mark"](si - 1))
                            ins = VEC.tensor_scalar(lr[:, :], pL[si % 2][:, :],
                                                    1e-30, None, ALU.add)
                            self.sVE2.inc(ins)
                            VEC.wait_ge(self.sPE.h, bm["mBC"][si])
                            ins = VEC.tensor_copy(rbs[:, :], pM0[:, :])
                            self.se(VEC, ins)
                            ins = VEC.tensor_tensor(
                                self.OT[:, 2048 * h + 512 * tg:
                                        2048 * h + 512 * (tg + 1)],
                                pO[si % 2][:, :], rbs[:, :], ALU.mult)
                            self.sVE2.inc(ins)
                    e4.append(self.snap())
                assert e4[0] == e4[1], (h, "VEC")

                head_end[h] = dict(
                    lastS=max(bmarks[0]["mS"][71], bmarks[1]["mS"][71]),
                    lastAV=max(bmarks[0]["mAV"][71], bmarks[1]["mAV"][71]),
                    sve2=self.sVE2.n)

            for eng in (TEN, VEC, SCA, GPS, SYN):
                eng.wait_ge(self.sVE2.h, self.sVE2.n)
                eng.wait_ge(self.sSC.h, self.sSC.n)
        self.esB2.close()

    def p4(self):
        nc, TEN, VEC, SCA, GPS, SYN = self.nc, self.TEN, self.VEC, self.SCA, self.GPS, self.SYN
        with ExitStack() as esP:
            fpt = [self.sb(esP, f"p4_fp{i}", [128, E], F32) for i in range(3)]
            orow = [self.sb(esP, f"p4_or{i}", [128, E], F32) for i in range(2)]
            py = [self.ps(esP, f"p4_py{j}", [128, 512], F32) for j in range(8)]

            TEN.wait_ge(self.sWL.h, self.wout_done)
            for s_ in self.sFB:
                SYN.wait_ge(s_.h, s_.n)
            m_fl = {}
            m_y = {}
            m_ev = {}
            for t in range(16):
                if t >= 3:
                    SYN.wait_ge(self.sVE2.h, m_ev[t - 3])
                ins = SYN.dma_start(out=fpt[t % 3][:, :],
                                    in_=self.fp_scr[128 * t:128 * (t + 1), :])
                m_fl[t] = self.sFP[t % 3].inc(ins, 16)
                par = t % 2
                if t >= 2:
                    TEN.wait_ge(self.sVE2.h, m_ev[t - 2])
                last = None
                for hh in range(16):
                    for oc in range(4):
                        last = TEN.matmul(
                            py[4 * par + oc][:, :],
                            self.OT[:, 2048 * hh + 128 * t:2048 * hh + 128 * (t + 1)],
                            self.wout_sb[:, 2048 * hh + 512 * oc:
                                         2048 * hh + 512 * (oc + 1)],
                            start=(hh == 0), stop=(hh == 15))
                m_y[t] = self.sPE.inc(last)
                VEC.wait_ge(self.sPE.h, m_y[t])
                VEC.wait_ge(self.sFP[t % 3].h, m_fl[t])
                if t >= 2:
                    VEC.wait_ge(self.sOU[t % 2].h, self.sOU[t % 2].n)
                for oc in range(4):
                    ins = VEC.tensor_tensor(orow[par][:, 512 * oc:512 * (oc + 1)],
                                            py[4 * par + oc][:, :],
                                            fpt[t % 3][:, 512 * oc:512 * (oc + 1)],
                                            ALU.add)
                m_ev[t] = self.sVE2.inc(ins)
                SYN.wait_ge(self.sVE2.h, m_ev[t])
                ins = SYN.dma_start(out=self.out[128 * t:128 * (t + 1), :],
                                    in_=orow[par][:, :])
                self.sOU[par].inc(ins, 16)
            SYN.wait_ge(self.sOU[0].h, self.sOU[0].n)
            SYN.wait_ge(self.sOU[1].h, self.sOU[1].n)
        self.esB.close()

    def dbg_dump(self, tens):
        nc, SYN, GPS = self.nc, self.SYN, self.GPS
        with ExitStack() as esD:
            for i, (name, t, shape, via_sbuf) in enumerate(tens):
                o = nc.dram_tensor("dbg_" + name, shape, t.dtype if hasattr(t, "dtype")
                                   else F32, kind="ExternalOutput")
                if via_sbuf:
                    tmp = self.sb(esD, f"dbg_t{i}", [128, shape[1]], t.dtype)
                    for rr_ in range(0, shape[0], 128):
                        ins = SYN.dma_start(out=tmp[:, :], in_=t[rr_:rr_ + 128, :])
                        n = self.sOUT.inc(ins, 16)
                        SYN.wait_ge(self.sOUT.h, n)
                        ins = SYN.dma_start(out=o[rr_:rr_ + 128, :], in_=tmp[:, :])
                        n = self.sOUT.inc(ins, 16)
                        SYN.wait_ge(self.sOUT.h, n)
                else:
                    ins = SYN.dma_start(out=o[:, :], in_=t)
                    n = self.sOUT.inc(ins, 16)
                    SYN.wait_ge(self.sOUT.h, n)
            SYN.wait_ge(self.sOUT.h, self.sOUT.n)

    def build(self, stop=4):
        self.p0()
        self.p1()
        if stop == 1:
            self.dbg_dump([
                ("ktT", self.ktT[:, :], [128, 4096], 0),
                ("vpn", self.vpn[:, :], [128, 4096], 0),
                ("kvhn", self.kvhn[:, :], [128, 1024], 0),
                ("mkrow", self.mkrow[:, :], [1, 32], 0),
                ("knm", self.knm[:, :], [1, 4], 0),
                ("xnT", self.xnT[:, :], [128, 16 * 2048], 0),
            ])
            self.esG.close(); self.esTk.close(); self.esTq.close()
            self.esA.close(); self.es.close()
            return self.nc
        self.p25a()
        self.p2()
        if stop == 2:
            self.dbg_dump([
                ("qscr", self.q_scr[:, :], [128, H * 2048], 0),
                ("maxq", self.maxqrow[:, :], [1, 16], 0),
            ])
            self.esPG.close(); self.esF.close(); self.esTq.close()
            self.esA.close(); self.es.close()
            return self.nc
        self.p25b()
        if stop == 25:
            self.dbg_dump([("fps", self.fp_scr, [2048, E], 1)])
            self.es.close()
            return self.nc
        self.p3()
        if stop == 3:
            self.dbg_dump([("OT", self.OT[:, :], [128, H * 2048], 0)])
            self.esB.close()
            self.es.close()
            return self.nc
        if DBG_EXTRA:
            for s_ in self.sFB:
                self.SYN.wait_ge(s_.h, s_.n)
            self.dbg_dump([("fps", self.fp_scr, [2048, E], 1)])
        self.p4()
        self.es.close()
        return self.nc


DBG_EXTRA = False
_NC_CACHE = {}


def get_nc(stop=4):
    if stop not in _NC_CACHE:
        _NC_CACHE[stop] = KB().build(stop=stop)
    return _NC_CACHE[stop]


def kernel(L0_attn_x, L0_attn_Wq, L0_attn_Wkv, L0_attn_Wkvh, L0_attn_Wg,
           L0_attn_Wout, L0_ffn_w_gate, L0_ffn_w_up, L0_ffn_w_down,
           L0_ffn_s_gate, L0_ffn_s_up, L0_ffn_s_down):
    x = np.ascontiguousarray(np.asarray(L0_attn_x, dtype=np.float32))
    WqT = np.asarray(L0_attn_Wq, np.float32).T
    wq_h = np.ascontiguousarray(
        WqT.reshape(16, 128, 16, 128).transpose(2, 1, 0, 3).reshape(E, E))
    wkv_h = np.ascontiguousarray(
        np.asarray(L0_attn_Wkv, np.float32).T.reshape(16, 128, 256)
        .transpose(1, 0, 2).reshape(128, 16 * 256))
    wsm_full = np.concatenate([np.asarray(L0_attn_Wkvh, np.float32).T,
                               np.asarray(L0_attn_Wg, np.float32).T], axis=1)
    wsm_h = np.ascontiguousarray(
        wsm_full.reshape(16, 128, 64).transpose(1, 0, 2).reshape(128, 16 * 64))
    wout_h = np.ascontiguousarray(np.asarray(L0_attn_Wout, np.float32).T)
    wgt_h = np.ascontiguousarray(
        np.asarray(L0_ffn_w_gate, np.int8).reshape(16, 128, 64, 128)
        .transpose(2, 1, 0, 3).reshape(INTER, E))
    wut_h = np.ascontiguousarray(
        np.asarray(L0_ffn_w_up, np.int8).reshape(16, 128, 64, 128)
        .transpose(2, 1, 0, 3).reshape(INTER, E))
    wdn_h = np.ascontiguousarray(np.asarray(L0_ffn_w_down, np.int8))
    sg_h = np.ascontiguousarray(np.asarray(L0_ffn_s_gate, np.float32).reshape(64, 128).T)
    su_h = np.ascontiguousarray(np.asarray(L0_ffn_s_up, np.float32).reshape(64, 128).T)
    sd_h = np.ascontiguousarray(np.asarray(L0_ffn_s_down, np.float32).reshape(1, E))

    nc = get_nc()
    in_maps = []
    rowmaps = []
    for c in range(8):
        b, half = divmod(c, 2)
        rows = (np.r_[0:1024, 3072:4096] if half == 0 else np.r_[1024:3072])
        rowmaps.append(rows)
        in_maps.append({
            "xb": np.ascontiguousarray(x[b]),
            "x_own": np.ascontiguousarray(x[b][rows]),
            "pos_own": np.ascontiguousarray(rows.astype(np.float32).reshape(1, 2048)),
            "cfg": np.array([[half, 0, 0, 0]], np.int32),
            "wq": wq_h, "wkv": wkv_h, "wsm": wsm_h, "wout": wout_h,
            "wgt": wgt_h, "wut": wut_h, "wdn": wdn_h,
            "sgv": sg_h, "suv": su_h, "sdv": sd_h,
        })
    res = run_bass_kernel_spmd(nc, in_maps, core_ids=list(range(8)))
    outp = np.empty((4, NSEQ, E), np.float32)
    for c in range(8):
        b = c // 2
        outp[b][rowmaps[c]] = res.results[c]["out"]
    return outp


# version 10
# speedup vs baseline: 1.1327x; 1.0735x over previous
import math
import numpy as np
from contextlib import ExitStack

import concourse.bass as bass
import concourse.mybir as mybir
from concourse.bass_utils import run_bass_kernel_spmd

F32 = mybir.dt.float32
BF16 = mybir.dt.bfloat16
I8 = mybir.dt.int8
I32 = mybir.dt.int32
AF = mybir.ActivationFunctionType
ALU = mybir.AluOpType
AX = mybir.AxisListType

PI = float(np.pi)
CW1 = 6.28125
CW2 = 2 * PI - CW1
E = 2048
H = 16
D = 128
NSEQ = 4096
INTER = 8192
TGABS = {0: [0, 512, 3072, 3584], 1: [1024, 1536, 2048, 2560]}
SLOT_LNT = [3, 2, 1, 0]
OWN_GROUPS = {0: [0, 1, 6, 7], 1: [2, 3, 4, 5]}
COL_EPS, COL_NHALF, COL_ONE, COL_NEG1, COL_LN1E4, COL_HALFC, COL_I2PI = range(7)


class Sem:
    def __init__(self, h):
        self.h = h
        self.n = 0

    def inc(self, ins, k=1):
        ins.then_inc(self.h, k)
        self.n += k
        return self.n


def pair_list(half):
    pairs = []
    for si, tg in enumerate(SLOT_LNT):
        a0 = TGABS[half][tg]
        mcN = (a0 + 512) // 128
        for mc in range(mcN):
            dv = mc - a0 // 128
            pairs.append(dict(si=si, tg=tg, mc=mc,
                              dv=dv if 0 <= dv <= 3 else None,
                              first=(mc == 0), last=(mc == mcN - 1)))
    return pairs


class KB:
    def __init__(self):
        nc = bass.Bass("TRN2", target_bir_lowering=False, debug=False)
        self.nc = nc
        self.TEN, self.VEC, self.SCA, self.GPS, self.SYN = \
            nc.tensor, nc.vector, nc.scalar, nc.gpsimd, nc.sync
        dt = nc.dram_tensor
        self.xb = dt("xb", [NSEQ, E], F32, kind="ExternalInput")
        self.x_own = dt("x_own", [2048, E], F32, kind="ExternalInput")
        self.pos_own = dt("pos_own", [1, 2048], F32, kind="ExternalInput")
        self.cfg = dt("cfg", [1, 4], I32, kind="ExternalInput")
        self.wq = dt("wq", [E, E], F32, kind="ExternalInput")
        self.wkv = dt("wkv", [128, 16 * 256], F32, kind="ExternalInput")
        self.wsm = dt("wsm", [128, 16 * 64], F32, kind="ExternalInput")
        self.wout = dt("wout", [E, E], F32, kind="ExternalInput")
        self.wgt = dt("wgt", [INTER, E], I8, kind="ExternalInput")
        self.wut = dt("wut", [INTER, E], I8, kind="ExternalInput")
        self.wdn = dt("wdn", [INTER, E], I8, kind="ExternalInput")
        self.sgv = dt("sgv", [128, 64], F32, kind="ExternalInput")
        self.suv = dt("suv", [128, 64], F32, kind="ExternalInput")
        self.sdv = dt("sdv", [1, E], F32, kind="ExternalInput")
        self.out = dt("out", [2048, E], F32, kind="ExternalOutput")
        self.q_scr = dt("q_scr", [128, H * 2048], BF16, kind="Internal")
        self.fp_scr = dt("fp_scr", [2048, E], F32, kind="Internal")
        self.kt_scr = dt("kt_scr", [128, 4096], BF16, kind="Internal")
        self.vp_scr = dt("vp_scr", [128, 4096], BF16, kind="Internal")
        self.kn_scr = dt("kn_scr", [128, 1024], F32, kind="Internal")

        self.es = ExitStack()
        self.sems = {k: Sem(self.es.enter_context(nc.semaphore(k))) for k in
                     ["sXL", "sWL", "sQS", "sQL", "sFS", "sFL", "sOUT", "sG2",
                      "sPE", "sSC", "sVE", "sVE2", "sGP", "sMS", "sSE", "sSEs"]}
        self.sSE = self.sems["sSE"]
        def mksems(prefix, k):
            return [Sem(self.es.enter_context(nc.semaphore(f"{prefix}{i}")))
                    for i in range(k)]
        self.sXLs = mksems("sXLs", 3)
        self.sWQ = mksems("sWQ", 2)
        self.sQSs = mksems("sQSs", 2)
        self.sWG = mksems("sWG", 2)
        self.sWD = mksems("sWD", 2)
        self.sXO = mksems("sXO", 2)
        self.sFB = mksems("sFB", 4)
        self.sQC = mksems("sQC", 2)
        self.sFP = mksems("sFP", 3)
        self.sOU = mksems("sOU", 2)
        self.sSEg = Sem(self.es.enter_context(nc.semaphore("sSEg")))
        for k, v in self.sems.items():
            setattr(self, k, v)

    def sb(self, st, name, shape, dtt):
        return st.enter_context(self.nc.sbuf_tensor(name, shape, dtt))

    def ps(self, st, name, shape, dtt):
        return st.enter_context(self.nc.psum_tensor(name, shape, dtt))

    def se(self, eng, ins):
        s = (self.sSEg if eng is self.GPS
             else self.sems["sSEs"] if eng is self.SCA else self.sSE)
        n = s.inc(ins)
        eng.wait_ge(s.h, n)

    def snap(self):
        return {k: s.n for k, s in self.sems.items()}

    def restore(self, st):
        for k, s in self.sems.items():
            s.n = st[k]

    def p0(self):
        nc, TEN, VEC, SCA, GPS, SYN = self.nc, self.TEN, self.VEC, self.SCA, self.GPS, self.SYN
        es = self.es
        self.c_ = self.sb(es, "c_", [128, 8], F32)
        self.onesb_c = self.sb(es, "onesb_c", [128, 1], BF16)
        self.onesb_r = self.sb(es, "onesb_r", [1, 128], BF16)
        self.onesf_r = self.sb(es, "onesf_r", [1, 128], F32)
        self.identb = self.sb(es, "identb", [128, 128], BF16)
        self.identf = self.sb(es, "identf", [128, 128], F32)
        self.cfgs = self.sb(es, "cfgs", [1, 4], I32)
        self.maxqrow = self.sb(es, "maxqrow", [1, 16], F32)
        self.mkrow = self.sb(es, "mkrow", [1, 32], F32)
        self.knm = self.sb(es, "knm", [1, 4], F32)
        self.b11 = self.sb(es, "b11", [1, 8], F32)
        c_ = self.c_

        ins = SYN.dma_start(out=self.cfgs[:, :], in_=self.cfg[:, :])
        self.sMS.inc(ins, 16)
        for col, val in [(COL_EPS, 1e-5), (COL_NHALF, -0.5), (COL_ONE, 1.0),
                         (COL_NEG1, -1.0), (COL_LN1E4, -math.log(1e4) / 64.0),
                         (COL_HALFC, 0.5), (COL_I2PI, 1.0 / (2 * PI))]:
            self.sGP.inc(GPS.memset(c_[:, col:col + 1], val))
        self.sGP.inc(GPS.memset(self.onesb_c[:, :], 1.0))
        self.sGP.inc(GPS.memset(self.onesb_r[:, :], 1.0))
        self.sGP.inc(GPS.memset(self.onesf_r[:, :], 1.0))
        self.sGP.inc(GPS.memset(self.maxqrow[:, :], 0.0))
        self.sGP.inc(GPS.memset(self.knm[:, :], 0.0))
        ins = GPS.memset(self.identf[:, :], 1.0)
        self.se(GPS, ins)
        ins = GPS.affine_select(self.identf[:, :], self.identf[:, :], [[1, 128]],
                                ALU.is_equal, 0.0, base=0, channel_multiplier=-1)
        self.sGP.inc(ins)
        ins = GPS.memset(self.identb[:, :], 1.0)
        self.se(GPS, ins)
        ins = GPS.affine_select(self.identb[:, :], self.identb[:, :], [[1, 128]],
                                ALU.is_equal, 0.0, base=0, channel_multiplier=-1)
        self.sGP.inc(ins)

        self.regs = {}
        for nm, e in [("t", TEN), ("v", VEC), ("s", SCA), ("g", GPS)]:
            e.wait_ge(self.sMS.h, 16)
            r = e.alloc_register("rhalf_" + nm)
            e.reg_load(r, self.cfgs[0:1, 0:1])
            self.regs[nm] = r

        self.esA = ExitStack()
        self.xnT = self.sb(self.esA, "xnT", [128, 16 * 2048], BF16)
        self.esTq = ExitStack()
        self.tabq = self.sb(self.esTq, "tabq", [128, 2048], F32)
        self.esTk = ExitStack()
        self.tabk = self.sb(self.esTk, "tabk", [128, 4096], F32)
        self.esG = ExitStack()
        self.ktT = self.sb(self.esG, "ktT", [128, 4096], BF16)
        self.vpn = self.sb(self.esG, "vpn", [128, 4096], BF16)
        self.kvhn = self.sb(self.esG, "kvhn", [128, 1024], F32)

        with ExitStack() as esP:
            ang = self.sb(esP, "p0_ang", [64, 4096], F32)
            angq = self.sb(esP, "p0_angq", [64, 2048], F32)
            tb1 = self.sb(esP, "p0_tb1", [64, 4096], F32)
            tb2 = self.sb(esP, "p0_tb2", [64, 4096], F32)
            tbi = self.sb(esP, "p0_tbi", [64, 4096], I32)
            invj = self.sb(esP, "p0_inv", [64, 1], F32)
            jcol = self.sb(esP, "p0_jcol", [64, 1], F32)
            posq = self.sb(esP, "p0_posq", [1, 2048], F32)
            pp0 = self.ps(esP, "pp0", [64, 512], F32)

            SYN.wait_ge(self.sMS.h, self.sMS.n)
            ins = SYN.dma_start(out=posq[:, :], in_=self.pos_own[:, :])
            self.sMS.inc(ins, 16)
            TEN.wait_ge(self.sGP.h, self.sGP.n)
            VEC.wait_ge(self.sGP.h, self.sGP.n)
            self.sGP.inc(GPS.iota(jcol[:, :], [[1, 1]], base=0, channel_multiplier=1,
                                   allow_small_or_imprecise_dtypes=True))
            self.sGP.inc(GPS.iota(ang[:, :], [[1, 4096]], base=0, channel_multiplier=0,
                                  allow_small_or_imprecise_dtypes=True))
            self.sGP.inc(GPS.memset(self.ktT[:, :], 0.0))
            self.sGP.inc(GPS.memset(self.vpn[:, :], 0.0))
            self.sGP.inc(GPS.memset(self.kvhn[:, :], 0.0))

            SCA.wait_ge(self.sGP.h, self.sGP.n)
            TEN.wait_ge(self.sGP.h, self.sGP.n)
            VEC.wait_ge(self.sGP.h, self.sGP.n)
            ins = SCA.activation(invj[:, :], jcol[:, :], AF.Exp,
                                 scale=c_[0:64, COL_LN1E4:COL_LN1E4 + 1])
            self.sSC.inc(ins)
            VEC.wait_ge(self.sSC.h, self.sSC.n)
            ins = VEC.tensor_scalar_mul(ang[:, :], ang[:, :], invj[:, :])
            self.sVE.inc(ins)
            VEC.wait_ge(self.sVE.h, self.sVE.n)
            for i in range(4):
                TEN.wait_ge(self.sMS.h, self.sMS.n)
                if i > 0:
                    TEN.wait_ge(self.sVE.h, self.sVE.n)
                ins = TEN.matmul(pp0[:, :], self.onesf_r[0:1, 0:64],
                                 posq[0:1, 512 * i:512 * (i + 1)], start=True, stop=True)
                self.sPE.inc(ins)
                VEC.wait_ge(self.sPE.h, self.sPE.n)
                ins = VEC.tensor_copy(angq[:, 512 * i:512 * (i + 1)], pp0[:, :])
                self.sVE.inc(ins)
                VEC.wait_ge(self.sVE.h, self.sVE.n)
            ins = VEC.tensor_scalar_mul(angq[:, :], angq[:, :], invj[:, :])
            self.se(VEC, ins)

            def sin_into(dst, src, n, add_half_pi, post_scale=None):
                a = tb1[:, 0:n]
                if add_half_pi:
                    ins = VEC.tensor_scalar(a, src[:, 0:n], PI / 2, None, ALU.add)
                else:
                    ins = VEC.tensor_copy(a, src[:, 0:n])
                self.se(VEC, ins)
                t_ = tb2[:, 0:n]
                ins = VEC.tensor_scalar_mul(t_, a, c_[0:64, COL_I2PI:COL_I2PI + 1])
                self.se(VEC, ins)
                ins = VEC.tensor_copy(tbi[:, 0:n], t_)
                self.se(VEC, ins)
                ins = VEC.tensor_copy(t_, tbi[:, 0:n])
                self.se(VEC, ins)
                ins = VEC.scalar_tensor_tensor(a, t_, -CW1, a, ALU.mult, ALU.add)
                self.se(VEC, ins)
                ins = VEC.scalar_tensor_tensor(a, t_, -CW2, a, ALU.mult, ALU.add)
                self.se(VEC, ins)
                ins = VEC.tensor_scalar(t_, a, PI, None, ALU.is_gt)
                self.se(VEC, ins)
                ins = VEC.scalar_tensor_tensor(a, t_, -2 * PI, a, ALU.mult, ALU.add)
                self.se(VEC, ins)
                ins = VEC.tensor_scalar(t_, a, -PI, None, ALU.is_lt)
                self.se(VEC, ins)
                ins = VEC.scalar_tensor_tensor(a, t_, 2 * PI, a, ALU.mult, ALU.add)
                self.sVE.inc(ins)
                SCA.wait_ge(self.sVE.h, self.sVE.n)
                ins = SCA.activation(dst, a, AF.Sin)
                self.sSC.inc(ins)
                VEC.wait_ge(self.sSC.h, self.sSC.n)
                if post_scale is not None:
                    ins = VEC.tensor_scalar(dst, dst, post_scale, None, ALU.mult)
                    self.sVE.inc(ins)

            s = 1.0 / math.sqrt(D)
            sin_into(self.tabk[64:128, 0:4096], ang, 4096, False, s)
            sin_into(self.tabk[0:64, 0:4096], ang, 4096, True, s)
            sin_into(self.tabq[64:128, 0:2048], angq, 2048, False)
            sin_into(self.tabq[0:64, 0:2048], angq, 2048, True)
            ins = VEC.tensor_copy(tb1[0:1, 0:1], tb1[0:1, 0:1])
            self.sVE.inc(ins)
            for eng in (TEN, SCA, GPS, SYN, VEC):
                eng.wait_ge(self.sVE.h, self.sVE.n)
                eng.wait_ge(self.sGP.h, self.sGP.n)
                eng.wait_ge(self.sSC.h, self.sSC.n)

    def p1(self):
        nc, TEN, VEC, SCA, GPS, SYN = self.nc, self.TEN, self.VEC, self.SCA, self.GPS, self.SYN
        c_ = self.c_
        esP1o = ExitStack()
        kvhgT = self.sb(esP1o, "p1_kvhg", [64, 4096], F32)
        with ExitStack() as esP:
            xt = [self.sb(esP, f"p1_xt{i}", [128, E], F32) for i in range(3)]
            xnb = [self.sb(esP, f"p1_xnb{i}", [128, E], BF16) for i in range(2)]
            sqd = [self.sb(esP, "p1_sqd0", [128, E], F32)] * 2
            xnt_tmp = self.sb(esP, "p1_xnt_tmp", [128, 16 * 512], BF16)
            wkv_sb = self.sb(esP, "p1_wkv", [128, 16 * 256], BF16)
            wsm_sb = self.sb(esP, "p1_wsm", [128, 16 * 64], BF16)
            tmpv = self.sb(esP, "p1_tmpv", [128, 512], BF16)
            rp1 = self.sb(esP, "p1_rp1", [64, 512], F32)
            rp2 = self.sb(esP, "p1_rp2", [64, 512], F32)
            rp3 = self.sb(esP, "p1_rp3", [64, 512], F32)
            rp4 = self.sb(esP, "p1_rp4", [64, 512], F32)
            st1 = [self.sb(esP, f"p1_s1{i}", [128, 1], F32) for i in range(2)]
            st2 = [self.sb(esP, f"p1_s2{i}", [128, 1], F32) for i in range(2)]
            stm = [self.sb(esP, f"p1_sm{i}", [128, 1], F32) for i in range(2)]
            str_ = [self.sb(esP, f"p1_sr{i}", [128, 1], F32) for i in range(2)]
            pxp = [self.ps(esP, f"p1_pxp{i}", [128, 512], BF16) for i in range(2)]
            pkp = self.ps(esP, "p1_pkp", [128, 512], F32)
            pvp = self.ps(esP, "p1_pvp", [128, 512], F32)
            pkg = self.ps(esP, "p1_pkg", [64, 512], F32)
            pvt = self.ps(esP, "p1_pvt", [128, 512], BF16)

            ins = GPS.dma_start(out=wkv_sb[:, :], in_=self.wkv[:, :])
            self.sWL.inc(ins, 16)
            GPS.wait_ge(self.sWL.h, self.sWL.n)
            ins = GPS.dma_start(out=wsm_sb[:, :], in_=self.wsm[:, :])
            wl_mark = self.sWL.inc(ins, 16)
            TEN.wait_ge(self.sWL.h, wl_mark)

            m_xn = {}
            m_sq = {}
            m_ev = {}
            m_tb = {}
            m_var = {}
            m_kp = {}
            m_vp = {}
            m_kg = {}
            m_vt = {}
            m_grp = {}
            GB = 0
            for g in range(8):
                for tt in range(4):
                    i = 4 * g + tt
                    if i >= 3:
                        SYN.wait_ge(self.sVE.h, m_xn[i - 3])
                        SYN.wait_ge(self.sSC.h, m_sq[i - 3])
                    ins = SYN.dma_start(
                        out=xt[i % 3][:, :],
                        in_=self.xb[512 * g + 128 * tt: 512 * g + 128 * (tt + 1), :])
                    xl = self.sXLs[i % 3].inc(ins, 16)
                    SCA.wait_ge(self.sXLs[i % 3].h, xl)
                    if i >= 1:
                        SCA.wait_ge(self.sVE2.h, m_var[i - 1])
                    ins = SCA.activation(sqd[i % 2][:, :], xt[i % 3][:, :], AF.Square)
                    m_sq[i] = self.sSC.inc(ins)
                    VEC.wait_ge(self.sXLs[i % 3].h, xl)
                    ins = VEC.reduce_sum(st1[i % 2][:, :], xt[i % 3][:, :], AX.X)
                    self.se(VEC, ins)
                    VEC.wait_ge(self.sSC.h, m_sq[i])
                    ins = VEC.reduce_sum(st2[i % 2][:, :], sqd[i % 2][:, :], AX.X)
                    self.se(VEC, ins)
                    ins = VEC.tensor_scalar(stm[i % 2][:, :], st1[i % 2][:, :], 1.0 / E,
                                             None, ALU.mult)
                    self.se(VEC, ins)
                    ins = VEC.tensor_scalar(st2[i % 2][:, :], st2[i % 2][:, :], 1.0 / E,
                                            None, ALU.mult)
                    self.se(VEC, ins)
                    ins = VEC.tensor_tensor(st1[i % 2][:, :], stm[i % 2][:, :],
                                            stm[i % 2][:, :], ALU.mult)
                    self.se(VEC, ins)
                    ins = VEC.tensor_tensor(st2[i % 2][:, :], st2[i % 2][:, :],
                                            st1[i % 2][:, :], ALU.subtract)
                    m_var[i] = self.sVE2.inc(ins)
                    SCA.wait_ge(self.sVE2.h, m_var[i])
                    ins = SCA.activation(st1[i % 2][:, :], st2[i % 2][:, :], AF.Ln,
                                         bias=c_[:, COL_EPS:COL_EPS + 1])
                    self.se(SCA, ins)
                    ins = SCA.activation(str_[i % 2][:, :], st1[i % 2][:, :], AF.Exp,
                                         scale=c_[:, COL_NHALF:COL_NHALF + 1])
                    rstd_m = self.sSC.inc(ins)
                    VEC.wait_ge(self.sSC.h, rstd_m)
                    ins = VEC.tensor_scalar(xnb[i % 2][:, :], xt[i % 3][:, :],
                                            stm[i % 2][:, :], str_[i % 2][:, :],
                                            ALU.subtract, ALU.mult)
                    m_xn[i] = self.sVE.inc(ins)
                    TEN.wait_ge(self.sVE.h, m_xn[i])
                    for bch in range(4):
                        if GB >= 2:
                            TEN.wait_ge(self.sVE2.h, m_ev[GB - 2])
                        last = None
                        for k in range(4):
                            ec = 4 * bch + k
                            last = TEN.transpose(
                                pxp[GB % 2][:, 128 * k:128 * (k + 1)],
                                xnb[i % 2][:, 128 * ec:128 * (ec + 1)],
                                self.identb[:, :])
                        m_tb[GB] = self.sPE.inc(last)
                        VEC.wait_ge(self.sPE.h, m_tb[GB])
                        if g > 0:
                            VEC.wait_ge(self.sPE.h, m_kg[g - 1])
                        dst = xnt_tmp[:, :].rearrange("p (c r) -> p c r", c=16)[
                            :, 4 * bch:4 * bch + 4, 128 * tt:128 * (tt + 1)]
                        src = pxp[GB % 2][:, :].rearrange("p (c r) -> p c r", c=4)
                        ins = VEC.tensor_copy(dst, src)
                        m_ev[GB] = self.sVE2.inc(ins)
                        GB += 1
                TEN.wait_ge(self.sVE2.h, m_ev[GB - 1])
                if g > 0:
                    TEN.wait_ge(self.sG2.h, m_grp[g - 1])
                last = None
                for cc in range(16):
                    last = TEN.matmul(pkp[:, :], wkv_sb[:, 256 * cc:256 * cc + 128],
                                      xnt_tmp[:, 512 * cc:512 * (cc + 1)],
                                      start=(cc == 0), stop=(cc == 15))
                m_kp[g] = self.sPE.inc(last)
                for cc in range(16):
                    last = TEN.matmul(pvp[:, :], wkv_sb[:, 256 * cc + 128:256 * (cc + 1)],
                                      xnt_tmp[:, 512 * cc:512 * (cc + 1)],
                                      start=(cc == 0), stop=(cc == 15))
                m_vp[g] = self.sPE.inc(last)
                for cc in range(16):
                    last = TEN.matmul(pkg[:, :], wsm_sb[:, 64 * cc:64 * (cc + 1)],
                                      xnt_tmp[:, 512 * cc:512 * (cc + 1)],
                                      start=(cc == 0), stop=(cc == 15))
                m_kg[g] = self.sPE.inc(last)
                if g > 0:
                    TEN.wait_ge(self.sG2.h, m_grp[g - 1])
                    last = None
                    for k in range(4):
                        last = TEN.transpose(pvt[:, 128 * k:128 * (k + 1)],
                                             tmpv[:, 128 * k:128 * (k + 1)],
                                             self.identb[:, :])
                    m_vt[g - 1] = self.sPE.inc(last)
                cS = self.tabk[0:64, 512 * g:512 * (g + 1)]
                sS = self.tabk[64:128, 512 * g:512 * (g + 1)]
                VEC.wait_ge(self.sPE.h, m_kp[g])
                self.sSE.inc(VEC.tensor_tensor(rp1[:, :], pkp[0:64, :], cS, ALU.mult))
                self.sSE.inc(VEC.tensor_tensor(rp2[:, :], pkp[64:128, :], sS, ALU.mult))
                self.sSE.inc(VEC.tensor_tensor(rp3[:, :], pkp[64:128, :], cS, ALU.mult))
                self.sSE.inc(VEC.tensor_tensor(rp4[:, :], pkp[0:64, :], sS, ALU.mult))
                VEC.wait_ge(self.sSE.h, self.sSE.n)
                self.sSE.inc(VEC.tensor_tensor(self.ktT[0:64, 512 * g:512 * (g + 1)],
                                               rp1[:, :], rp2[:, :], ALU.subtract))
                self.sSE.inc(VEC.tensor_tensor(self.ktT[64:128, 512 * g:512 * (g + 1)],
                                               rp3[:, :], rp4[:, :], ALU.add))
                VEC.wait_ge(self.sSE.h, self.sSE.n)
                VEC.wait_ge(self.sPE.h, m_vp[g])
                if g > 0:
                    VEC.wait_ge(self.sPE.h, m_vt[g - 1])
                self.sG2.inc(VEC.tensor_copy(tmpv[:, :], pvp[:, :]))
                VEC.wait_ge(self.sPE.h, m_kg[g])
                self.sG2.inc(VEC.tensor_copy(kvhgT[:, 512 * g:512 * (g + 1)], pkg[:, :]))
                if g > 0:
                    self.sG2.inc(VEC.tensor_copy(self.vpn[:, 512 * (g - 1):512 * g],
                                                 pvt[:, :]))
                stH = self.snap()
                endsH = []
                for half in (0, 1):
                    self.restore(stH)
                    ctx = (VEC.If_eq(self.regs["v"], 0) if half == 0 else VEC.Else())
                    with ctx:
                        if g in OWN_GROUPS[half]:
                            loc = OWN_GROUPS[half].index(g) * 512
                            dst = self.xnT[:, :].rearrange("p (c r) -> p c r", c=16)[
                                :, :, loc:loc + 512]
                            src = xnt_tmp[:, :].rearrange("p (c r) -> p c r", c=16)
                            ins = VEC.tensor_copy(dst, src)
                        else:
                            ins = VEC.tensor_copy(rp1[0:1, 0:1], rp1[0:1, 0:1])
                        m_grp[g] = self.sG2.inc(ins)
                    endsH.append(self.snap())
                assert endsH[0] == endsH[1]
            TEN.wait_ge(self.sG2.h, m_grp[7])
            last = None
            for k in range(4):
                last = TEN.transpose(pvt[:, 128 * k:128 * (k + 1)],
                                     tmpv[:, 128 * k:128 * (k + 1)], self.identb[:, :])
            m_vt[7] = self.sPE.inc(last)
            VEC.wait_ge(self.sPE.h, m_vt[7])
            ins = VEC.tensor_copy(self.vpn[:, 512 * 7:512 * 8], pvt[:, :])
            self.sG2.inc(ins)

            for eng in (self.TEN, self.SCA, self.GPS, self.SYN):
                eng.wait_ge(self.sG2.h, self.sG2.n)
        with ExitStack() as esP:
            bufA = self.sb(esP, "p1_bA", [32, 4096], F32)
            bufB = self.sb(esP, "p1_bB", [32, 4096], F32)
            bufC = self.sb(esP, "p1_bC", [32, 4096], F32)
            self.S2 = self.sb(esP, "p1_S2", [32, 4096], F32)
            mkcol = self.sb(esP, "p1_mkc", [32, 1], F32)
            sqb = self.sb(esP, "p1_sqb", [128, 512], BF16)
            pkn = self.ps(esP, "p1_pkn", [1, 512], F32)
            pmt = self.ps(esP, "p1_pmt", [128, 128], F32)

            gT = kvhgT[32:64, :]
            kvT = kvhgT[0:32, :]
            SCA.wait_ge(self.sG2.h, self.sG2.n)
            ins = SCA.activation(bufA[:, :], gT, AF.Abs)
            self.se(SCA, ins)
            ins = SCA.activation(bufB[:, :], bufA[:, :], AF.Exp,
                                 scale=c_[0:32, COL_NEG1:COL_NEG1 + 1])
            self.se(SCA, ins)
            ins = SCA.activation(bufA[:, :], bufB[:, :], AF.Ln,
                                 bias=c_[0:32, COL_ONE:COL_ONE + 1])
            ls_m = self.sSC.inc(ins)
            VEC.wait_ge(self.sSC.h, ls_m)
            ins = VEC.tensor_scalar_min(bufC[:, :], gT, 0.0)
            self.se(VEC, ins)
            ins = VEC.tensor_tensor(bufB[:, :], bufC[:, :], bufA[:, :], ALU.subtract)
            self.se(VEC, ins)
            ins = VEC.tensor_tensor_scan(bufC[:, :], bufB[:, :], bufB[:, :], 0.0,
                                         ALU.add, ALU.bypass)
            self.se(VEC, ins)
            ins = VEC.tensor_tensor(bufA[:, :], bufC[:, :], bufB[:, :], ALU.subtract)
            lc_m = self.sVE.inc(ins)
            SCA.wait_ge(self.sVE.h, lc_m)
            ins = SCA.activation(bufC[:, :], bufA[:, :], AF.Exp)
            e_m = self.sSC.inc(ins)
            VEC.wait_ge(self.sSC.h, e_m)
            ins = VEC.tensor_tensor(bufB[:, :], kvT, bufC[:, :], ALU.mult)
            self.se(VEC, ins)
            ins = VEC.tensor_tensor_scan(self.S2[:, :], bufB[:, :], bufB[:, :], 0.0,
                                         ALU.add, ALU.bypass)
            self.se(VEC, ins)
            ins = VEC.reduce_max(mkcol[:, :], self.S2[:, :], AX.X,
                                 apply_absolute_value=True)
            s2_m = self.sVE.inc(ins)

            TEN.wait_ge(self.sVE.h, s2_m)
            for b4 in range(8):
                if b4 >= 1:
                    TEN.wait_ge(self.sVE2.h, self.sVE2.n)
                last = None
                for k in range(4):
                    mc = 4 * b4 + k
                    last = TEN.transpose(pmt[:, 32 * k:32 * (k + 1)],
                                         self.S2[:, 128 * mc:128 * (mc + 1)],
                                         self.identf[0:32, 0:32])
                pe_m = self.sPE.inc(last)
                VEC.wait_ge(self.sPE.h, pe_m)
                ins = VEC.tensor_copy(self.kvhn[:, 128 * b4:128 * (b4 + 1)], pmt[:, :])
                self.sVE2.inc(ins)
            TEN.wait_ge(self.sVE2.h, self.sVE2.n)
            last = TEN.transpose(pmt[0:1, 0:32],
                                 mkcol[:, :], self.identf[0:32, 0:32])
            pe_m = self.sPE.inc(last)
            VEC.wait_ge(self.sPE.h, pe_m)
            ins = VEC.tensor_copy(self.mkrow[:, :], pmt[0:1, 0:32])
            self.sVE2.inc(ins)
            for ch in range(8):
                SCA.wait_ge(self.sVE2.h, self.sVE2.n)
                SCA.wait_ge(self.sG2.h, self.sG2.n)
                ins = SCA.activation(sqb[:, :], self.ktT[:, 512 * ch:512 * (ch + 1)],
                                     AF.Square)
                sc_m = self.sSC.inc(ins)
                TEN.wait_ge(self.sSC.h, sc_m)
                if ch >= 1:
                    TEN.wait_ge(self.sVE2.h, self.sVE2.n)
                ins = TEN.matmul(pkn[:, :], self.onesb_c[:, :], sqb[:, :],
                                 start=True, stop=True)
                pe_m = self.sPE.inc(ins)
                VEC.wait_ge(self.sPE.h, pe_m)
                ins = VEC.reduce_max(self.knm[0:1, 1:2], pkn[:, :], AX.X)
                self.se(VEC, ins)
                ins = VEC.tensor_tensor(self.knm[0:1, 0:1], self.knm[0:1, 0:1],
                                        self.knm[0:1, 1:2], ALU.max)
                n = self.sVE2.inc(ins)
                VEC.wait_ge(self.sVE2.h, n)
            for eng in (TEN, SCA, GPS, SYN):
                eng.wait_ge(self.sVE2.h, self.sVE2.n)
                eng.wait_ge(self.sVE.h, self.sVE.n)
        esP1o.close()

    def p2(self):
        nc, TEN, VEC, SCA, GPS, SYN = self.nc, self.TEN, self.VEC, self.SCA, self.GPS, self.SYN
        with ExitStack() as esP:
            wqb = [self.sb(esP, f"p2_wq{i}", [128, E], BF16) for i in range(2)]
            qro = [self.sb(esP, f"p2_qr{i}", [128, 512], BF16) for i in range(2)]
            sqb = [self.sb(esP, f"p2_sq{i}", [128, 512], BF16) for i in range(2)]
            rp1 = self.sb(esP, "p2_rp1", [64, 512], F32)
            rp2 = self.sb(esP, "p2_rp2", [64, 512], F32)
            rp3 = self.sb(esP, "p2_rp3", [64, 512], F32)
            rp4 = self.sb(esP, "p2_rp4", [64, 512], F32)
            mx1 = self.sb(esP, "p2_mx1", [1, 4], F32)
            pq = [self.ps(esP, f"p2_pq{i}", [128, 512], F32) for i in range(2)]
            pq2 = self.ps(esP, "p2_pq2", [1, 512], F32)

            m_wl = {}
            m_mm = {}
            m_rope = {}
            m_sq = {}
            m_q2 = {}
            m_qs = {}
            m_mx = {}
            GPS.wait_ge(self.sPE.h, self.f_sdrfree)
            for och in range(16):
                if och >= 2:
                    GPS.wait_ge(self.sPE.h, m_mm[(och - 2, 3)])
                ins = GPS.dma_start(out=wqb[och % 2][:, :],
                                    in_=self.wq[128 * och:128 * (och + 1), :])
                m_wl[och] = self.sWQ[och % 2].inc(ins, 16)
                TEN.wait_ge(self.sWQ[och % 2].h, m_wl[och])
                for tg in range(4):
                    u = 4 * och + tg
                    if u >= 2:
                        TEN.wait_ge(self.sVE.h, m_rope[u - 2])
                    last = None
                    for cc in range(16):
                        last = TEN.matmul(pq[u % 2][:, :],
                                          wqb[och % 2][:, 128 * cc:128 * (cc + 1)],
                                          self.xnT[:, 2048 * cc + 512 * tg:
                                                   2048 * cc + 512 * (tg + 1)],
                                          start=(cc == 0), stop=(cc == 15))
                    m_mm[(och, tg)] = self.sPE.inc(last)
                    VEC.wait_ge(self.sPE.h, m_mm[(och, tg)])
                    if u >= 2:
                        VEC.wait_ge(self.sQSs[u % 2].h, m_qs[u - 2])
                        VEC.wait_ge(self.sSC.h, m_sq[u - 2])
                    cS = self.tabq[0:64, 512 * tg:512 * (tg + 1)]
                    sS = self.tabq[64:128, 512 * tg:512 * (tg + 1)]
                    self.sSE.inc(VEC.tensor_tensor(rp1[:, :], pq[u % 2][0:64, :], cS, ALU.mult))
                    self.sSE.inc(VEC.tensor_tensor(rp2[:, :], pq[u % 2][64:128, :], sS, ALU.mult))
                    self.sSE.inc(VEC.tensor_tensor(rp3[:, :], pq[u % 2][64:128, :], cS, ALU.mult))
                    self.sSE.inc(VEC.tensor_tensor(rp4[:, :], pq[u % 2][0:64, :], sS, ALU.mult))
                    VEC.wait_ge(self.sSE.h, self.sSE.n)
                    ins = VEC.tensor_tensor(qro[u % 2][0:64, :], rp1[:, :], rp2[:, :],
                                            ALU.subtract)
                    self.sVE.inc(ins)
                    ins = VEC.tensor_tensor(qro[u % 2][64:128, :], rp3[:, :], rp4[:, :],
                                            ALU.add)
                    m_rope[u] = self.sVE.inc(ins)
                    GPS.wait_ge(self.sVE.h, m_rope[u])
                    ins = GPS.dma_start(
                        out=self.q_scr[:, 2048 * och + 512 * tg:
                                       2048 * och + 512 * (tg + 1)],
                        in_=qro[u % 2][:, :])
                    m_qs[u] = self.sQSs[u % 2].inc(ins, 16)
                    SCA.wait_ge(self.sVE.h, m_rope[u])
                    if u >= 1:
                        SCA.wait_ge(self.sPE.h, m_q2.get(u - 1, 0))
                    ins = SCA.activation(sqb[u % 2][:, :], qro[u % 2][:, :], AF.Square)
                    m_sq[u] = self.sSC.inc(ins)
                    TEN.wait_ge(self.sSC.h, m_sq[u])
                    if u >= 1:
                        TEN.wait_ge(self.sVE2.h, m_mx[u - 1])
                    ins = TEN.matmul(pq2[:, :], self.onesb_c[:, :], sqb[u % 2][:, :],
                                     start=True, stop=True)
                    m_q2[u] = self.sPE.inc(ins)
                    VEC.wait_ge(self.sPE.h, m_q2[u])
                    ins = VEC.reduce_max(mx1[0:1, 0:1], pq2[:, :], AX.X)
                    self.se(VEC, ins)
                    ins = VEC.tensor_tensor(self.maxqrow[0:1, och:och + 1],
                                            self.maxqrow[0:1, och:och + 1],
                                            mx1[0:1, 0:1], ALU.max)
                    m_mx[u] = self.sVE2.inc(ins)
                    VEC.wait_ge(self.sVE2.h, m_mx[u])
                for j in range(4):
                    self._gu_unit(0, 4 * och + j, self.f_pg0, self.f_pu0)
            for eng in (TEN, SCA, GPS, SYN, VEC):
                eng.wait_ge(self.sVE2.h, self.sVE2.n)
                eng.wait_ge(self.sQSs[0].h, self.sQSs[0].n)
                eng.wait_ge(self.sQSs[1].h, self.sQSs[1].n)

    def p25a(self):
        nc, TEN, VEC, SCA, GPS, SYN = self.nc, self.TEN, self.VEC, self.SCA, self.GPS, self.SYN
        ins = SYN.dma_start(out=self.kt_scr[:, :], in_=self.ktT[:, :])
        self.sG2.inc(ins, 16)
        SYN.wait_ge(self.sG2.h, self.sG2.n)
        ins = SYN.dma_start(out=self.vp_scr[:, :], in_=self.vpn[:, :])
        self.sG2.inc(ins, 16)
        SYN.wait_ge(self.sG2.h, self.sG2.n)
        ins = SYN.dma_start(out=self.kn_scr[:, :], in_=self.kvhn[:, :])
        self.sG2.inc(ins, 16)
        SYN.wait_ge(self.sG2.h, self.sG2.n)
        for eng in (TEN, VEC, SCA, GPS):
            eng.wait_ge(self.sG2.h, self.sG2.n)
        self.esG.close()
        self.esTk.close()

        esF = ExitStack()
        self.esF = esF
        self.f_fuse = self.sb(esF, "p25_fuse", [128, 64 * 512], BF16)
        self.f_wgb = [self.sb(esF, f"p25_wg{i}", [128, E], BF16) for i in range(2)]
        self.f_wub = [self.sb(esF, f"p25_wu{i}", [128, E], BF16) for i in range(2)]
        self.f_sgb = [self.sb(esF, f"p25_sg{i}", [128, 512], F32) for i in range(2)]
        self.f_usb = [self.sb(esF, f"p25_us{i}", [128, 512], F32) for i in range(2)]
        self.f_ssg = self.sb(esF, "p25_ssg", [128, 64], F32)
        self.f_ssu = self.sb(esF, "p25_ssu", [128, 64], F32)
        self.f_sdb = self.sb(esF, "p25_sdb", [128, E], F32)

        ins = GPS.dma_start(out=self.f_ssg[:, :], in_=self.sgv[:, :])
        self.sWL.inc(ins, 16)
        GPS.wait_ge(self.sWL.h, self.sWL.n)
        ins = GPS.dma_start(out=self.f_ssu[:, :], in_=self.suv[:, :])
        self.sWL.inc(ins, 16)
        GPS.wait_ge(self.sWL.h, self.sWL.n)

        with ExitStack() as esQ:
            sdr = self.sb(esQ, "p25_sdr", [1, E], F32)
            psd = self.ps(esQ, "p25_psd", [128, 512], F32)
            ins = GPS.dma_start(out=sdr[:, :], in_=self.sdv[:, :])
            wl0 = self.sWL.inc(ins, 16)
            TEN.wait_ge(self.sWL.h, wl0)
            for i in range(4):
                if i > 0:
                    TEN.wait_ge(self.sVE.h, self.sVE.n)
                ins = TEN.matmul(psd[:, :], self.onesf_r[0:1, 0:128],
                                 sdr[0:1, 512 * i:512 * (i + 1)],
                                 start=True, stop=True)
                pe_m = self.sPE.inc(ins)
                VEC.wait_ge(self.sPE.h, pe_m)
                ins = VEC.tensor_copy(self.f_sdb[:, 512 * i:512 * (i + 1)], psd[:, :])
                self.sVE.inc(ins)
            TEN.wait_ge(self.sVE.h, self.sVE.n)
            self.sdb_done = self.sVE.n
            self.f_sdrfree = self.sPE.n
            SCA.wait_ge(self.sWL.h, wl0)

        self.ffb_ring = 0
        self.xsl_ring = 0
        self.xsl_done = {}
        self.sXO2 = self.sXO + [Sem(self.es.enter_context(self.nc.semaphore("sXO2c")))]
        self.fm_w = {}
        self.fm_gu = {}
        self.fm_sc = {}
        self.fm_fu = {}
        self.fm_dn = {}
        self.f_dn_done = {}
        self.f_t0marks = (self.sVE.n, self.sVE2.n, self.sSC.n)
        self.esPG = ExitStack()
        self.f_pg0 = [self.ps(self.esPG, f"p25_pg0_{i}", [128, 512], F32)
                      for i in range(2)]
        self.f_pu0 = [self.ps(self.esPG, f"p25_pu0_{i}", [128, 512], F32)
                      for i in range(2)]

    def _gu_unit(self, tb, ic, pg, pu):
        TEN, VEC, SCA, GPS = self.TEN, self.VEC, self.SCA, self.GPS
        wgb, wub, sgb, usb = self.f_wgb, self.f_wub, self.f_sgb, self.f_usb
        if ic == 0:
            if tb > 0:
                TEN.wait_ge(self.sVE2.h, self.sVE2.n)
            else:
                TEN.wait_ge(self.sVE.h, self.f_t0marks[0])
                TEN.wait_ge(self.sVE2.h, self.f_t0marks[1])
                TEN.wait_ge(self.sSC.h, self.f_t0marks[2])
        if ic >= 2:
            GPS.wait_ge(self.sPE.h, self.fm_gu[(tb, ic - 2)])
        sw = self.sWG[ic % 2]
        ins = GPS.dma_start(out=wgb[ic % 2][:, :],
                            in_=self.wgt[128 * ic:128 * (ic + 1), :])
        sw.inc(ins, 16)
        ins = GPS.dma_start(out=wub[ic % 2][:, :],
                            in_=self.wut[128 * ic:128 * (ic + 1), :])
        self.fm_w[(tb, ic)] = sw.inc(ins, 16)
        TEN.wait_ge(sw.h, self.fm_w[(tb, ic)])
        if ic >= 2:
            TEN.wait_ge(self.sSC.h, self.fm_sc[(tb, ic - 2)])
        last = None
        for cc in range(16):
            last = TEN.matmul(pg[ic % 2][:, :],
                              wgb[ic % 2][:, 128 * cc:128 * (cc + 1)],
                              self.xnT[:, 2048 * cc + 512 * tb:
                                       2048 * cc + 512 * (tb + 1)],
                              start=(cc == 0), stop=(cc == 15))
        for cc in range(16):
            last = TEN.matmul(pu[ic % 2][:, :],
                              wub[ic % 2][:, 128 * cc:128 * (cc + 1)],
                              self.xnT[:, 2048 * cc + 512 * tb:
                                       2048 * cc + 512 * (tb + 1)],
                              start=(cc == 0), stop=(cc == 15))
        self.fm_gu[(tb, ic)] = self.sPE.inc(last)
        SCA.wait_ge(self.sPE.h, self.fm_gu[(tb, ic)])
        if ic >= 2:
            SCA.wait_ge(self.sVE.h, self.fm_fu[(tb, ic - 2)])
        self.sSC.inc(SCA.activation(sgb[ic % 2][:, :], pg[ic % 2][:, :],
                                    AF.Silu, scale=self.f_ssg[:, ic:ic + 1]))
        ins = SCA.activation(usb[ic % 2][:, :], pu[ic % 2][:, :],
                             AF.Copy, scale=self.f_ssu[:, ic:ic + 1])
        self.fm_sc[(tb, ic)] = self.sSC.inc(ins)
        VEC.wait_ge(self.sSC.h, self.fm_sc[(tb, ic)])
        if ic == 0:
            VEC.wait_ge(self.sVE.h, self.sdb_done)
        if tb > 0 and ic == 0:
            VEC.wait_ge(self.sPE.h, self.f_dn_done[tb - 1])
        ins = VEC.tensor_tensor(self.f_fuse[:, 512 * ic:512 * (ic + 1)],
                                sgb[ic % 2][:, :], usb[ic % 2][:, :],
                                ALU.mult)
        self.fm_fu[(tb, ic)] = self.sVE.inc(ins)

    def _down(self, tb):
        TEN, VEC, GPS, SYN = self.TEN, self.VEC, self.GPS, self.SYN
        wdb, fuse, sdb = self.f_wdb, self.f_fuse, self.f_sdb
        xsl, ffb, tmpe = self.f_xsl, self.f_ffb, self.f_tmpe
        with ExitStack() as esQ:
            pd = [self.ps(esQ, f"p25_pd{tb}_{j}", [128, 512], F32) for j in range(8)]
            for p2 in range(2):
                for q4 in range(16):
                    par = q4 % 2
                    seq = (tb * 2 + p2) * 16 + q4
                    if seq >= 2:
                        GPS.wait_ge(self.sPE.h, self.fm_dn[seq - 2])
                    src_ap = bass.AP(self.wdn,
                                     (512 * q4) * E + 1024 * p2,
                                     [[E, 128], [128 * E, 4], [1, 1024]])
                    ins = GPS.dma_start(out=wdb[par][:, :], in_=src_ap)
                    wd_m = self.sWD[par].inc(ins, 16)
                    TEN.wait_ge(self.sWD[par].h, wd_m)
                    if q4 == 0:
                        TEN.wait_ge(self.sVE.h, self.fm_fu[(tb, 63)])
                        if (tb, p2) != (0, 0):
                            TEN.wait_ge(self.sVE2.h, self.sVE2.n)
                    last = None
                    for sub in range(4):
                        kc = 4 * q4 + sub
                        for tt in range(4):
                            for og2 in range(2):
                                last = TEN.matmul(
                                    pd[2 * tt + og2][:, :],
                                    fuse[:, 512 * kc + 128 * tt:
                                         512 * kc + 128 * (tt + 1)],
                                    wdb[par][:, 1024 * sub + 512 * og2:
                                             1024 * sub + 512 * (og2 + 1)],
                                    start=(kc == 0), stop=(kc == 63))
                    self.fm_dn[seq] = self.sPE.inc(last)
                pe_last = self.sPE.n
                VEC.wait_ge(self.sPE.h, pe_last)
                for k8 in range(8):
                    tt, og2 = divmod(k8, 2)
                    ri = self.xsl_ring
                    xslot = ri % 3
                    self.xsl_ring += 1
                    trow = 512 * tb + 128 * tt
                    tcol = 1024 * p2 + 512 * og2
                    if ri >= 3:
                        SYN.wait_ge(self.sVE2.h, self.xsl_done[ri - 3])
                    ins = SYN.dma_start(
                        out=xsl[xslot][:, :],
                        in_=self.x_own[trow:trow + 128,
                                       tcol:tcol + 512])
                    m_x = self.sXO2[xslot].inc(ins, 16)
                    slot = self.ffb_ring % 3
                    self.ffb_ring += 1
                    if self.sFB[slot].n > 0:
                        VEC.wait_ge(self.sFB[slot].h, self.sFB[slot].n)
                    VEC.wait_ge(self.sXO2[xslot].h, m_x)
                    ins = VEC.tensor_tensor(tmpe[k8 % 2][:, :],
                                            pd[2 * tt + og2][:, :],
                                            sdb[:, tcol:tcol + 512],
                                            ALU.mult)
                    self.se(VEC, ins)
                    ins = VEC.tensor_tensor(
                        ffb[slot][:, :], tmpe[k8 % 2][:, :],
                        xsl[xslot][:, :], ALU.add)
                    ve_m = self.sVE2.inc(ins)
                    self.xsl_done[ri] = ve_m
                    SYN.wait_ge(self.sVE2.h, ve_m)
                    ins = SYN.dma_start(
                        out=self.fp_scr[trow:trow + 128,
                                        tcol:tcol + 512],
                        in_=ffb[slot][:, :])
                    self.sFB[slot].inc(ins, 16)
                self.f_dn_done[tb] = pe_last

    def p25b(self):
        nc, TEN, VEC, SCA, GPS, SYN = self.nc, self.TEN, self.VEC, self.SCA, self.GPS, self.SYN
        with ExitStack() as esW:
            self.f_wdb = [self.sb(esW, f"p25_wd{i}", [128, 2 * E], BF16)
                          for i in range(2)]
            self.f_xsl = [self.sb(esW, f"p25_xs{i}", [128, 512], F32) for i in range(3)]
            self.f_ffb = [self.sb(esW, f"p25_ff{i}", [128, 512], F32) for i in range(3)]
            self.f_tmpe = [self.sb(esW, f"p25_te{i}", [128, 512], F32) for i in range(2)]
            self.esPG.close()
            self._down(0)
            for tb in range(1, 4):
                with ExitStack() as esQ:
                    pg = [self.ps(esQ, f"p25_pg{tb}_{i}", [128, 512], F32)
                          for i in range(2)]
                    pu = [self.ps(esQ, f"p25_pu{tb}_{i}", [128, 512], F32)
                          for i in range(2)]
                    for ic in range(64):
                        self._gu_unit(tb, ic, pg, pu)
                self._down(tb)
            if DBG_EXTRA:
                self.dbg_dump([("sdb", self.f_sdb[:, :], [128, E], 0),
                               ("ssg", self.f_ssg[:, :], [128, 64], 0)])
            for eng in (TEN, VEC, SCA, GPS, SYN):
                eng.wait_ge(self.sVE2.h, self.sVE2.n)
        self.esF.close()
        self.esTq.close()
        self.esA.close()

    def p3(self):
        nc, TEN, VEC, SCA, GPS, SYN = self.nc, self.TEN, self.VEC, self.SCA, self.GPS, self.SYN
        c_ = self.c_
        esB = ExitStack()
        self.esB = esB
        self.dmask = self.sb(esB, "b_dmask", [128, 4 * 512], BF16)
        self.OT = self.sb(esB, "b_OT", [128, H * 2048], BF16)
        self.wout_sb = self.sb(esB, "b_wout", [128, 16 * 2048], BF16)
        esB2 = ExitStack()
        self.esB2 = esB2
        ktT = self.sb(esB2, "b_ktT", [128, 4096], BF16)
        vpn = self.sb(esB2, "b_vpn", [128, 4096], BF16)
        kvhn = self.sb(esB2, "b_kvhn", [128, 1024], F32)
        qc = [self.sb(esB2, f"b_qc{i}", [128, 2048], BF16) for i in range(2)]
        vpp = [self.sb(esB2, "b_vpp0", [128, 4096], BF16)] * 2
        PT = [self.sb(esB2, f"b_pt{i}", [128, 512], BF16) for i in range(3)]
        rbs = self.sb(esB2, "b_rbs", [128, 512], BF16)
        negB = [self.sb(esB2, f"b_nb{i}", [128, 1], F32) for i in range(2)]
        lr = self.sb(esB2, "b_lr", [1, 512], F32)
        rr = self.sb(esB2, "b_rr", [1, 512], F32)

        for v_ in range(4):
            ins = GPS.memset(self.dmask[:, 512 * v_:512 * (v_ + 1)], 1.0)
            self.se(GPS, ins)
            ins = GPS.affine_select(self.dmask[:, 512 * v_:512 * (v_ + 1)],
                                    self.dmask[:, 512 * v_:512 * (v_ + 1)],
                                    [[1, 512]], ALU.is_ge, 0.0,
                                    base=-128 * v_, channel_multiplier=-1)
            self.sGP.inc(ins)
        gp_mask_done = self.sGP.n
        for eng in (TEN, VEC, SCA):
            eng.wait_ge(self.sGP.h, gp_mask_done)
        ins = SYN.dma_start(out=ktT[:, :], in_=self.kt_scr[:, :])
        self.sG2.inc(ins, 16)
        SYN.wait_ge(self.sG2.h, self.sG2.n)
        ins = SYN.dma_start(out=vpn[:, :], in_=self.vp_scr[:, :])
        self.sG2.inc(ins, 16)
        SYN.wait_ge(self.sG2.h, self.sG2.n)
        ins = SYN.dma_start(out=kvhn[:, :], in_=self.kn_scr[:, :])
        g2m = self.sG2.inc(ins, 16)
        for b in range(16):
            if b > 0:
                GPS.wait_ge(self.sWL.h, self.sWL.n)
            ins = GPS.dma_start(out=self.wout_sb[:, 2048 * b:2048 * (b + 1)],
                                in_=self.wout[128 * b:128 * (b + 1), :])
            self.sWL.inc(ins, 16)
        self.wout_done = self.sWL.n
        GPS.wait_ge(self.sQSs[0].h, self.sQSs[0].n)
        GPS.wait_ge(self.sQSs[1].h, self.sQSs[1].n)
        ins = GPS.dma_start(out=qc[0][:, :], in_=self.q_scr[:, 0:2048])
        self.sQC[0].inc(ins, 16)
        m_ql = {0: (0, self.sQC[0].n)}

        with ExitStack() as esQ:
            pS = [self.ps(esQ, f"b_pS{i}", [128, 512], F32) for i in range(3)]
            pO = [self.ps(esQ, f"b_pO{i}", [128, 512], F32) for i in range(2)]
            pL = [self.ps(esQ, f"b_pL{i}", [1, 512], F32) for i in range(2)]
            pM0 = self.ps(esQ, "b_pM0", [128, 512], F32)
            pM = [pM0, pM0]

            for eng in (TEN, VEC, SCA):
                eng.wait_ge(self.sG2.h, g2m)

            head_end = {}
            for h in range(H):
                if h + 1 < H:
                    if h >= 1:
                        GPS.wait_ge(self.sPE.h, head_end[h - 1]["lastS"])
                    ins = GPS.dma_start(out=qc[(h + 1) % 2][:, :],
                                        in_=self.q_scr[:, 2048 * (h + 1):2048 * (h + 2)])
                    sl = (h + 1) % 2
                    m_ql[h + 1] = (sl, self.sQC[sl].inc(ins, 16))

                ins = VEC.tensor_tensor(self.b11[0:1, 0:1], self.mkrow[0:1, h:h + 1],
                                         self.mkrow[0:1, h:h + 1], ALU.mult)
                self.se(VEC, ins)
                ins = VEC.tensor_tensor(self.b11[0:1, 1:2], self.b11[0:1, 0:1],
                                        self.knm[0:1, 0:1], ALU.mult)
                self.se(VEC, ins)
                ins = VEC.tensor_tensor(self.b11[0:1, 2:3], self.b11[0:1, 1:2],
                                        self.maxqrow[0:1, h:h + 1], ALU.mult)
                ve_m = self.sVE.inc(ins)
                SCA.wait_ge(self.sVE.h, ve_m)
                ins = SCA.activation(self.b11[0:1, 3:4], self.b11[0:1, 2:3], AF.Ln)
                self.se(SCA, ins)
                ins = SCA.activation(self.b11[0:1, 4:5], self.b11[0:1, 3:4], AF.Exp,
                                     scale=c_[0:1, COL_HALFC:COL_HALFC + 1])
                sc_m = self.sSC.inc(ins)
                VEC.wait_ge(self.sSC.h, sc_m)
                ins = VEC.tensor_scalar(self.b11[0:1, 5:6], self.b11[0:1, 4:5], -1.0,
                                        None, ALU.mult)
                ve_m = self.sVE.inc(ins)
                TEN.wait_ge(self.sVE.h, ve_m)
                if h > 0:
                    TEN.wait_ge(self.sVE2.h, head_end[h - 1]["sve2"])
                ins = TEN.matmul(pM0[:, 0:1], self.onesf_r[0:1, :],
                                 self.b11[0:1, 5:6], start=True, stop=True)
                pe_m = self.sPE.inc(ins)
                VEC.wait_ge(self.sPE.h, pe_m)
                ins = VEC.tensor_copy(negB[h % 2][:, :], pM0[:, 0:1])
                nb_ev = self.sVE.inc(ins)
                he1 = head_end.get(h - 1)
                for mc in range(32):
                    if mc == 0 and he1 is not None:
                        VEC.wait_ge(self.sPE.h, he1["lastAV"])
                    ins = VEC.tensor_scalar_mul(
                        vpp[h % 2][:, 128 * mc:128 * (mc + 1)],
                        vpn[:, 128 * mc:128 * (mc + 1)],
                        kvhn[:, 32 * mc + 16 + h:32 * mc + 16 + h + 1])
                    self.sVE.inc(ins)
                hprep = self.sVE.n

                TEN.wait_ge(self.sQC[m_ql[h][0]].h, m_ql[h][1])
                st0 = self.snap()
                bmarks = {}
                ends = []
                for half in (0, 1):
                    self.restore(st0)
                    pairs = pair_list(half)
                    NP = len(pairs)
                    assert NP == 72, NP
                    slot_last = {}
                    for pj, pr in enumerate(pairs):
                        if pr["last"]:
                            slot_last[pr["si"]] = pj

                    def exp_mark(pj, pairs=pairs, st0=st0):
                        return st0["sSC"] + pj + 1 + pairs[pj]["si"]

                    def rinv_mark(si, slot_last=slot_last, st0=st0):
                        return st0["sSC"] + slot_last[si] + 2 + si

                    def lv_mark(si, st0=st0):
                        return st0["sVE2"] + 2 * si + 1

                    def oev_mark(si, st0=st0):
                        return st0["sVE2"] + 2 * si + 2

                    gp_mark = {}
                    d = 0
                    for pj, pr in enumerate(pairs):
                        if pr["dv"] is not None:
                            d += 1
                            gp_mark[pj] = st0["sGP"] + d
                    mS = {}
                    mAV = {}
                    mBC = {}

                    def emit_lav(pj, pairs=pairs, gp_mark=gp_mark, mS=mS, mAV=mAV,
                                 mBC=mBC, exp_mark=exp_mark, rinv_mark=rinv_mark,
                                 oev_mark=oev_mark, nb_ev=nb_ev, h=h):
                        pr = pairs[pj]
                        si, mc = pr["si"], pr["mc"]
                        if pr["dv"] is not None:
                            TEN.wait_ge(self.sGP.h, gp_mark[pj])
                        else:
                            TEN.wait_ge(self.sSC.h, exp_mark(pj))
                        if pr["first"] and si >= 2:
                            TEN.wait_ge(self.sVE2.h, oev_mark(si - 2))
                        TEN.matmul(pL[si % 2][:, :], self.onesb_c[:, :],
                                   PT[pj % 3][:, :], start=pr["first"], stop=pr["last"])
                        ins = TEN.matmul(pO[si % 2][:, :],
                                         vpp[h % 2][:, 128 * mc:128 * (mc + 1)],
                                         PT[pj % 3][:, :],
                                         start=pr["first"], stop=pr["last"])
                        mAV[pj] = self.sPE.inc(ins)
                        if pr["last"]:
                            TEN.wait_ge(self.sSC.h, rinv_mark(si))
                            if si == 0:
                                TEN.wait_ge(self.sVE.h, nb_ev)
                            else:
                                TEN.wait_ge(self.sVE2.h, oev_mark(si - 1))
                            ins = TEN.matmul(pM0[:, :], self.onesf_r[0:1, :],
                                             rr[0:1, :], start=True, stop=True)
                            mBC[si] = self.sPE.inc(ins)

                    with (TEN.If_eq(self.regs["t"], 0) if half == 0 else TEN.Else()):
                        TEN.wait_ge(self.sVE.h, hprep)
                        TEN.wait_ge(self.sSC.h, st0["sSC"])
                        TEN.wait_ge(self.sVE2.h, st0["sVE2"])
                        TEN.wait_ge(self.sGP.h, st0["sGP"])
                        for pj, pr in enumerate(pairs):
                            if pj >= 3:
                                TEN.wait_ge(self.sSC.h, exp_mark(pj - 3))
                            ins = TEN.matmul(pS[pj % 3][:, :],
                                             ktT[:, 128 * pr["mc"]:
                                                 128 * (pr["mc"] + 1)],
                                             qc[h % 2][:, 512 * pr["tg"]:
                                                       512 * (pr["tg"] + 1)],
                                             start=True, stop=True)
                            mS[pj] = self.sPE.inc(ins)
                            if pj >= 2:
                                emit_lav(pj - 2)
                        emit_lav(NP - 2)
                        emit_lav(NP - 1)
                    bmarks[half] = dict(mS=mS, mAV=mAV, mBC=mBC, exp_mark=exp_mark,
                                        rinv_mark=rinv_mark, lv_mark=lv_mark,
                                        oev_mark=oev_mark, gp_mark=gp_mark,
                                        slot_last=slot_last, pairs=pairs)
                    ends.append(self.snap())
                assert ends[0] == ends[1], (h, "TEN", ends)

                st1 = self.snap()
                e2 = []
                for half in (0, 1):
                    self.restore(st1)
                    bm = bmarks[half]
                    with (SCA.If_eq(self.regs["s"], 0) if half == 0 else SCA.Else()):
                        SCA.wait_ge(self.sVE.h, nb_ev)
                        for pj, pr in enumerate(bm["pairs"]):
                            SCA.wait_ge(self.sPE.h, bm["mS"][pj])
                            if pj >= 3:
                                SCA.wait_ge(self.sPE.h, bm["mAV"][pj - 3])
                            mc_ = pr["mc"]
                            ins = SCA.activation(PT[pj % 3][:, :], pS[pj % 3][:, :],
                                                 AF.Exp, bias=negB[h % 2][:, :],
                                                 scale=kvhn[:, 32 * mc_ + h:
                                                            32 * mc_ + h + 1])
                            self.sSC.inc(ins)
                            if pr["last"]:
                                SCA.wait_ge(self.sVE2.h, bm["lv_mark"](pr["si"]))
                                if pr["si"] >= 1:
                                    SCA.wait_ge(self.sPE.h, bm["mBC"][pr["si"] - 1])
                                SCA.activation(rr[:, :], lr[:, :], AF.Ln)
                                ins = SCA.activation(rr[:, :], rr[:, :], AF.Exp,
                                                     scale=c_[0:1, COL_NEG1:COL_NEG1 + 1])
                                self.sSC.inc(ins)
                    e2.append(self.snap())
                assert e2[0] == e2[1], (h, "SCA")

                st2 = self.snap()
                e3 = []
                for half in (0, 1):
                    self.restore(st2)
                    bm = bmarks[half]
                    with (GPS.If_eq(self.regs["g"], 0) if half == 0 else GPS.Else()):
                        for pj, pr in enumerate(bm["pairs"]):
                            if pr["dv"] is None:
                                continue
                            GPS.wait_ge(self.sSC.h, bm["exp_mark"](pj))
                            ins = GPS.affine_select(
                                PT[pj % 3][:, :], PT[pj % 3][:, :], [[1, 512]],
                                ALU.is_ge, 0.0, base=-128 * pr["dv"],
                                channel_multiplier=-1)
                            self.sGP.inc(ins)
                    e3.append(self.snap())
                assert e3[0] == e3[1], (h, "GPS")

                st3 = self.snap()
                e4 = []
                for half in (0, 1):
                    self.restore(st3)
                    bm = bmarks[half]
                    with (VEC.If_eq(self.regs["v"], 0) if half == 0 else VEC.Else()):
                        for pj, pr in enumerate(bm["pairs"]):
                            if not pr["last"]:
                                continue
                            si = pr["si"]
                            tg = SLOT_LNT[si]
                            VEC.wait_ge(self.sPE.h, bm["mAV"][pj])
                            if si >= 1:
                                VEC.wait_ge(self.sSC.h, bm["rinv_mark"](si - 1))
                            ins = VEC.tensor_scalar(lr[:, :], pL[si % 2][:, :],
                                                    1e-30, None, ALU.add)
                            self.sVE2.inc(ins)
                            VEC.wait_ge(self.sPE.h, bm["mBC"][si])
                            ins = VEC.tensor_copy(rbs[:, :], pM0[:, :])
                            self.se(VEC, ins)
                            ins = VEC.tensor_tensor(
                                self.OT[:, 2048 * h + 512 * tg:
                                        2048 * h + 512 * (tg + 1)],
                                pO[si % 2][:, :], rbs[:, :], ALU.mult)
                            self.sVE2.inc(ins)
                    e4.append(self.snap())
                assert e4[0] == e4[1], (h, "VEC")

                head_end[h] = dict(
                    lastS=max(bmarks[0]["mS"][71], bmarks[1]["mS"][71]),
                    lastAV=max(bmarks[0]["mAV"][71], bmarks[1]["mAV"][71]),
                    sve2=self.sVE2.n)

            for eng in (TEN, VEC, SCA, GPS, SYN):
                eng.wait_ge(self.sVE2.h, self.sVE2.n)
                eng.wait_ge(self.sSC.h, self.sSC.n)
        self.esB2.close()

    def p4(self):
        nc, TEN, VEC, SCA, GPS, SYN = self.nc, self.TEN, self.VEC, self.SCA, self.GPS, self.SYN
        with ExitStack() as esP:
            fpt = [self.sb(esP, f"p4_fp{i}", [128, E], F32) for i in range(3)]
            orow = [self.sb(esP, f"p4_or{i}", [128, E], F32) for i in range(2)]
            py = [self.ps(esP, f"p4_py{j}", [128, 512], F32) for j in range(8)]

            TEN.wait_ge(self.sWL.h, self.wout_done)
            for s_ in self.sFB:
                SYN.wait_ge(s_.h, s_.n)
            m_fl = {}
            m_y = {}
            m_ev = {}
            for t in range(16):
                if t >= 3:
                    SYN.wait_ge(self.sVE2.h, m_ev[t - 3])
                ins = SYN.dma_start(out=fpt[t % 3][:, :],
                                    in_=self.fp_scr[128 * t:128 * (t + 1), :])
                m_fl[t] = self.sFP[t % 3].inc(ins, 16)
                par = t % 2
                if t >= 2:
                    TEN.wait_ge(self.sVE2.h, m_ev[t - 2])
                last = None
                for hh in range(16):
                    for oc in range(4):
                        last = TEN.matmul(
                            py[4 * par + oc][:, :],
                            self.OT[:, 2048 * hh + 128 * t:2048 * hh + 128 * (t + 1)],
                            self.wout_sb[:, 2048 * hh + 512 * oc:
                                         2048 * hh + 512 * (oc + 1)],
                            start=(hh == 0), stop=(hh == 15))
                m_y[t] = self.sPE.inc(last)
                VEC.wait_ge(self.sPE.h, m_y[t])
                VEC.wait_ge(self.sFP[t % 3].h, m_fl[t])
                if t >= 2:
                    VEC.wait_ge(self.sOU[t % 2].h, self.sOU[t % 2].n)
                for oc in range(4):
                    ins = VEC.tensor_tensor(orow[par][:, 512 * oc:512 * (oc + 1)],
                                            py[4 * par + oc][:, :],
                                            fpt[t % 3][:, 512 * oc:512 * (oc + 1)],
                                            ALU.add)
                m_ev[t] = self.sVE2.inc(ins)
                SYN.wait_ge(self.sVE2.h, m_ev[t])
                ins = SYN.dma_start(out=self.out[128 * t:128 * (t + 1), :],
                                    in_=orow[par][:, :])
                self.sOU[par].inc(ins, 16)
            SYN.wait_ge(self.sOU[0].h, self.sOU[0].n)
            SYN.wait_ge(self.sOU[1].h, self.sOU[1].n)
        self.esB.close()

    def dbg_dump(self, tens):
        nc, SYN, GPS = self.nc, self.SYN, self.GPS
        with ExitStack() as esD:
            for i, (name, t, shape, via_sbuf) in enumerate(tens):
                o = nc.dram_tensor("dbg_" + name, shape, t.dtype if hasattr(t, "dtype")
                                   else F32, kind="ExternalOutput")
                if via_sbuf:
                    tmp = self.sb(esD, f"dbg_t{i}", [128, shape[1]], t.dtype)
                    for rr_ in range(0, shape[0], 128):
                        ins = SYN.dma_start(out=tmp[:, :], in_=t[rr_:rr_ + 128, :])
                        n = self.sOUT.inc(ins, 16)
                        SYN.wait_ge(self.sOUT.h, n)
                        ins = SYN.dma_start(out=o[rr_:rr_ + 128, :], in_=tmp[:, :])
                        n = self.sOUT.inc(ins, 16)
                        SYN.wait_ge(self.sOUT.h, n)
                else:
                    ins = SYN.dma_start(out=o[:, :], in_=t)
                    n = self.sOUT.inc(ins, 16)
                    SYN.wait_ge(self.sOUT.h, n)
            SYN.wait_ge(self.sOUT.h, self.sOUT.n)

    def build(self, stop=4):
        self.p0()
        self.p1()
        if stop == 1:
            self.dbg_dump([
                ("ktT", self.ktT[:, :], [128, 4096], 0),
                ("vpn", self.vpn[:, :], [128, 4096], 0),
                ("kvhn", self.kvhn[:, :], [128, 1024], 0),
                ("mkrow", self.mkrow[:, :], [1, 32], 0),
                ("knm", self.knm[:, :], [1, 4], 0),
                ("xnT", self.xnT[:, :], [128, 16 * 2048], 0),
            ])
            self.esG.close(); self.esTk.close(); self.esTq.close()
            self.esA.close(); self.es.close()
            return self.nc
        self.p25a()
        self.p2()
        if stop == 2:
            self.dbg_dump([
                ("qscr", self.q_scr[:, :], [128, H * 2048], 0),
                ("maxq", self.maxqrow[:, :], [1, 16], 0),
            ])
            self.esPG.close(); self.esF.close(); self.esTq.close()
            self.esA.close(); self.es.close()
            return self.nc
        self.p25b()
        if stop == 25:
            self.dbg_dump([("fps", self.fp_scr, [2048, E], 1)])
            self.es.close()
            return self.nc
        self.p3()
        if stop == 3:
            self.dbg_dump([("OT", self.OT[:, :], [128, H * 2048], 0)])
            self.esB.close()
            self.es.close()
            return self.nc
        if DBG_EXTRA:
            for s_ in self.sFB:
                self.SYN.wait_ge(s_.h, s_.n)
            self.dbg_dump([("fps", self.fp_scr, [2048, E], 1)])
        self.p4()
        self.es.close()
        return self.nc


DBG_EXTRA = False
_NC_CACHE = {}


def get_nc(stop=4):
    if stop not in _NC_CACHE:
        _NC_CACHE[stop] = KB().build(stop=stop)
    return _NC_CACHE[stop]


def kernel(L0_attn_x, L0_attn_Wq, L0_attn_Wkv, L0_attn_Wkvh, L0_attn_Wg,
           L0_attn_Wout, L0_ffn_w_gate, L0_ffn_w_up, L0_ffn_w_down,
           L0_ffn_s_gate, L0_ffn_s_up, L0_ffn_s_down):
    x = np.ascontiguousarray(np.asarray(L0_attn_x, dtype=np.float32))
    WqT = np.asarray(L0_attn_Wq, np.float32).T
    wq_h = np.ascontiguousarray(
        WqT.reshape(16, 128, 16, 128).transpose(2, 1, 0, 3).reshape(E, E))
    wkv_h = np.ascontiguousarray(
        np.asarray(L0_attn_Wkv, np.float32).T.reshape(16, 128, 256)
        .transpose(1, 0, 2).reshape(128, 16 * 256))
    wsm_full = np.concatenate([np.asarray(L0_attn_Wkvh, np.float32).T,
                               np.asarray(L0_attn_Wg, np.float32).T], axis=1)
    wsm_h = np.ascontiguousarray(
        wsm_full.reshape(16, 128, 64).transpose(1, 0, 2).reshape(128, 16 * 64))
    wout_h = np.ascontiguousarray(np.asarray(L0_attn_Wout, np.float32).T)
    wgt_h = np.ascontiguousarray(
        np.asarray(L0_ffn_w_gate, np.int8).reshape(16, 128, 64, 128)
        .transpose(2, 1, 0, 3).reshape(INTER, E))
    wut_h = np.ascontiguousarray(
        np.asarray(L0_ffn_w_up, np.int8).reshape(16, 128, 64, 128)
        .transpose(2, 1, 0, 3).reshape(INTER, E))
    wdn_h = np.ascontiguousarray(np.asarray(L0_ffn_w_down, np.int8))
    sg_h = np.ascontiguousarray(np.asarray(L0_ffn_s_gate, np.float32).reshape(64, 128).T)
    su_h = np.ascontiguousarray(np.asarray(L0_ffn_s_up, np.float32).reshape(64, 128).T)
    sd_h = np.ascontiguousarray(np.asarray(L0_ffn_s_down, np.float32).reshape(1, E))

    nc = get_nc()
    in_maps = []
    rowmaps = []
    for c in range(8):
        b, half = divmod(c, 2)
        rows = (np.r_[0:1024, 3072:4096] if half == 0 else np.r_[1024:3072])
        rowmaps.append(rows)
        in_maps.append({
            "xb": np.ascontiguousarray(x[b]),
            "x_own": np.ascontiguousarray(x[b][rows]),
            "pos_own": np.ascontiguousarray(rows.astype(np.float32).reshape(1, 2048)),
            "cfg": np.array([[half, 0, 0, 0]], np.int32),
            "wq": wq_h, "wkv": wkv_h, "wsm": wsm_h, "wout": wout_h,
            "wgt": wgt_h, "wut": wut_h, "wdn": wdn_h,
            "sgv": sg_h, "suv": su_h, "sdv": sd_h,
        })
    res = run_bass_kernel_spmd(nc, in_maps, core_ids=list(range(8)))
    outp = np.empty((4, NSEQ, E), np.float32)
    for c in range(8):
        b = c // 2
        outp[b][rowmaps[c]] = res.results[c]["out"]
    return outp


# version 11
# speedup vs baseline: 1.1517x; 1.0168x over previous
import math
import numpy as np
from contextlib import ExitStack

import concourse.bass as bass
import concourse.mybir as mybir
from concourse.bass_utils import run_bass_kernel_spmd

F32 = mybir.dt.float32
BF16 = mybir.dt.bfloat16
I8 = mybir.dt.int8
I32 = mybir.dt.int32
AF = mybir.ActivationFunctionType
ALU = mybir.AluOpType
AX = mybir.AxisListType

PI = float(np.pi)
CW1 = 6.28125
CW2 = 2 * PI - CW1
E = 2048
H = 16
D = 128
NSEQ = 4096
INTER = 8192
TGABS = {0: [0, 512, 3072, 3584], 1: [1024, 1536, 2048, 2560]}
SLOT_LNT = [3, 2, 1, 0]
OWN_GROUPS = {0: [0, 1, 6, 7], 1: [2, 3, 4, 5]}
COL_EPS, COL_NHALF, COL_ONE, COL_NEG1, COL_LN1E4, COL_HALFC, COL_I2PI = range(7)


class Sem:
    def __init__(self, h):
        self.h = h
        self.n = 0

    def inc(self, ins, k=1):
        ins.then_inc(self.h, k)
        self.n += k
        return self.n


def pair_list(half):
    pairs = []
    for si, tg in enumerate(SLOT_LNT):
        a0 = TGABS[half][tg]
        mcN = (a0 + 512) // 128
        for mc in range(mcN):
            dv = mc - a0 // 128
            pairs.append(dict(si=si, tg=tg, mc=mc,
                              dv=dv if 0 <= dv <= 3 else None,
                              first=(mc == 0), last=(mc == mcN - 1)))
    return pairs


class KB:
    def __init__(self):
        nc = bass.Bass("TRN2", target_bir_lowering=False, debug=False)
        self.nc = nc
        self.TEN, self.VEC, self.SCA, self.GPS, self.SYN = \
            nc.tensor, nc.vector, nc.scalar, nc.gpsimd, nc.sync
        dt = nc.dram_tensor
        self.xb = dt("xb", [NSEQ, E], F32, kind="ExternalInput")
        self.x_own = dt("x_own", [2048, E], F32, kind="ExternalInput")
        self.pos_own = dt("pos_own", [1, 2048], F32, kind="ExternalInput")
        self.cfg = dt("cfg", [1, 4], I32, kind="ExternalInput")
        self.wq = dt("wq", [E, E], F32, kind="ExternalInput")
        self.wkv = dt("wkv", [128, 16 * 256], F32, kind="ExternalInput")
        self.wsm = dt("wsm", [128, 16 * 64], F32, kind="ExternalInput")
        self.wout = dt("wout", [E, E], F32, kind="ExternalInput")
        self.wgt = dt("wgt", [INTER, E], I8, kind="ExternalInput")
        self.wut = dt("wut", [INTER, E], I8, kind="ExternalInput")
        self.wdn = dt("wdn", [INTER, E], I8, kind="ExternalInput")
        self.sgv = dt("sgv", [128, 64], F32, kind="ExternalInput")
        self.suv = dt("suv", [128, 64], F32, kind="ExternalInput")
        self.sdv = dt("sdv", [1, E], F32, kind="ExternalInput")
        self.out = dt("out", [2048, E], F32, kind="ExternalOutput")
        self.q_scr = dt("q_scr", [128, H * 2048], BF16, kind="Internal")
        self.fp_scr = dt("fp_scr", [2048, E], F32, kind="Internal")
        self.kt_scr = dt("kt_scr", [128, 4096], BF16, kind="Internal")
        self.vp_scr = dt("vp_scr", [128, 4096], BF16, kind="Internal")
        self.kn_scr = dt("kn_scr", [128, 1024], F32, kind="Internal")

        self.es = ExitStack()
        self.sems = {k: Sem(self.es.enter_context(nc.semaphore(k))) for k in
                     ["sXL", "sWL", "sQS", "sQL", "sFS", "sFL", "sOUT", "sG2",
                      "sPE", "sSC", "sVE", "sVE2", "sGP", "sMS", "sSE", "sSEs"]}
        self.sSE = self.sems["sSE"]
        def mksems(prefix, k):
            return [Sem(self.es.enter_context(nc.semaphore(f"{prefix}{i}")))
                    for i in range(k)]
        self.sXLs = mksems("sXLs", 3)
        self.sWQ = mksems("sWQ", 2)
        self.sQSs = mksems("sQSs", 2)
        self.sWG = mksems("sWG", 2)
        self.sWD = mksems("sWD", 2)
        self.sXO = mksems("sXO", 2)
        self.sFB = mksems("sFB", 4)
        self.sQC = mksems("sQC", 2)
        self.sFP = mksems("sFP", 3)
        self.sOU = mksems("sOU", 2)
        self.sSEg = Sem(self.es.enter_context(nc.semaphore("sSEg")))
        for k, v in self.sems.items():
            setattr(self, k, v)

    def sb(self, st, name, shape, dtt):
        return st.enter_context(self.nc.sbuf_tensor(name, shape, dtt))

    def ps(self, st, name, shape, dtt):
        return st.enter_context(self.nc.psum_tensor(name, shape, dtt))

    def se(self, eng, ins):
        s = (self.sSEg if eng is self.GPS
             else self.sems["sSEs"] if eng is self.SCA else self.sSE)
        n = s.inc(ins)
        eng.wait_ge(s.h, n)

    def snap(self):
        return {k: s.n for k, s in self.sems.items()}

    def restore(self, st):
        for k, s in self.sems.items():
            s.n = st[k]

    def p0(self):
        nc, TEN, VEC, SCA, GPS, SYN = self.nc, self.TEN, self.VEC, self.SCA, self.GPS, self.SYN
        es = self.es
        self.c_ = self.sb(es, "c_", [128, 8], F32)
        self.onesb_c = self.sb(es, "onesb_c", [128, 1], BF16)
        self.onesb_r = self.sb(es, "onesb_r", [1, 128], BF16)
        self.onesf_r = self.sb(es, "onesf_r", [1, 128], F32)
        self.identb = self.sb(es, "identb", [128, 128], BF16)
        self.identf = self.sb(es, "identf", [128, 128], F32)
        self.cfgs = self.sb(es, "cfgs", [1, 4], I32)
        self.maxqrow = self.sb(es, "maxqrow", [1, 16], F32)
        self.mkrow = self.sb(es, "mkrow", [1, 32], F32)
        self.knm = self.sb(es, "knm", [1, 4], F32)
        self.b11 = self.sb(es, "b11", [1, 8], F32)
        c_ = self.c_

        ins = SYN.dma_start(out=self.cfgs[:, :], in_=self.cfg[:, :])
        self.sMS.inc(ins, 16)
        for col, val in [(COL_EPS, 1e-5), (COL_NHALF, -0.5), (COL_ONE, 1.0),
                         (COL_NEG1, -1.0), (COL_LN1E4, -math.log(1e4) / 64.0),
                         (COL_HALFC, 0.5), (COL_I2PI, 1.0 / (2 * PI))]:
            self.sGP.inc(GPS.memset(c_[:, col:col + 1], val))
        self.sGP.inc(GPS.memset(self.onesb_c[:, :], 1.0))
        self.sGP.inc(GPS.memset(self.onesb_r[:, :], 1.0))
        self.sGP.inc(GPS.memset(self.onesf_r[:, :], 1.0))
        self.sGP.inc(GPS.memset(self.maxqrow[:, :], 0.0))
        self.sGP.inc(GPS.memset(self.knm[:, :], 0.0))
        ins = GPS.memset(self.identf[:, :], 1.0)
        self.se(GPS, ins)
        ins = GPS.affine_select(self.identf[:, :], self.identf[:, :], [[1, 128]],
                                ALU.is_equal, 0.0, base=0, channel_multiplier=-1)
        self.sGP.inc(ins)
        ins = GPS.memset(self.identb[:, :], 1.0)
        self.se(GPS, ins)
        ins = GPS.affine_select(self.identb[:, :], self.identb[:, :], [[1, 128]],
                                ALU.is_equal, 0.0, base=0, channel_multiplier=-1)
        self.sGP.inc(ins)

        self.regs = {}
        for nm, e in [("t", TEN), ("v", VEC), ("s", SCA), ("g", GPS)]:
            e.wait_ge(self.sMS.h, 16)
            r = e.alloc_register("rhalf_" + nm)
            e.reg_load(r, self.cfgs[0:1, 0:1])
            self.regs[nm] = r

        self.esA = ExitStack()
        self.xnT = self.sb(self.esA, "xnT", [128, 16 * 2048], BF16)
        self.esTq = ExitStack()
        self.tabq = self.sb(self.esTq, "tabq", [128, 2048], F32)
        self.esTk = ExitStack()
        self.tabk = self.sb(self.esTk, "tabk", [128, 4096], F32)
        self.esG = ExitStack()
        self.ktT = self.sb(self.esG, "ktT", [128, 4096], BF16)
        self.vpn = self.sb(self.esG, "vpn", [128, 4096], BF16)
        self.kvhn = self.sb(self.esG, "kvhn", [128, 1024], F32)

        with ExitStack() as esP:
            ang = self.sb(esP, "p0_ang", [64, 4096], F32)
            angq = self.sb(esP, "p0_angq", [64, 2048], F32)
            tb1 = self.sb(esP, "p0_tb1", [64, 4096], F32)
            tb2 = self.sb(esP, "p0_tb2", [64, 4096], F32)
            tbi = self.sb(esP, "p0_tbi", [64, 4096], I32)
            invj = self.sb(esP, "p0_inv", [64, 1], F32)
            jcol = self.sb(esP, "p0_jcol", [64, 1], F32)
            posq = self.sb(esP, "p0_posq", [1, 2048], F32)
            pp0 = self.ps(esP, "pp0", [64, 512], F32)

            SYN.wait_ge(self.sMS.h, self.sMS.n)
            ins = SYN.dma_start(out=posq[:, :], in_=self.pos_own[:, :])
            self.sMS.inc(ins, 16)
            TEN.wait_ge(self.sGP.h, self.sGP.n)
            VEC.wait_ge(self.sGP.h, self.sGP.n)
            self.sGP.inc(GPS.iota(jcol[:, :], [[1, 1]], base=0, channel_multiplier=1,
                                   allow_small_or_imprecise_dtypes=True))
            self.sGP.inc(GPS.iota(ang[:, :], [[1, 4096]], base=0, channel_multiplier=0,
                                  allow_small_or_imprecise_dtypes=True))
            self.sGP.inc(GPS.memset(self.ktT[:, :], 0.0))
            self.sGP.inc(GPS.memset(self.vpn[:, :], 0.0))
            self.sGP.inc(GPS.memset(self.kvhn[:, :], 0.0))

            SCA.wait_ge(self.sGP.h, self.sGP.n)
            TEN.wait_ge(self.sGP.h, self.sGP.n)
            VEC.wait_ge(self.sGP.h, self.sGP.n)
            ins = SCA.activation(invj[:, :], jcol[:, :], AF.Exp,
                                 scale=c_[0:64, COL_LN1E4:COL_LN1E4 + 1])
            self.sSC.inc(ins)
            VEC.wait_ge(self.sSC.h, self.sSC.n)
            ins = VEC.tensor_scalar_mul(ang[:, :], ang[:, :], invj[:, :])
            self.sVE.inc(ins)
            VEC.wait_ge(self.sVE.h, self.sVE.n)
            for i in range(4):
                TEN.wait_ge(self.sMS.h, self.sMS.n)
                if i > 0:
                    TEN.wait_ge(self.sVE.h, self.sVE.n)
                ins = TEN.matmul(pp0[:, :], self.onesf_r[0:1, 0:64],
                                 posq[0:1, 512 * i:512 * (i + 1)], start=True, stop=True)
                self.sPE.inc(ins)
                VEC.wait_ge(self.sPE.h, self.sPE.n)
                ins = VEC.tensor_copy(angq[:, 512 * i:512 * (i + 1)], pp0[:, :])
                self.sVE.inc(ins)
                VEC.wait_ge(self.sVE.h, self.sVE.n)
            ins = VEC.tensor_scalar_mul(angq[:, :], angq[:, :], invj[:, :])
            self.se(VEC, ins)

            def sin_into(dst, src, n, add_half_pi, post_scale=None):
                a = tb1[:, 0:n]
                if add_half_pi:
                    ins = VEC.tensor_scalar(a, src[:, 0:n], PI / 2, None, ALU.add)
                else:
                    ins = VEC.tensor_copy(a, src[:, 0:n])
                self.se(VEC, ins)
                t_ = tb2[:, 0:n]
                ins = VEC.tensor_scalar_mul(t_, a, c_[0:64, COL_I2PI:COL_I2PI + 1])
                self.se(VEC, ins)
                ins = VEC.tensor_copy(tbi[:, 0:n], t_)
                self.se(VEC, ins)
                ins = VEC.tensor_copy(t_, tbi[:, 0:n])
                self.se(VEC, ins)
                ins = VEC.scalar_tensor_tensor(a, t_, -CW1, a, ALU.mult, ALU.add)
                self.se(VEC, ins)
                ins = VEC.scalar_tensor_tensor(a, t_, -CW2, a, ALU.mult, ALU.add)
                self.se(VEC, ins)
                ins = VEC.tensor_scalar(t_, a, PI, None, ALU.is_gt)
                self.se(VEC, ins)
                ins = VEC.scalar_tensor_tensor(a, t_, -2 * PI, a, ALU.mult, ALU.add)
                self.se(VEC, ins)
                ins = VEC.tensor_scalar(t_, a, -PI, None, ALU.is_lt)
                self.se(VEC, ins)
                ins = VEC.scalar_tensor_tensor(a, t_, 2 * PI, a, ALU.mult, ALU.add)
                self.sVE.inc(ins)
                SCA.wait_ge(self.sVE.h, self.sVE.n)
                ins = SCA.activation(dst, a, AF.Sin)
                self.sSC.inc(ins)
                VEC.wait_ge(self.sSC.h, self.sSC.n)
                if post_scale is not None:
                    ins = VEC.tensor_scalar(dst, dst, post_scale, None, ALU.mult)
                    self.sVE.inc(ins)

            s = 1.0 / math.sqrt(D)
            sin_into(self.tabk[64:128, 0:4096], ang, 4096, False, s)
            sin_into(self.tabk[0:64, 0:4096], ang, 4096, True, s)
            sin_into(self.tabq[64:128, 0:2048], angq, 2048, False)
            sin_into(self.tabq[0:64, 0:2048], angq, 2048, True)
            ins = VEC.tensor_copy(tb1[0:1, 0:1], tb1[0:1, 0:1])
            self.sVE.inc(ins)
            for eng in (TEN, SCA, GPS, SYN, VEC):
                eng.wait_ge(self.sVE.h, self.sVE.n)
                eng.wait_ge(self.sGP.h, self.sGP.n)
                eng.wait_ge(self.sSC.h, self.sSC.n)

    def p1(self):
        nc, TEN, VEC, SCA, GPS, SYN = self.nc, self.TEN, self.VEC, self.SCA, self.GPS, self.SYN
        c_ = self.c_
        esP1o = ExitStack()
        kvhgT = self.sb(esP1o, "p1_kvhg", [64, 4096], F32)
        with ExitStack() as esP:
            xt = [self.sb(esP, f"p1_xt{i}", [128, E], F32) for i in range(3)]
            xnb = [self.sb(esP, f"p1_xnb{i}", [128, E], BF16) for i in range(2)]
            sqd = [self.sb(esP, "p1_sqd0", [128, E], F32)] * 2
            xnt_tmp = self.sb(esP, "p1_xnt_tmp", [128, 16 * 512], BF16)
            wkv_sb = self.sb(esP, "p1_wkv", [128, 16 * 256], BF16)
            wsm_sb = self.sb(esP, "p1_wsm", [128, 16 * 64], BF16)
            tmpv = self.sb(esP, "p1_tmpv", [128, 512], BF16)
            rp1 = self.sb(esP, "p1_rp1", [64, 512], F32)
            rp2 = self.sb(esP, "p1_rp2", [64, 512], F32)
            rp3 = self.sb(esP, "p1_rp3", [64, 512], F32)
            rp4 = self.sb(esP, "p1_rp4", [64, 512], F32)
            st1 = [self.sb(esP, f"p1_s1{i}", [128, 1], F32) for i in range(2)]
            st2 = [self.sb(esP, f"p1_s2{i}", [128, 1], F32) for i in range(2)]
            stm = [self.sb(esP, f"p1_sm{i}", [128, 1], F32) for i in range(2)]
            str_ = [self.sb(esP, f"p1_sr{i}", [128, 1], F32) for i in range(2)]
            pxp = [self.ps(esP, f"p1_pxp{i}", [128, 512], BF16) for i in range(2)]
            pkp = self.ps(esP, "p1_pkp", [128, 512], F32)
            pvp = self.ps(esP, "p1_pvp", [128, 512], F32)
            pkg = self.ps(esP, "p1_pkg", [64, 512], F32)
            pvt = self.ps(esP, "p1_pvt", [128, 512], BF16)

            ins = GPS.dma_start(out=wkv_sb[:, :], in_=self.wkv[:, :])
            self.sWL.inc(ins, 16)
            GPS.wait_ge(self.sWL.h, self.sWL.n)
            ins = GPS.dma_start(out=wsm_sb[:, :], in_=self.wsm[:, :])
            wl_mark = self.sWL.inc(ins, 16)
            TEN.wait_ge(self.sWL.h, wl_mark)

            m_xn = {}
            m_sq = {}
            m_ev = {}
            m_tb = {}
            m_var = {}
            m_kp = {}
            m_vp = {}
            m_kg = {}
            m_vt = {}
            m_grp = {}
            GB = 0
            for g in range(8):
                for tt in range(4):
                    i = 4 * g + tt
                    if i >= 3:
                        SYN.wait_ge(self.sVE.h, m_xn[i - 3])
                        SYN.wait_ge(self.sSC.h, m_sq[i - 3])
                    ins = SYN.dma_start(
                        out=xt[i % 3][:, :],
                        in_=self.xb[512 * g + 128 * tt: 512 * g + 128 * (tt + 1), :])
                    xl = self.sXLs[i % 3].inc(ins, 16)
                    SCA.wait_ge(self.sXLs[i % 3].h, xl)
                    if i >= 1:
                        SCA.wait_ge(self.sVE2.h, m_var[i - 1])
                    ins = SCA.activation(sqd[i % 2][:, :], xt[i % 3][:, :], AF.Square)
                    m_sq[i] = self.sSC.inc(ins)
                    VEC.wait_ge(self.sXLs[i % 3].h, xl)
                    ins = VEC.reduce_sum(st1[i % 2][:, :], xt[i % 3][:, :], AX.X)
                    self.se(VEC, ins)
                    VEC.wait_ge(self.sSC.h, m_sq[i])
                    ins = VEC.reduce_sum(st2[i % 2][:, :], sqd[i % 2][:, :], AX.X)
                    self.se(VEC, ins)
                    ins = VEC.tensor_scalar(stm[i % 2][:, :], st1[i % 2][:, :], 1.0 / E,
                                             None, ALU.mult)
                    self.se(VEC, ins)
                    ins = VEC.tensor_scalar(st2[i % 2][:, :], st2[i % 2][:, :], 1.0 / E,
                                            None, ALU.mult)
                    self.se(VEC, ins)
                    ins = VEC.tensor_tensor(st1[i % 2][:, :], stm[i % 2][:, :],
                                            stm[i % 2][:, :], ALU.mult)
                    self.se(VEC, ins)
                    ins = VEC.tensor_tensor(st2[i % 2][:, :], st2[i % 2][:, :],
                                            st1[i % 2][:, :], ALU.subtract)
                    m_var[i] = self.sVE2.inc(ins)
                    SCA.wait_ge(self.sVE2.h, m_var[i])
                    ins = SCA.activation(st1[i % 2][:, :], st2[i % 2][:, :], AF.Ln,
                                         bias=c_[:, COL_EPS:COL_EPS + 1])
                    self.se(SCA, ins)
                    ins = SCA.activation(str_[i % 2][:, :], st1[i % 2][:, :], AF.Exp,
                                         scale=c_[:, COL_NHALF:COL_NHALF + 1])
                    rstd_m = self.sSC.inc(ins)
                    VEC.wait_ge(self.sSC.h, rstd_m)
                    ins = VEC.tensor_scalar(xnb[i % 2][:, :], xt[i % 3][:, :],
                                            stm[i % 2][:, :], str_[i % 2][:, :],
                                            ALU.subtract, ALU.mult)
                    m_xn[i] = self.sVE.inc(ins)
                    TEN.wait_ge(self.sVE.h, m_xn[i])
                    for bch in range(4):
                        if GB >= 2:
                            TEN.wait_ge(self.sVE2.h, m_ev[GB - 2])
                        last = None
                        for k in range(4):
                            ec = 4 * bch + k
                            last = TEN.transpose(
                                pxp[GB % 2][:, 128 * k:128 * (k + 1)],
                                xnb[i % 2][:, 128 * ec:128 * (ec + 1)],
                                self.identb[:, :])
                        m_tb[GB] = self.sPE.inc(last)
                        VEC.wait_ge(self.sPE.h, m_tb[GB])
                        if g > 0:
                            VEC.wait_ge(self.sPE.h, m_kg[g - 1])
                        dst = xnt_tmp[:, :].rearrange("p (c r) -> p c r", c=16)[
                            :, 4 * bch:4 * bch + 4, 128 * tt:128 * (tt + 1)]
                        src = pxp[GB % 2][:, :].rearrange("p (c r) -> p c r", c=4)
                        ins = VEC.tensor_copy(dst, src)
                        m_ev[GB] = self.sVE2.inc(ins)
                        GB += 1
                TEN.wait_ge(self.sVE2.h, m_ev[GB - 1])
                if g > 0:
                    TEN.wait_ge(self.sG2.h, m_grp[g - 1])
                last = None
                for cc in range(16):
                    last = TEN.matmul(pkp[:, :], wkv_sb[:, 256 * cc:256 * cc + 128],
                                      xnt_tmp[:, 512 * cc:512 * (cc + 1)],
                                      start=(cc == 0), stop=(cc == 15))
                m_kp[g] = self.sPE.inc(last)
                for cc in range(16):
                    last = TEN.matmul(pvp[:, :], wkv_sb[:, 256 * cc + 128:256 * (cc + 1)],
                                      xnt_tmp[:, 512 * cc:512 * (cc + 1)],
                                      start=(cc == 0), stop=(cc == 15))
                m_vp[g] = self.sPE.inc(last)
                for cc in range(16):
                    last = TEN.matmul(pkg[:, :], wsm_sb[:, 64 * cc:64 * (cc + 1)],
                                      xnt_tmp[:, 512 * cc:512 * (cc + 1)],
                                      start=(cc == 0), stop=(cc == 15))
                m_kg[g] = self.sPE.inc(last)
                if g > 0:
                    TEN.wait_ge(self.sG2.h, m_grp[g - 1])
                    last = None
                    for k in range(4):
                        last = TEN.transpose(pvt[:, 128 * k:128 * (k + 1)],
                                             tmpv[:, 128 * k:128 * (k + 1)],
                                             self.identb[:, :])
                    m_vt[g - 1] = self.sPE.inc(last)
                cS = self.tabk[0:64, 512 * g:512 * (g + 1)]
                sS = self.tabk[64:128, 512 * g:512 * (g + 1)]
                VEC.wait_ge(self.sPE.h, m_kp[g])
                self.sSE.inc(VEC.tensor_tensor(rp1[:, :], pkp[0:64, :], cS, ALU.mult))
                self.sSE.inc(VEC.tensor_tensor(rp2[:, :], pkp[64:128, :], sS, ALU.mult))
                self.sSE.inc(VEC.tensor_tensor(rp3[:, :], pkp[64:128, :], cS, ALU.mult))
                self.sSE.inc(VEC.tensor_tensor(rp4[:, :], pkp[0:64, :], sS, ALU.mult))
                VEC.wait_ge(self.sSE.h, self.sSE.n)
                self.sSE.inc(VEC.tensor_tensor(self.ktT[0:64, 512 * g:512 * (g + 1)],
                                               rp1[:, :], rp2[:, :], ALU.subtract))
                self.sSE.inc(VEC.tensor_tensor(self.ktT[64:128, 512 * g:512 * (g + 1)],
                                               rp3[:, :], rp4[:, :], ALU.add))
                VEC.wait_ge(self.sSE.h, self.sSE.n)
                VEC.wait_ge(self.sPE.h, m_vp[g])
                if g > 0:
                    VEC.wait_ge(self.sPE.h, m_vt[g - 1])
                self.sG2.inc(VEC.tensor_copy(tmpv[:, :], pvp[:, :]))
                VEC.wait_ge(self.sPE.h, m_kg[g])
                self.sG2.inc(VEC.tensor_copy(kvhgT[:, 512 * g:512 * (g + 1)], pkg[:, :]))
                if g > 0:
                    self.sG2.inc(VEC.tensor_copy(self.vpn[:, 512 * (g - 1):512 * g],
                                                 pvt[:, :]))
                stH = self.snap()
                endsH = []
                for half in (0, 1):
                    self.restore(stH)
                    ctx = (VEC.If_eq(self.regs["v"], 0) if half == 0 else VEC.Else())
                    with ctx:
                        if g in OWN_GROUPS[half]:
                            loc = OWN_GROUPS[half].index(g) * 512
                            dst = self.xnT[:, :].rearrange("p (c r) -> p c r", c=16)[
                                :, :, loc:loc + 512]
                            src = xnt_tmp[:, :].rearrange("p (c r) -> p c r", c=16)
                            ins = VEC.tensor_copy(dst, src)
                        else:
                            ins = VEC.tensor_copy(rp1[0:1, 0:1], rp1[0:1, 0:1])
                        m_grp[g] = self.sG2.inc(ins)
                    endsH.append(self.snap())
                assert endsH[0] == endsH[1]
            TEN.wait_ge(self.sG2.h, m_grp[7])
            last = None
            for k in range(4):
                last = TEN.transpose(pvt[:, 128 * k:128 * (k + 1)],
                                     tmpv[:, 128 * k:128 * (k + 1)], self.identb[:, :])
            m_vt[7] = self.sPE.inc(last)
            VEC.wait_ge(self.sPE.h, m_vt[7])
            ins = VEC.tensor_copy(self.vpn[:, 512 * 7:512 * 8], pvt[:, :])
            self.sG2.inc(ins)

            for eng in (self.TEN, self.SCA, self.GPS, self.SYN):
                eng.wait_ge(self.sG2.h, self.sG2.n)
        with ExitStack() as esP:
            bufA = self.sb(esP, "p1_bA", [32, 4096], F32)
            bufB = self.sb(esP, "p1_bB", [32, 4096], F32)
            bufC = self.sb(esP, "p1_bC", [32, 4096], F32)
            self.S2 = self.sb(esP, "p1_S2", [32, 4096], F32)
            mkcol = self.sb(esP, "p1_mkc", [32, 1], F32)
            sqb = self.sb(esP, "p1_sqb", [128, 512], BF16)
            pkn = self.ps(esP, "p1_pkn", [1, 512], F32)
            pmt = self.ps(esP, "p1_pmt", [128, 128], F32)

            gT = kvhgT[32:64, :]
            kvT = kvhgT[0:32, :]
            SCA.wait_ge(self.sG2.h, self.sG2.n)
            ins = SCA.activation(bufA[:, :], gT, AF.Abs)
            self.se(SCA, ins)
            ins = SCA.activation(bufB[:, :], bufA[:, :], AF.Exp,
                                 scale=c_[0:32, COL_NEG1:COL_NEG1 + 1])
            self.se(SCA, ins)
            ins = SCA.activation(bufA[:, :], bufB[:, :], AF.Ln,
                                 bias=c_[0:32, COL_ONE:COL_ONE + 1])
            ls_m = self.sSC.inc(ins)
            VEC.wait_ge(self.sSC.h, ls_m)
            ins = VEC.tensor_scalar_min(bufC[:, :], gT, 0.0)
            self.se(VEC, ins)
            ins = VEC.tensor_tensor(bufB[:, :], bufC[:, :], bufA[:, :], ALU.subtract)
            self.se(VEC, ins)
            ins = VEC.tensor_tensor_scan(bufC[:, :], bufB[:, :], bufB[:, :], 0.0,
                                         ALU.add, ALU.bypass)
            self.se(VEC, ins)
            ins = VEC.tensor_tensor(bufA[:, :], bufC[:, :], bufB[:, :], ALU.subtract)
            lc_m = self.sVE.inc(ins)
            SCA.wait_ge(self.sVE.h, lc_m)
            ins = SCA.activation(bufC[:, :], bufA[:, :], AF.Exp)
            e_m = self.sSC.inc(ins)
            VEC.wait_ge(self.sSC.h, e_m)
            ins = VEC.tensor_tensor(bufB[:, :], kvT, bufC[:, :], ALU.mult)
            self.se(VEC, ins)
            ins = VEC.tensor_tensor_scan(self.S2[:, :], bufB[:, :], bufB[:, :], 0.0,
                                         ALU.add, ALU.bypass)
            self.se(VEC, ins)
            ins = VEC.reduce_max(mkcol[:, :], self.S2[:, :], AX.X,
                                 apply_absolute_value=True)
            s2_m = self.sVE.inc(ins)

            TEN.wait_ge(self.sVE.h, s2_m)
            for b4 in range(8):
                if b4 >= 1:
                    TEN.wait_ge(self.sVE2.h, self.sVE2.n)
                last = None
                for k in range(4):
                    mc = 4 * b4 + k
                    last = TEN.transpose(pmt[:, 32 * k:32 * (k + 1)],
                                         self.S2[:, 128 * mc:128 * (mc + 1)],
                                         self.identf[0:32, 0:32])
                pe_m = self.sPE.inc(last)
                VEC.wait_ge(self.sPE.h, pe_m)
                ins = VEC.tensor_copy(self.kvhn[:, 128 * b4:128 * (b4 + 1)], pmt[:, :])
                self.sVE2.inc(ins)
            TEN.wait_ge(self.sVE2.h, self.sVE2.n)
            last = TEN.transpose(pmt[0:1, 0:32],
                                 mkcol[:, :], self.identf[0:32, 0:32])
            pe_m = self.sPE.inc(last)
            VEC.wait_ge(self.sPE.h, pe_m)
            ins = VEC.tensor_copy(self.mkrow[:, :], pmt[0:1, 0:32])
            self.sVE2.inc(ins)
            for ch in range(8):
                SCA.wait_ge(self.sVE2.h, self.sVE2.n)
                SCA.wait_ge(self.sG2.h, self.sG2.n)
                ins = SCA.activation(sqb[:, :], self.ktT[:, 512 * ch:512 * (ch + 1)],
                                     AF.Square)
                sc_m = self.sSC.inc(ins)
                TEN.wait_ge(self.sSC.h, sc_m)
                if ch >= 1:
                    TEN.wait_ge(self.sVE2.h, self.sVE2.n)
                ins = TEN.matmul(pkn[:, :], self.onesb_c[:, :], sqb[:, :],
                                 start=True, stop=True)
                pe_m = self.sPE.inc(ins)
                VEC.wait_ge(self.sPE.h, pe_m)
                ins = VEC.reduce_max(self.knm[0:1, 1:2], pkn[:, :], AX.X)
                self.se(VEC, ins)
                ins = VEC.tensor_tensor(self.knm[0:1, 0:1], self.knm[0:1, 0:1],
                                        self.knm[0:1, 1:2], ALU.max)
                n = self.sVE2.inc(ins)
                VEC.wait_ge(self.sVE2.h, n)
            for eng in (TEN, SCA, GPS, SYN):
                eng.wait_ge(self.sVE2.h, self.sVE2.n)
                eng.wait_ge(self.sVE.h, self.sVE.n)
        esP1o.close()

    def p2(self):
        nc, TEN, VEC, SCA, GPS, SYN = self.nc, self.TEN, self.VEC, self.SCA, self.GPS, self.SYN
        with ExitStack() as esP:
            wqb = [self.sb(esP, f"p2_wq{i}", [128, E], BF16) for i in range(2)]
            qro = [self.sb(esP, f"p2_qr{i}", [128, 512], BF16) for i in range(2)]
            sqb = [self.sb(esP, f"p2_sq{i}", [128, 512], BF16) for i in range(2)]
            rp1 = self.sb(esP, "p2_rp1", [64, 512], F32)
            rp2 = self.sb(esP, "p2_rp2", [64, 512], F32)
            rp3 = self.sb(esP, "p2_rp3", [64, 512], F32)
            rp4 = self.sb(esP, "p2_rp4", [64, 512], F32)
            mx1 = self.sb(esP, "p2_mx1", [1, 4], F32)
            pq = [self.ps(esP, f"p2_pq{i}", [128, 512], F32) for i in range(2)]
            pq2 = self.ps(esP, "p2_pq2", [1, 512], F32)

            m_wl = {}
            m_mm = {}
            m_rope = {}
            m_sq = {}
            m_q2 = {}
            m_qs = {}
            m_mx = {}
            GPS.wait_ge(self.sPE.h, self.f_sdrfree)
            for och in range(16):
                if och >= 2:
                    GPS.wait_ge(self.sPE.h, m_mm[(och - 2, 3)])
                ins = GPS.dma_start(out=wqb[och % 2][:, :],
                                    in_=self.wq[128 * och:128 * (och + 1), :])
                m_wl[och] = self.sWQ[och % 2].inc(ins, 16)
                TEN.wait_ge(self.sWQ[och % 2].h, m_wl[och])
                for tg in range(4):
                    u = 4 * och + tg
                    if u >= 2:
                        TEN.wait_ge(self.sVE.h, m_rope[u - 2])
                    last = None
                    for cc in range(16):
                        last = TEN.matmul(pq[u % 2][:, :],
                                          wqb[och % 2][:, 128 * cc:128 * (cc + 1)],
                                          self.xnT[:, 2048 * cc + 512 * tg:
                                                   2048 * cc + 512 * (tg + 1)],
                                          start=(cc == 0), stop=(cc == 15))
                    m_mm[(och, tg)] = self.sPE.inc(last)
                    VEC.wait_ge(self.sPE.h, m_mm[(och, tg)])
                    if u >= 2:
                        VEC.wait_ge(self.sQSs[u % 2].h, m_qs[u - 2])
                        VEC.wait_ge(self.sSC.h, m_sq[u - 2])
                    cS = self.tabq[0:64, 512 * tg:512 * (tg + 1)]
                    sS = self.tabq[64:128, 512 * tg:512 * (tg + 1)]
                    self.sSE.inc(VEC.tensor_tensor(rp1[:, :], pq[u % 2][0:64, :], cS, ALU.mult))
                    self.sSE.inc(VEC.tensor_tensor(rp2[:, :], pq[u % 2][64:128, :], sS, ALU.mult))
                    self.sSE.inc(VEC.tensor_tensor(rp3[:, :], pq[u % 2][64:128, :], cS, ALU.mult))
                    self.sSE.inc(VEC.tensor_tensor(rp4[:, :], pq[u % 2][0:64, :], sS, ALU.mult))
                    VEC.wait_ge(self.sSE.h, self.sSE.n)
                    ins = VEC.tensor_tensor(qro[u % 2][0:64, :], rp1[:, :], rp2[:, :],
                                            ALU.subtract)
                    self.sVE.inc(ins)
                    ins = VEC.tensor_tensor(qro[u % 2][64:128, :], rp3[:, :], rp4[:, :],
                                            ALU.add)
                    m_rope[u] = self.sVE.inc(ins)
                    GPS.wait_ge(self.sVE.h, m_rope[u])
                    ins = GPS.dma_start(
                        out=self.q_scr[:, 2048 * och + 512 * tg:
                                       2048 * och + 512 * (tg + 1)],
                        in_=qro[u % 2][:, :])
                    m_qs[u] = self.sQSs[u % 2].inc(ins, 16)
                    SCA.wait_ge(self.sVE.h, m_rope[u])
                    if u >= 1:
                        SCA.wait_ge(self.sPE.h, m_q2.get(u - 1, 0))
                    ins = SCA.activation(sqb[u % 2][:, :], qro[u % 2][:, :], AF.Square)
                    m_sq[u] = self.sSC.inc(ins)
                    TEN.wait_ge(self.sSC.h, m_sq[u])
                    if u >= 1:
                        TEN.wait_ge(self.sVE2.h, m_mx[u - 1])
                    ins = TEN.matmul(pq2[:, :], self.onesb_c[:, :], sqb[u % 2][:, :],
                                     start=True, stop=True)
                    m_q2[u] = self.sPE.inc(ins)
                    VEC.wait_ge(self.sPE.h, m_q2[u])
                    ins = VEC.reduce_max(mx1[0:1, 0:1], pq2[:, :], AX.X)
                    self.se(VEC, ins)
                    ins = VEC.tensor_tensor(self.maxqrow[0:1, och:och + 1],
                                            self.maxqrow[0:1, och:och + 1],
                                            mx1[0:1, 0:1], ALU.max)
                    m_mx[u] = self.sVE2.inc(ins)
                    VEC.wait_ge(self.sVE2.h, m_mx[u])
                for j in range(4):
                    self._gu_unit(0, 4 * och + j, self.f_pg0, self.f_pu0)
            for eng in (TEN, SCA, GPS, SYN, VEC):
                eng.wait_ge(self.sVE2.h, self.sVE2.n)
                eng.wait_ge(self.sQSs[0].h, self.sQSs[0].n)
                eng.wait_ge(self.sQSs[1].h, self.sQSs[1].n)

    def p25a(self):
        nc, TEN, VEC, SCA, GPS, SYN = self.nc, self.TEN, self.VEC, self.SCA, self.GPS, self.SYN
        ins = SYN.dma_start(out=self.kt_scr[:, :], in_=self.ktT[:, :])
        self.sG2.inc(ins, 16)
        SYN.wait_ge(self.sG2.h, self.sG2.n)
        ins = SYN.dma_start(out=self.vp_scr[:, :], in_=self.vpn[:, :])
        self.sG2.inc(ins, 16)
        SYN.wait_ge(self.sG2.h, self.sG2.n)
        ins = SYN.dma_start(out=self.kn_scr[:, :], in_=self.kvhn[:, :])
        self.sG2.inc(ins, 16)
        SYN.wait_ge(self.sG2.h, self.sG2.n)
        for eng in (TEN, VEC, SCA, GPS):
            eng.wait_ge(self.sG2.h, self.sG2.n)
        self.esG.close()
        self.esTk.close()

        esF = ExitStack()
        self.esF = esF
        self.f_fuse = self.sb(esF, "p25_fuse", [128, 64 * 512], BF16)
        self.f_wgb = [self.sb(esF, f"p25_wg{i}", [128, E], BF16) for i in range(2)]
        self.f_wub = [self.sb(esF, f"p25_wu{i}", [128, E], BF16) for i in range(2)]
        self.f_sgb = [self.sb(esF, f"p25_sg{i}", [128, 512], F32) for i in range(2)]
        self.f_usb = [self.sb(esF, f"p25_us{i}", [128, 512], F32) for i in range(2)]
        self.f_ssg = self.sb(esF, "p25_ssg", [128, 64], F32)
        self.f_ssu = self.sb(esF, "p25_ssu", [128, 64], F32)
        self.f_sdb = self.sb(esF, "p25_sdb", [128, E], F32)

        ins = GPS.dma_start(out=self.f_ssg[:, :], in_=self.sgv[:, :])
        self.sWL.inc(ins, 16)
        GPS.wait_ge(self.sWL.h, self.sWL.n)
        ins = GPS.dma_start(out=self.f_ssu[:, :], in_=self.suv[:, :])
        self.sWL.inc(ins, 16)
        GPS.wait_ge(self.sWL.h, self.sWL.n)

        with ExitStack() as esQ:
            sdr = self.sb(esQ, "p25_sdr", [1, E], F32)
            psd = self.ps(esQ, "p25_psd", [128, 512], F32)
            ins = GPS.dma_start(out=sdr[:, :], in_=self.sdv[:, :])
            wl0 = self.sWL.inc(ins, 16)
            TEN.wait_ge(self.sWL.h, wl0)
            for i in range(4):
                if i > 0:
                    TEN.wait_ge(self.sVE.h, self.sVE.n)
                ins = TEN.matmul(psd[:, :], self.onesf_r[0:1, 0:128],
                                 sdr[0:1, 512 * i:512 * (i + 1)],
                                 start=True, stop=True)
                pe_m = self.sPE.inc(ins)
                VEC.wait_ge(self.sPE.h, pe_m)
                ins = VEC.tensor_copy(self.f_sdb[:, 512 * i:512 * (i + 1)], psd[:, :])
                self.sVE.inc(ins)
            TEN.wait_ge(self.sVE.h, self.sVE.n)
            self.sdb_done = self.sVE.n
            self.f_sdrfree = self.sPE.n
            SCA.wait_ge(self.sWL.h, wl0)

        self.ffb_ring = 0
        self.xsl_ring = 0
        self.xsl_done = {}
        self.sXO2 = self.sXO + [Sem(self.es.enter_context(self.nc.semaphore("sXO2c")))]
        self.fm_w = {}
        self.fm_gu = {}
        self.fm_sc = {}
        self.fm_fu = {}
        self.fm_dn = {}
        self.f_dn_done = {}
        self.f_t0marks = (self.sVE.n, self.sVE2.n, self.sSC.n)
        self.esPG = ExitStack()
        self.f_pg0 = [self.ps(self.esPG, f"p25_pg0_{i}", [128, 512], F32)
                      for i in range(2)]
        self.f_pu0 = [self.ps(self.esPG, f"p25_pu0_{i}", [128, 512], F32)
                      for i in range(2)]

    def _gu_unit(self, tb, ic, pg, pu):
        TEN, VEC, SCA, GPS = self.TEN, self.VEC, self.SCA, self.GPS
        wgb, wub, sgb, usb = self.f_wgb, self.f_wub, self.f_sgb, self.f_usb
        if ic == 0:
            if tb > 0:
                TEN.wait_ge(self.sVE2.h, self.sVE2.n)
            else:
                TEN.wait_ge(self.sVE.h, self.f_t0marks[0])
                TEN.wait_ge(self.sVE2.h, self.f_t0marks[1])
                TEN.wait_ge(self.sSC.h, self.f_t0marks[2])
        if ic >= 2:
            GPS.wait_ge(self.sPE.h, self.fm_gu[(tb, ic - 2)])
        sw = self.sWG[ic % 2]
        ins = GPS.dma_start(out=wgb[ic % 2][:, :],
                            in_=self.wgt[128 * ic:128 * (ic + 1), :])
        sw.inc(ins, 16)
        ins = GPS.dma_start(out=wub[ic % 2][:, :],
                            in_=self.wut[128 * ic:128 * (ic + 1), :])
        self.fm_w[(tb, ic)] = sw.inc(ins, 16)
        TEN.wait_ge(sw.h, self.fm_w[(tb, ic)])
        if ic >= 2:
            TEN.wait_ge(self.sSC.h, self.fm_sc[(tb, ic - 2)])
        last = None
        for cc in range(16):
            last = TEN.matmul(pg[ic % 2][:, :],
                              wgb[ic % 2][:, 128 * cc:128 * (cc + 1)],
                              self.xnT[:, 2048 * cc + 512 * tb:
                                       2048 * cc + 512 * (tb + 1)],
                              start=(cc == 0), stop=(cc == 15))
        for cc in range(16):
            last = TEN.matmul(pu[ic % 2][:, :],
                              wub[ic % 2][:, 128 * cc:128 * (cc + 1)],
                              self.xnT[:, 2048 * cc + 512 * tb:
                                       2048 * cc + 512 * (tb + 1)],
                              start=(cc == 0), stop=(cc == 15))
        self.fm_gu[(tb, ic)] = self.sPE.inc(last)
        SCA.wait_ge(self.sPE.h, self.fm_gu[(tb, ic)])
        if ic >= 2:
            SCA.wait_ge(self.sVE.h, self.fm_fu[(tb, ic - 2)])
        self.sSC.inc(SCA.activation(sgb[ic % 2][:, :], pg[ic % 2][:, :],
                                    AF.Silu, scale=self.f_ssg[:, ic:ic + 1]))
        ins = SCA.activation(usb[ic % 2][:, :], pu[ic % 2][:, :],
                             AF.Copy, scale=self.f_ssu[:, ic:ic + 1])
        self.fm_sc[(tb, ic)] = self.sSC.inc(ins)
        VEC.wait_ge(self.sSC.h, self.fm_sc[(tb, ic)])
        if ic == 0:
            VEC.wait_ge(self.sVE.h, self.sdb_done)
        if tb > 0 and ic == 0:
            VEC.wait_ge(self.sPE.h, self.f_dn_done[tb - 1])
        ins = VEC.tensor_tensor(self.f_fuse[:, 512 * ic:512 * (ic + 1)],
                                sgb[ic % 2][:, :], usb[ic % 2][:, :],
                                ALU.mult)
        self.fm_fu[(tb, ic)] = self.sVE.inc(ins)

    def _down(self, tb):
        TEN, VEC, GPS, SYN = self.TEN, self.VEC, self.GPS, self.SYN
        wdb, fuse, sdb = self.f_wdb, self.f_fuse, self.f_sdb
        xsl, ffb, tmpe = self.f_xsl, self.f_ffb, self.f_tmpe
        with ExitStack() as esQ:
            pd = [self.ps(esQ, f"p25_pd{tb}_{j}", [128, 512], F32) for j in range(8)]
            for p2 in range(2):
                for q4 in range(16):
                    par = q4 % 2
                    seq = (tb * 2 + p2) * 16 + q4
                    if seq >= 2:
                        GPS.wait_ge(self.sPE.h, self.fm_dn[seq - 2])
                    src_ap = bass.AP(self.wdn,
                                     (512 * q4) * E + 1024 * p2,
                                     [[E, 128], [128 * E, 4], [1, 1024]])
                    ins = GPS.dma_start(out=wdb[par][:, :], in_=src_ap)
                    wd_m = self.sWD[par].inc(ins, 16)
                    TEN.wait_ge(self.sWD[par].h, wd_m)
                    if q4 == 0:
                        TEN.wait_ge(self.sVE.h, self.fm_fu[(tb, 63)])
                        if (tb, p2) != (0, 0):
                            TEN.wait_ge(self.sVE2.h, self.sVE2.n)
                    last = None
                    for sub in range(4):
                        kc = 4 * q4 + sub
                        for tt in range(4):
                            for og2 in range(2):
                                last = TEN.matmul(
                                    pd[2 * tt + og2][:, :],
                                    fuse[:, 512 * kc + 128 * tt:
                                         512 * kc + 128 * (tt + 1)],
                                    wdb[par][:, 1024 * sub + 512 * og2:
                                             1024 * sub + 512 * (og2 + 1)],
                                    start=(kc == 0), stop=(kc == 63))
                    self.fm_dn[seq] = self.sPE.inc(last)
                pe_last = self.sPE.n
                VEC.wait_ge(self.sPE.h, pe_last)
                for k8 in range(8):
                    tt, og2 = divmod(k8, 2)
                    ri = self.xsl_ring
                    xslot = ri % 3
                    self.xsl_ring += 1
                    trow = 512 * tb + 128 * tt
                    tcol = 1024 * p2 + 512 * og2
                    if ri >= 3:
                        SYN.wait_ge(self.sVE2.h, self.xsl_done[ri - 3])
                    ins = SYN.dma_start(
                        out=xsl[xslot][:, :],
                        in_=self.x_own[trow:trow + 128,
                                       tcol:tcol + 512])
                    m_x = self.sXO2[xslot].inc(ins, 16)
                    slot = self.ffb_ring % 3
                    self.ffb_ring += 1
                    if self.sFB[slot].n > 0:
                        VEC.wait_ge(self.sFB[slot].h, self.sFB[slot].n)
                    VEC.wait_ge(self.sXO2[xslot].h, m_x)
                    ins = VEC.tensor_tensor(tmpe[k8 % 2][:, :],
                                            pd[2 * tt + og2][:, :],
                                            sdb[:, tcol:tcol + 512],
                                            ALU.mult)
                    self.se(VEC, ins)
                    ins = VEC.tensor_tensor(
                        ffb[slot][:, :], tmpe[k8 % 2][:, :],
                        xsl[xslot][:, :], ALU.add)
                    ve_m = self.sVE2.inc(ins)
                    self.xsl_done[ri] = ve_m
                    SYN.wait_ge(self.sVE2.h, ve_m)
                    ins = SYN.dma_start(
                        out=self.fp_scr[trow:trow + 128,
                                        tcol:tcol + 512],
                        in_=ffb[slot][:, :])
                    self.sFB[slot].inc(ins, 16)
                self.f_dn_done[tb] = pe_last

    def p25b(self):
        nc, TEN, VEC, SCA, GPS, SYN = self.nc, self.TEN, self.VEC, self.SCA, self.GPS, self.SYN
        with ExitStack() as esW:
            self.f_wdb = [self.sb(esW, f"p25_wd{i}", [128, 2 * E], BF16)
                          for i in range(2)]
            self.f_xsl = [self.sb(esW, f"p25_xs{i}", [128, 512], F32) for i in range(3)]
            self.f_ffb = [self.sb(esW, f"p25_ff{i}", [128, 512], F32) for i in range(3)]
            self.f_tmpe = [self.sb(esW, f"p25_te{i}", [128, 512], F32) for i in range(2)]
            self.esPG.close()
            self._down(0)
            for tb in range(1, 4):
                with ExitStack() as esQ:
                    pg = [self.ps(esQ, f"p25_pg{tb}_{i}", [128, 512], F32)
                          for i in range(2)]
                    pu = [self.ps(esQ, f"p25_pu{tb}_{i}", [128, 512], F32)
                          for i in range(2)]
                    for ic in range(64):
                        self._gu_unit(tb, ic, pg, pu)
                self._down(tb)
            if DBG_EXTRA:
                self.dbg_dump([("sdb", self.f_sdb[:, :], [128, E], 0),
                               ("ssg", self.f_ssg[:, :], [128, 64], 0)])
            for eng in (TEN, VEC, SCA, GPS, SYN):
                eng.wait_ge(self.sVE2.h, self.sVE2.n)
        self.esF.close()
        self.esTq.close()
        self.esA.close()

    def p3(self):
        nc, TEN, VEC, SCA, GPS, SYN = self.nc, self.TEN, self.VEC, self.SCA, self.GPS, self.SYN
        c_ = self.c_
        esB = ExitStack()
        self.esB = esB
        self.dmask = self.sb(esB, "b_dmask", [128, 4 * 512], BF16)
        self.OT = self.sb(esB, "b_OT", [128, H * 2048], BF16)
        self.wout_sb = self.sb(esB, "b_wout", [128, 16 * 2048], BF16)
        esB2 = ExitStack()
        self.esB2 = esB2
        ktT = self.sb(esB2, "b_ktT", [128, 4096], BF16)
        vpn = self.sb(esB2, "b_vpn", [128, 4096], BF16)
        kvhn = self.sb(esB2, "b_kvhn", [128, 1024], F32)
        qc = [self.sb(esB2, f"b_qc{i}", [128, 2048], BF16) for i in range(2)]
        vpp = [self.sb(esB2, f"b_vpp{i}", [128, 4096], BF16) for i in range(2)]
        PT = [self.sb(esB2, f"b_pt{i}", [128, 512], BF16) for i in range(3)]
        rbs = self.sb(esB2, "b_rbs", [128, 512], BF16)
        negB_all = self.sb(esB2, "b_nball", [128, 16], F32)
        bb = self.sb(esB2, "b_bb", [1, 16], F32)
        bb2 = self.sb(esB2, "b_bb2", [1, 16], F32)
        lr = self.sb(esB2, "b_lr", [1, 512], F32)
        rr = self.sb(esB2, "b_rr", [1, 512], F32)

        for v_ in range(4):
            ins = GPS.memset(self.dmask[:, 512 * v_:512 * (v_ + 1)], 1.0)
            self.se(GPS, ins)
            ins = GPS.affine_select(self.dmask[:, 512 * v_:512 * (v_ + 1)],
                                    self.dmask[:, 512 * v_:512 * (v_ + 1)],
                                    [[1, 512]], ALU.is_ge, 0.0,
                                    base=-128 * v_, channel_multiplier=-1)
            self.sGP.inc(ins)
        gp_mask_done = self.sGP.n
        for eng in (TEN, VEC, SCA):
            eng.wait_ge(self.sGP.h, gp_mask_done)
        ins = SYN.dma_start(out=ktT[:, :], in_=self.kt_scr[:, :])
        self.sG2.inc(ins, 16)
        SYN.wait_ge(self.sG2.h, self.sG2.n)
        ins = SYN.dma_start(out=vpn[:, :], in_=self.vp_scr[:, :])
        self.sG2.inc(ins, 16)
        SYN.wait_ge(self.sG2.h, self.sG2.n)
        ins = SYN.dma_start(out=kvhn[:, :], in_=self.kn_scr[:, :])
        g2m = self.sG2.inc(ins, 16)
        for b in range(16):
            if b > 0:
                GPS.wait_ge(self.sWL.h, self.sWL.n)
            ins = GPS.dma_start(out=self.wout_sb[:, 2048 * b:2048 * (b + 1)],
                                in_=self.wout[128 * b:128 * (b + 1), :])
            self.sWL.inc(ins, 16)
        self.wout_done = self.sWL.n
        GPS.wait_ge(self.sQSs[0].h, self.sQSs[0].n)
        GPS.wait_ge(self.sQSs[1].h, self.sQSs[1].n)
        ins = GPS.dma_start(out=qc[0][:, :], in_=self.q_scr[:, 0:2048])
        self.sQC[0].inc(ins, 16)
        m_ql = {0: (0, self.sQC[0].n)}

        with ExitStack() as esQ:
            pS = [self.ps(esQ, f"b_pS{i}", [128, 512], F32) for i in range(3)]
            pO = [self.ps(esQ, f"b_pO{i}", [128, 512], F32) for i in range(2)]
            pL = [self.ps(esQ, f"b_pL{i}", [1, 512], F32) for i in range(2)]
            pM0 = self.ps(esQ, "b_pM0", [128, 512], F32)
            pM = [pM0, pM0]

            for eng in (TEN, VEC, SCA):
                eng.wait_ge(self.sG2.h, g2m)

            ins = VEC.tensor_tensor(bb[0:1, :], self.mkrow[0:1, 0:16],
                                    self.mkrow[0:1, 0:16], ALU.mult)
            self.se(VEC, ins)
            ins = VEC.tensor_scalar(bb2[0:1, :], bb[0:1, :], self.knm[0:1, 0:1],
                                    None, ALU.mult)
            self.se(VEC, ins)
            ins = VEC.tensor_tensor(bb[0:1, :], bb2[0:1, :],
                                    self.maxqrow[0:1, 0:16], ALU.mult)
            ve_m = self.sVE.inc(ins)
            SCA.wait_ge(self.sVE.h, ve_m)
            ins = SCA.activation(bb2[0:1, :], bb[0:1, :], AF.Ln)
            self.se(SCA, ins)
            ins = SCA.activation(bb[0:1, :], bb2[0:1, :], AF.Exp,
                                 scale=c_[0:1, COL_HALFC:COL_HALFC + 1])
            sc_m = self.sSC.inc(ins)
            VEC.wait_ge(self.sSC.h, sc_m)
            ins = VEC.tensor_scalar(bb2[0:1, :], bb[0:1, :], -1.0, None, ALU.mult)
            ve_m = self.sVE.inc(ins)
            TEN.wait_ge(self.sVE.h, ve_m)
            ins = TEN.matmul(pM0[:, 0:16], self.onesf_r[0:1, :],
                             bb2[0:1, 0:16], start=True, stop=True)
            pe_m = self.sPE.inc(ins)
            VEC.wait_ge(self.sPE.h, pe_m)
            ins = VEC.tensor_copy(negB_all[:, :], pM0[:, 0:16])
            nball_ev = self.sVE.inc(ins)

            head_end = {}
            for h in range(H):
                if h + 1 < H:
                    if h >= 1:
                        GPS.wait_ge(self.sPE.h, head_end[h - 1]["lastS"])
                    ins = GPS.dma_start(out=qc[(h + 1) % 2][:, :],
                                        in_=self.q_scr[:, 2048 * (h + 1):2048 * (h + 2)])
                    sl = (h + 1) % 2
                    m_ql[h + 1] = (sl, self.sQC[sl].inc(ins, 16))

                he2v = head_end.get(h - 2)
                for mc in range(32):
                    if mc == 0 and he2v is not None:
                        VEC.wait_ge(self.sPE.h, he2v["lastAV"])
                    ins = VEC.tensor_scalar_mul(
                        vpp[h % 2][:, 128 * mc:128 * (mc + 1)],
                        vpn[:, 128 * mc:128 * (mc + 1)],
                        kvhn[:, 32 * mc + 16 + h:32 * mc + 16 + h + 1])
                    self.sVE.inc(ins)
                hprep = self.sVE.n

                TEN.wait_ge(self.sQC[m_ql[h][0]].h, m_ql[h][1])
                st0 = self.snap()
                bmarks = {}
                ends = []
                for half in (0, 1):
                    self.restore(st0)
                    pairs = pair_list(half)
                    NP = len(pairs)
                    assert NP == 72, NP
                    slot_last = {}
                    for pj, pr in enumerate(pairs):
                        if pr["last"]:
                            slot_last[pr["si"]] = pj

                    def exp_mark(pj, pairs=pairs, st0=st0):
                        return st0["sSC"] + pj + 1 + pairs[pj]["si"]

                    def rinv_mark(si, slot_last=slot_last, st0=st0):
                        return st0["sSC"] + slot_last[si] + 2 + si

                    def lv_mark(si, st0=st0):
                        return st0["sVE2"] + 2 * si + 1

                    def oev_mark(si, st0=st0):
                        return st0["sVE2"] + 2 * si + 2

                    gp_mark = {}
                    d = 0
                    for pj, pr in enumerate(pairs):
                        if pr["dv"] is not None:
                            d += 1
                            gp_mark[pj] = st0["sGP"] + d
                    mS = {}
                    mAV = {}
                    mBC = {}

                    def emit_lav(pj, pairs=pairs, gp_mark=gp_mark, mS=mS, mAV=mAV,
                                 mBC=mBC, exp_mark=exp_mark, rinv_mark=rinv_mark,
                                 oev_mark=oev_mark, hprep=hprep, h=h):
                        pr = pairs[pj]
                        si, mc = pr["si"], pr["mc"]
                        if pj == 0:
                            TEN.wait_ge(self.sVE.h, hprep)
                        if pr["dv"] is not None:
                            TEN.wait_ge(self.sGP.h, gp_mark[pj])
                        else:
                            TEN.wait_ge(self.sSC.h, exp_mark(pj))
                        if pr["first"] and si >= 2:
                            TEN.wait_ge(self.sVE2.h, oev_mark(si - 2))
                        TEN.matmul(pL[si % 2][:, :], self.onesb_c[:, :],
                                   PT[pj % 3][:, :], start=pr["first"], stop=pr["last"])
                        ins = TEN.matmul(pO[si % 2][:, :],
                                         vpp[h % 2][:, 128 * mc:128 * (mc + 1)],
                                         PT[pj % 3][:, :],
                                         start=pr["first"], stop=pr["last"])
                        mAV[pj] = self.sPE.inc(ins)
                        if pr["last"]:
                            TEN.wait_ge(self.sSC.h, rinv_mark(si))
                            if si == 0:
                                if h > 0:
                                    TEN.wait_ge(self.sVE2.h,
                                                head_end[h - 1]["sve2"])
                                else:
                                    TEN.wait_ge(self.sVE.h, nball_ev)
                            else:
                                TEN.wait_ge(self.sVE2.h, oev_mark(si - 1))
                            ins = TEN.matmul(pM0[:, :], self.onesf_r[0:1, :],
                                             rr[0:1, :], start=True, stop=True)
                            mBC[si] = self.sPE.inc(ins)

                    with (TEN.If_eq(self.regs["t"], 0) if half == 0 else TEN.Else()):
                        TEN.wait_ge(self.sSC.h, st0["sSC"])
                        TEN.wait_ge(self.sVE2.h, st0["sVE2"])
                        TEN.wait_ge(self.sGP.h, st0["sGP"])
                        for pj, pr in enumerate(pairs):
                            if pj >= 3:
                                TEN.wait_ge(self.sSC.h, exp_mark(pj - 3))
                            ins = TEN.matmul(pS[pj % 3][:, :],
                                             ktT[:, 128 * pr["mc"]:
                                                 128 * (pr["mc"] + 1)],
                                             qc[h % 2][:, 512 * pr["tg"]:
                                                       512 * (pr["tg"] + 1)],
                                             start=True, stop=True)
                            mS[pj] = self.sPE.inc(ins)
                            if pj >= 2:
                                emit_lav(pj - 2)
                        emit_lav(NP - 2)
                        emit_lav(NP - 1)
                    bmarks[half] = dict(mS=mS, mAV=mAV, mBC=mBC, exp_mark=exp_mark,
                                        rinv_mark=rinv_mark, lv_mark=lv_mark,
                                        oev_mark=oev_mark, gp_mark=gp_mark,
                                        slot_last=slot_last, pairs=pairs)
                    ends.append(self.snap())
                assert ends[0] == ends[1], (h, "TEN", ends)

                st1 = self.snap()
                e2 = []
                for half in (0, 1):
                    self.restore(st1)
                    bm = bmarks[half]
                    with (SCA.If_eq(self.regs["s"], 0) if half == 0 else SCA.Else()):
                        SCA.wait_ge(self.sVE.h, nball_ev)
                        for pj, pr in enumerate(bm["pairs"]):
                            SCA.wait_ge(self.sPE.h, bm["mS"][pj])
                            if pj >= 3:
                                SCA.wait_ge(self.sPE.h, bm["mAV"][pj - 3])
                            mc_ = pr["mc"]
                            ins = SCA.activation(PT[pj % 3][:, :], pS[pj % 3][:, :],
                                                 AF.Exp, bias=negB_all[:, h:h + 1],
                                                 scale=kvhn[:, 32 * mc_ + h:
                                                            32 * mc_ + h + 1])
                            self.sSC.inc(ins)
                            if pr["last"]:
                                SCA.wait_ge(self.sVE2.h, bm["lv_mark"](pr["si"]))
                                if pr["si"] >= 1:
                                    SCA.wait_ge(self.sPE.h, bm["mBC"][pr["si"] - 1])
                                SCA.activation(rr[:, :], lr[:, :], AF.Ln)
                                ins = SCA.activation(rr[:, :], rr[:, :], AF.Exp,
                                                     scale=c_[0:1, COL_NEG1:COL_NEG1 + 1])
                                self.sSC.inc(ins)
                    e2.append(self.snap())
                assert e2[0] == e2[1], (h, "SCA")

                st2 = self.snap()
                e3 = []
                for half in (0, 1):
                    self.restore(st2)
                    bm = bmarks[half]
                    with (GPS.If_eq(self.regs["g"], 0) if half == 0 else GPS.Else()):
                        for pj, pr in enumerate(bm["pairs"]):
                            if pr["dv"] is None:
                                continue
                            GPS.wait_ge(self.sSC.h, bm["exp_mark"](pj))
                            ins = GPS.affine_select(
                                PT[pj % 3][:, :], PT[pj % 3][:, :], [[1, 512]],
                                ALU.is_ge, 0.0, base=-128 * pr["dv"],
                                channel_multiplier=-1)
                            self.sGP.inc(ins)
                    e3.append(self.snap())
                assert e3[0] == e3[1], (h, "GPS")

                st3 = self.snap()
                e4 = []
                for half in (0, 1):
                    self.restore(st3)
                    bm = bmarks[half]
                    with (VEC.If_eq(self.regs["v"], 0) if half == 0 else VEC.Else()):
                        for pj, pr in enumerate(bm["pairs"]):
                            if not pr["last"]:
                                continue
                            si = pr["si"]
                            tg = SLOT_LNT[si]
                            VEC.wait_ge(self.sPE.h, bm["mAV"][pj])
                            if si >= 1:
                                VEC.wait_ge(self.sSC.h, bm["rinv_mark"](si - 1))
                            ins = VEC.tensor_scalar(lr[:, :], pL[si % 2][:, :],
                                                    1e-30, None, ALU.add)
                            self.sVE2.inc(ins)
                            VEC.wait_ge(self.sPE.h, bm["mBC"][si])
                            ins = VEC.tensor_copy(rbs[:, :], pM0[:, :])
                            self.se(VEC, ins)
                            ins = VEC.tensor_tensor(
                                self.OT[:, 2048 * h + 512 * tg:
                                        2048 * h + 512 * (tg + 1)],
                                pO[si % 2][:, :], rbs[:, :], ALU.mult)
                            self.sVE2.inc(ins)
                    e4.append(self.snap())
                assert e4[0] == e4[1], (h, "VEC")

                head_end[h] = dict(
                    lastS=max(bmarks[0]["mS"][71], bmarks[1]["mS"][71]),
                    lastAV=max(bmarks[0]["mAV"][71], bmarks[1]["mAV"][71]),
                    sve2=self.sVE2.n)

            for eng in (TEN, VEC, SCA, GPS, SYN):
                eng.wait_ge(self.sVE2.h, self.sVE2.n)
                eng.wait_ge(self.sSC.h, self.sSC.n)
        self.esB2.close()

    def p4(self):
        nc, TEN, VEC, SCA, GPS, SYN = self.nc, self.TEN, self.VEC, self.SCA, self.GPS, self.SYN
        with ExitStack() as esP:
            fpt = [self.sb(esP, f"p4_fp{i}", [128, E], F32) for i in range(3)]
            orow = [self.sb(esP, f"p4_or{i}", [128, E], F32) for i in range(2)]
            py = [self.ps(esP, f"p4_py{j}", [128, 512], F32) for j in range(8)]

            TEN.wait_ge(self.sWL.h, self.wout_done)
            for s_ in self.sFB:
                SYN.wait_ge(s_.h, s_.n)
            m_fl = {}
            m_y = {}
            m_ev = {}
            for t in range(16):
                if t >= 3:
                    SYN.wait_ge(self.sVE2.h, m_ev[t - 3])
                ins = SYN.dma_start(out=fpt[t % 3][:, :],
                                    in_=self.fp_scr[128 * t:128 * (t + 1), :])
                m_fl[t] = self.sFP[t % 3].inc(ins, 16)
                par = t % 2
                if t >= 2:
                    TEN.wait_ge(self.sVE2.h, m_ev[t - 2])
                last = None
                for hh in range(16):
                    for oc in range(4):
                        last = TEN.matmul(
                            py[4 * par + oc][:, :],
                            self.OT[:, 2048 * hh + 128 * t:2048 * hh + 128 * (t + 1)],
                            self.wout_sb[:, 2048 * hh + 512 * oc:
                                         2048 * hh + 512 * (oc + 1)],
                            start=(hh == 0), stop=(hh == 15))
                m_y[t] = self.sPE.inc(last)
                VEC.wait_ge(self.sPE.h, m_y[t])
                VEC.wait_ge(self.sFP[t % 3].h, m_fl[t])
                if t >= 2:
                    VEC.wait_ge(self.sOU[t % 2].h, self.sOU[t % 2].n)
                for oc in range(4):
                    ins = VEC.tensor_tensor(orow[par][:, 512 * oc:512 * (oc + 1)],
                                            py[4 * par + oc][:, :],
                                            fpt[t % 3][:, 512 * oc:512 * (oc + 1)],
                                            ALU.add)
                m_ev[t] = self.sVE2.inc(ins)
                SYN.wait_ge(self.sVE2.h, m_ev[t])
                ins = SYN.dma_start(out=self.out[128 * t:128 * (t + 1), :],
                                    in_=orow[par][:, :])
                self.sOU[par].inc(ins, 16)
            SYN.wait_ge(self.sOU[0].h, self.sOU[0].n)
            SYN.wait_ge(self.sOU[1].h, self.sOU[1].n)
        self.esB.close()

    def dbg_dump(self, tens):
        nc, SYN, GPS = self.nc, self.SYN, self.GPS
        with ExitStack() as esD:
            for i, (name, t, shape, via_sbuf) in enumerate(tens):
                o = nc.dram_tensor("dbg_" + name, shape, t.dtype if hasattr(t, "dtype")
                                   else F32, kind="ExternalOutput")
                if via_sbuf:
                    tmp = self.sb(esD, f"dbg_t{i}", [128, shape[1]], t.dtype)
                    for rr_ in range(0, shape[0], 128):
                        ins = SYN.dma_start(out=tmp[:, :], in_=t[rr_:rr_ + 128, :])
                        n = self.sOUT.inc(ins, 16)
                        SYN.wait_ge(self.sOUT.h, n)
                        ins = SYN.dma_start(out=o[rr_:rr_ + 128, :], in_=tmp[:, :])
                        n = self.sOUT.inc(ins, 16)
                        SYN.wait_ge(self.sOUT.h, n)
                else:
                    ins = SYN.dma_start(out=o[:, :], in_=t)
                    n = self.sOUT.inc(ins, 16)
                    SYN.wait_ge(self.sOUT.h, n)
            SYN.wait_ge(self.sOUT.h, self.sOUT.n)

    def build(self, stop=4):
        self.p0()
        self.p1()
        if stop == 1:
            self.dbg_dump([
                ("ktT", self.ktT[:, :], [128, 4096], 0),
                ("vpn", self.vpn[:, :], [128, 4096], 0),
                ("kvhn", self.kvhn[:, :], [128, 1024], 0),
                ("mkrow", self.mkrow[:, :], [1, 32], 0),
                ("knm", self.knm[:, :], [1, 4], 0),
                ("xnT", self.xnT[:, :], [128, 16 * 2048], 0),
            ])
            self.esG.close(); self.esTk.close(); self.esTq.close()
            self.esA.close(); self.es.close()
            return self.nc
        self.p25a()
        self.p2()
        if stop == 2:
            self.dbg_dump([
                ("qscr", self.q_scr[:, :], [128, H * 2048], 0),
                ("maxq", self.maxqrow[:, :], [1, 16], 0),
            ])
            self.esPG.close(); self.esF.close(); self.esTq.close()
            self.esA.close(); self.es.close()
            return self.nc
        self.p25b()
        if stop == 25:
            self.dbg_dump([("fps", self.fp_scr, [2048, E], 1)])
            self.es.close()
            return self.nc
        self.p3()
        if stop == 3:
            self.dbg_dump([("OT", self.OT[:, :], [128, H * 2048], 0)])
            self.esB.close()
            self.es.close()
            return self.nc
        if DBG_EXTRA:
            for s_ in self.sFB:
                self.SYN.wait_ge(s_.h, s_.n)
            self.dbg_dump([("fps", self.fp_scr, [2048, E], 1)])
        self.p4()
        self.es.close()
        return self.nc


DBG_EXTRA = False
_NC_CACHE = {}


def get_nc(stop=4):
    if stop not in _NC_CACHE:
        _NC_CACHE[stop] = KB().build(stop=stop)
    return _NC_CACHE[stop]


def kernel(L0_attn_x, L0_attn_Wq, L0_attn_Wkv, L0_attn_Wkvh, L0_attn_Wg,
           L0_attn_Wout, L0_ffn_w_gate, L0_ffn_w_up, L0_ffn_w_down,
           L0_ffn_s_gate, L0_ffn_s_up, L0_ffn_s_down):
    x = np.ascontiguousarray(np.asarray(L0_attn_x, dtype=np.float32))
    WqT = np.asarray(L0_attn_Wq, np.float32).T
    wq_h = np.ascontiguousarray(
        WqT.reshape(16, 128, 16, 128).transpose(2, 1, 0, 3).reshape(E, E))
    wkv_h = np.ascontiguousarray(
        np.asarray(L0_attn_Wkv, np.float32).T.reshape(16, 128, 256)
        .transpose(1, 0, 2).reshape(128, 16 * 256))
    wsm_full = np.concatenate([np.asarray(L0_attn_Wkvh, np.float32).T,
                               np.asarray(L0_attn_Wg, np.float32).T], axis=1)
    wsm_h = np.ascontiguousarray(
        wsm_full.reshape(16, 128, 64).transpose(1, 0, 2).reshape(128, 16 * 64))
    wout_h = np.ascontiguousarray(np.asarray(L0_attn_Wout, np.float32).T)
    wgt_h = np.ascontiguousarray(
        np.asarray(L0_ffn_w_gate, np.int8).reshape(16, 128, 64, 128)
        .transpose(2, 1, 0, 3).reshape(INTER, E))
    wut_h = np.ascontiguousarray(
        np.asarray(L0_ffn_w_up, np.int8).reshape(16, 128, 64, 128)
        .transpose(2, 1, 0, 3).reshape(INTER, E))
    wdn_h = np.ascontiguousarray(np.asarray(L0_ffn_w_down, np.int8))
    sg_h = np.ascontiguousarray(np.asarray(L0_ffn_s_gate, np.float32).reshape(64, 128).T)
    su_h = np.ascontiguousarray(np.asarray(L0_ffn_s_up, np.float32).reshape(64, 128).T)
    sd_h = np.ascontiguousarray(np.asarray(L0_ffn_s_down, np.float32).reshape(1, E))

    nc = get_nc()
    in_maps = []
    rowmaps = []
    for c in range(8):
        b, half = divmod(c, 2)
        rows = (np.r_[0:1024, 3072:4096] if half == 0 else np.r_[1024:3072])
        rowmaps.append(rows)
        in_maps.append({
            "xb": np.ascontiguousarray(x[b]),
            "x_own": np.ascontiguousarray(x[b][rows]),
            "pos_own": np.ascontiguousarray(rows.astype(np.float32).reshape(1, 2048)),
            "cfg": np.array([[half, 0, 0, 0]], np.int32),
            "wq": wq_h, "wkv": wkv_h, "wsm": wsm_h, "wout": wout_h,
            "wgt": wgt_h, "wut": wut_h, "wdn": wdn_h,
            "sgv": sg_h, "suv": su_h, "sdv": sd_h,
        })
    res = run_bass_kernel_spmd(nc, in_maps, core_ids=list(range(8)))
    outp = np.empty((4, NSEQ, E), np.float32)
    for c in range(8):
        b = c // 2
        outp[b][rowmaps[c]] = res.results[c]["out"]
    return outp
